# version 01
# baseline (speedup 1.0000x reference)
import math
from contextlib import ExitStack

import numpy as np
import concourse.bass as bass
import concourse.mybir as mybir
from concourse.bass_utils import run_bass_kernel_spmd

F32 = mybir.dt.float32
BF16 = mybir.dt.bfloat16
I32 = mybir.dt.int32
AF = mybir.ActivationFunctionType
ALU = mybir.AluOpType
AX = mybir.AxisListType

T = 2048
D = 1024
NCORE = 8


COMPUTE_ENGINES = ("tensor", "vector", "scalar", "gpsimd", "sync")
DMA_QUEUES = ("qsync", "qgp", "qscal")
QUEUE_ENGINE = {"qsync": "sync", "qgp": "gpsimd", "qscal": "scalar"}
ALL_STREAMS = COMPUTE_ENGINES + DMA_QUEUES


class Op:
    __slots__ = ("stream", "fn", "reads", "writes", "idx", "label",
                 "sem", "inc_to", "waits", "pre_wait")

    def __init__(self, stream, fn, reads, writes, idx, label=""):
        self.stream = stream
        self.fn = fn
        self.reads = reads
        self.writes = writes
        self.idx = idx
        self.label = label
        self.sem = None
        self.inc_to = 0
        self.waits = {}
        self.pre_wait = None


N_QSEM = 8


class Sched:

    def __init__(self, nc):
        self.nc = nc
        self.ops = []
        self.last_writer = {}
        self.readers = {}
        self.eng_seq = {s: 0 for s in COMPUTE_ENGINES}
        self.q_idx = {q: 0 for q in DMA_QUEUES}
        self.sem_uses = {}
        self.bar = {}

    def barrier(self):
        snap = {}
        for op in self.ops:
            snap[op.sem] = max(snap.get(op.sem, 0), op.inc_to)
        self.bar = snap

    def add(self, stream, fn, r=(), w=(), label=""):
        op = Op(stream, fn, tuple(r), tuple(w), len(self.ops), label)
        if stream in DMA_QUEUES:
            k = self.q_idx[stream] % N_QSEM
            self.q_idx[stream] += 1
            sem = f"{stream}_{k}"
            uses = self.sem_uses.get(sem, 0)
            if uses > 0:
                op.pre_wait = (sem, uses * 16)
            self.sem_uses[sem] = uses + 1
            op.sem = sem
            op.inc_to = (uses + 1) * 16
        else:
            self.eng_seq[stream] += 1
            op.sem = f"e_{stream}"
            op.inc_to = self.eng_seq[stream]
        deps = []
        for t in op.reads:
            lw = self.last_writer.get(t)
            if lw is not None:
                deps.append(lw)
        for t in op.writes:
            lw = self.last_writer.get(t)
            if lw is not None:
                deps.append(lw)
            deps.extend(self.readers.get(t, ()))
        for d in deps:
            if d is op:
                continue
            cur = op.waits.get(d.sem, 0)
            op.waits[d.sem] = max(cur, d.inc_to)
        for sname, v in self.bar.items():
            if op.waits.get(sname, 0) < v:
                op.waits[sname] = v
        for t in op.reads:
            self.readers.setdefault(t, []).append(op)
        for t in op.writes:
            self.last_writer[t] = op
            self.readers[t] = []
        self.ops.append(op)
        return op

    def finalize_and_emit(self, es):
        nc = self.nc
        sem_names = set()
        for op in self.ops:
            sem_names.add(op.sem)
            sem_names.update(op.waits.keys())
            if op.pre_wait:
                sem_names.add(op.pre_wait[0])
        sems = {n: es.enter_context(nc.semaphore(n)) for n in sorted(sem_names)}
        per_stream = {s: [] for s in ALL_STREAMS}
        for op in self.ops:
            per_stream[op.stream].append(op)

        hw_map = {}
        for stream in ALL_STREAMS:
            if per_stream[stream]:
                hw = QUEUE_ENGINE.get(stream, stream)
                hw_map.setdefault(hw, []).append(stream)

        def make_runner(streams):
            ops = sorted([o for s in streams for o in per_stream[s]],
                         key=lambda o: o.idx)

            def runner(eng):
                waited = {}
                for op in ops:
                    for sname, v in sorted(op.waits.items()):
                        if waited.get(sname, 0) < v:
                            eng.wait_ge(sems[sname], v)
                            waited[sname] = v
                    if op.pre_wait is not None:
                        sname, v = op.pre_wait
                        if waited.get(sname, 0) < v:
                            eng.wait_ge(sems[sname], v)
                            waited[sname] = v
                    ins = op.fn(eng)
                    assert ins is not None,                         f"op {op.label} on {op.stream} returned no instruction"
                    ins.then_inc(sems[op.sem],
                                 16 if op.stream in DMA_QUEUES else 1)
            return runner

        with nc.Block() as blk:
            for hw, streams in hw_map.items():
                getattr(blk, hw)(make_runner(streams))

        print(f"[sched] ops={len(self.ops)} sems={len(sem_names)} "
              f"per-stream={ {k: len(v) for k, v in per_stream.items() if v} }")



class Builder:
    def __init__(self, nc, es, n_sub=6):
        self.nc = nc
        self.es = es
        self.sched = Sched(nc)
        self.n_sub = n_sub
        self.name_ctr = 0

    def sb(self, shape, dtype, name=None, stack=None):
        self.name_ctr += 1
        nm = f"{name}_{self.name_ctr}" if name else f"t{self.name_ctr}"
        import concourse.mybir as _mb
        nbytes = int(np.prod(shape[1:])) * _mb.dt.size(dtype) if len(shape) > 1 else _mb.dt.size(dtype)
        self.alloc_log = getattr(self, "alloc_log", [])
        self.alloc_log.append((nm, nbytes))
        return (stack or self.es).enter_context(
            self.nc.sbuf_tensor(nm, list(shape), dtype))

    def op(self, stream, fn, r=(), w=(), label=""):
        return self.sched.add(stream, fn, r, w, label)

    def dma(self, queue, out_ap, in_ap, r=(), w=(), label="dma", transpose=False,
            slow_ok=False):
        if transpose:
            fn = lambda e: e.dma_start_transpose(out_ap, in_ap)
        else:
            fn = lambda e: e.dma_start(out=out_ap, in_=in_ap,
                                       allow_slow_non_contiguous=slow_ok)
        return self.sched.add(queue, fn, r, w, label)

    def build(self, inputs_meta):
        nc = self.nc
        self.hbm = {}
        for name, shape, dt in inputs_meta:
            self.hbm[name] = nc.dram_tensor(name, list(shape), dt,
                                            kind="ExternalInput")
        self.out_hbm = nc.dram_tensor("out", [T, D], F32,
                                      kind="ExternalOutput")

        self.x = self.sb([128, 8 * T], F32, "x_res")
        self.u = self.sb([128, 8 * T], BF16, "u_ln")
        self.ones_col_b = self.sb([128, 1], BF16, "ones_col_b")
        self.ones_row_f = self.sb([1, 128], F32, "ones_row_f")
        self.ident_b = self.sb([128, 128], BF16, "ident_b")
        self.ident_f = self.sb([128, 128], F32, "ident_f")
        self.halfkk = self.sb([128, 1], F32, "halfkk")
        self.tiny = self.sb([128, 8], F32, "tinybuf")
        self.rows = self.sb([97, 2 * T], F32, "rows")
        self.PS = self.es.enter_context(
            self.nc.psum_tensor("PS", [128, 4096], F32))
        self.pool = [self.PS[:, 512 * b:512 * (b + 1)] for b in range(8)]
        self.w1qT = self.nc.dram_tensor("w1qT_scr", [32 * 8 * 128, 128], BF16)
        self.wcT = self.nc.dram_tensor("wcT_scr", [131072, 128], BF16)
        self.CsT_hbm = self.nc.dram_tensor("CsT_scr", [128, 32 * 256], BF16)
        self.w2qT = self.nc.dram_tensor("w2qT_scr", [8 * 32 * 128, 128], BF16)
        self.h_scr = self.nc.dram_tensor("h_scr", [4096, T], BF16)
        self.winT = self.nc.dram_tensor("winT_scr", [32 * 8 * 128, 128], BF16)
        self.wxT = self.nc.dram_tensor("wxT_scr", [(16 * 16 + 16) * 128, 128], BF16)
        self.woT = self.nc.dram_tensor("woT_scr", [8 * 16 * 128, 128], BF16)

        self.constbuf = self.sb([128, 16], F32, "constbuf")
        self.n_const = 0
        self.setup_consts()
        self.load_x()
        if self.n_sub >= 1:
            self.prep_l0_ffn_weights()
        if self.n_sub >= 4:
            self.prep_l1_ffn_weights()
        if self.n_sub >= 5:
            self.prep_l1_attn_weights()

        steps = [
            lambda: (self.layernorm("ln0"), self.l0_ffn(0)),
            lambda: (self.layernorm("ln1"), self.l0_attn()),
            lambda: (self.layernorm("ln2"), self.l0_ffn(1)),
            lambda: (self.layernorm("ln3"), self.l1_ffn(0)),
            lambda: (self.layernorm("ln4"), self.l1_attn()),
            lambda: (self.layernorm("ln5"), self.l1_ffn(1)),
        ]
        for i in range(min(self.n_sub, 6)):
            steps[i]()

        self.store_out()
        fin_tags = tuple(f"osent{tt}" for tt in range(16))
        self.op("sync", lambda e: e.nop(), r=fin_tags, label="final")
        self.sched.finalize_and_emit(self.es)

    def setup_consts(self):
        op = self.op
        op("vector", lambda e: e.memset(self.ones_col_b[:], 1.0),
           w=("ones_col_b",), label="ones_col")
        op("vector", lambda e: e.memset(self.ones_row_f[:], 1.0),
           w=("ones_row_f",), label="ones_row")
        stC = ExitStack()
        it_col = self.sb([128, 128], F32, "it_col", stack=stC)
        it_row = self.sb([128, 128], F32, "it_row", stack=stC)
        op("gpsimd", lambda e: e.iota(it_col[:], [[1, 128]], channel_multiplier=0, allow_small_or_imprecise_dtypes=True),
           w=("it_col",), label="iota_col")
        op("gpsimd", lambda e: e.iota(it_row[:], [[0, 128]], channel_multiplier=1, allow_small_or_imprecise_dtypes=True),
           w=("it_row",), label="iota_row")
        op("vector", lambda e: e.tensor_tensor(self.ident_f[:], it_col[:],
                                               it_row[:], ALU.is_equal),
           r=("it_col", "it_row"), w=("ident_f",), label="ident_f")
        op("vector", lambda e: e.tensor_copy(self.ident_b[:], self.ident_f[:]),
           r=("ident_f",), w=("ident_b",), label="ident_b")
        can_i = self.sb([128, 1], I32, "can_i", stack=stC)
        op("vector", lambda e: e.memset(self.tiny[:, 0:1], 0.75), w=("tiny0",))
        op("vector", lambda e: e.tensor_copy(can_i[:], self.tiny[:, 0:1]),
           r=("tiny0",), w=("can_i",), label="canary_cvt")
        op("vector", lambda e: e.tensor_scalar(self.halfkk[:], can_i[:], -0.5, 0.5,
                                               ALU.mult, ALU.add),
           r=("can_i",), w=("halfkk",), label="halfkk")
        self.sched.barrier()
        stC.close()

    def load_x(self):
        xin = self.hbm["L0_attn_x"]
        stio = ExitStack()
        stage = [self.sb([128, D], F32, f"iostage{i}", stack=stio) for i in range(2)]
        for tt in range(16):
            s = stage[tt % 2]
            stag = f"iostage{tt % 2}"
            self.dma("qsync", s[:],
                     bass.AP(xin, tt * 128 * D, [[D, 128], [1, D]]),
                     r=(), w=(stag,), label=f"xload{tt}")
            for dc in range(8):
                pb = self.pool[4 + dc % 4]
                ptag = f"ps{4 + dc % 4}"
                self.op("tensor",
                        lambda e, s=s, dc=dc, pb=pb: e.transpose(
                            pb[:, 0:128], s[:, dc * 128:(dc + 1) * 128],
                            self.ident_f[:]),
                        r=(stag, "ident_f"), w=(ptag,), label=f"xT{tt}_{dc}")
                self.op("scalar",
                        lambda e, pb=pb, dc=dc, tt=tt: e.copy(
                            self.x[:, dc * T + tt * 128: dc * T + (tt + 1) * 128],
                            pb[:, 0:128]),
                        r=(ptag,), w=(f"x:{dc}:{tt}",), label=f"xEv{tt}_{dc}")
        stio.close()
        self.sched.barrier()

    def x_tags(self, dc=None, tts=None):
        dcs = range(8) if dc is None else [dc]
        tl = range(16) if tts is None else tts
        return tuple(f"x:{d}:{t}" for d in dcs for t in tl)

    def u_tags(self, dc=None):
        dcs = range(8) if dc is None else [dc]
        return tuple(f"u:{d}" for d in dcs)

    def constf(self, val):
        key = float(val)
        self.cvals = getattr(self, "cvals", {})
        if key in self.cvals:
            return key
        idx = self.n_const
        self.n_const += 1
        assert idx < 16
        ap = self.constbuf[:, idx:idx + 1]
        self.op("vector", lambda e, ap=ap, key=key: e.memset(ap, key),
                w=("constbuf",), label=f"const_{idx}")
        self.nc.const_aps.register(F32, key, ap)
        self.cvals[key] = True
        return key

    def row(self, p, seg=0, lo=0, hi=T):
        return self.rows[p:p + 1, seg * T + lo: seg * T + hi]

    def layernorm(self, tag):
        op = self.op
        stL = ExitStack()
        self.mrs_sb = self.sb([128, T], F32, "mrs_sb", stack=stL)
        self.ln_tmp = self.sb([128, T], F32, "ln_tmp", stack=stL)
        op("vector", lambda e: e.tensor_copy(self.u[:], self.x[:]),
           r=self.x_tags(), w=self.u_tags(), label=f"{tag}_shadow")
        for j in range(4):
            def mmsx(e, j=j):
                ins = None
                for dc in range(8):
                    ins = e.matmul(
                        self.PS[0:1, j * 512:(j + 1) * 512],
                        self.ones_col_b[:],
                        self.u[:, dc * T + j * 512: dc * T + (j + 1) * 512],
                        start=(dc == 0), stop=(dc == 7))
                return ins
            op("tensor", mmsx, r=self.u_tags() + ("ones_col_b",),
               w=(f"ps{j}",), label=f"{tag}_sx{j}")
        op("vector", lambda e: e.tensor_copy(self.row(32, 0), self.PS[0:1, 0:2048]),
           r=("ps0", "ps1", "ps2", "ps3"), w=("r32_0",), label=f"{tag}_evsx")
        op("scalar", lambda e: e.activation(self.u[:], self.u[:], AF.Square),
           r=self.u_tags(), w=self.u_tags(), label=f"{tag}_sq")
        for j in range(4):
            def mmsq(e, j=j):
                ins = None
                for dc in range(8):
                    ins = e.matmul(
                        self.PS[0:1, 2048 + j * 512: 2048 + (j + 1) * 512],
                        self.ones_col_b[:],
                        self.u[:, dc * T + j * 512: dc * T + (j + 1) * 512],
                        start=(dc == 0), stop=(dc == 7))
                return ins
            op("tensor", mmsq, r=self.u_tags() + ("ones_col_b",),
               w=(f"ps{4 + j}",), label=f"{tag}_sq{j}")
        op("vector", lambda e: e.tensor_copy(self.row(32, 1), self.PS[0:1, 2048:4096]),
           r=("ps4", "ps5", "ps6", "ps7"), w=("r32_1",), label=f"{tag}_evsq")
        op("vector", lambda e: e.tensor_scalar_mul(self.row(64, 0), self.row(32, 0), 1.0 / D),
           r=("r32_0",), w=("r64_0",), label=f"{tag}_m")
        op("vector", lambda e: e.tensor_scalar_mul(self.row(32, 1), self.row(32, 1), 1.0 / D),
           r=("r32_1",), w=("r32_1",), label=f"{tag}_q")
        op("vector", lambda e: e.tensor_mul(self.row(32, 0), self.row(64, 0), self.row(64, 0)),
           r=("r64_0",), w=("r32_0",), label=f"{tag}_mm")
        op("vector", lambda e: e.tensor_sub(self.row(32, 1), self.row(32, 1), self.row(32, 0)),
           r=("r32_1", "r32_0"), w=("r32_1",), label=f"{tag}_var")
        op("vector", lambda e: e.tensor_scalar_add(self.row(32, 1), self.row(32, 1), 1e-5),
           r=("r32_1",), w=("r32_1",), label=f"{tag}_eps")
        op("vector", lambda e: e.reciprocal(self.row(32, 0), self.row(32, 1)),
           r=("r32_1",), w=("r32_0",), label=f"{tag}_recip")
        op("scalar", lambda e: e.activation(self.row(0, 0), self.row(32, 0), AF.Sqrt),
           r=("r32_0",), w=("r0_0",), label=f"{tag}_sqrt")
        op("vector", lambda e: e.tensor_mul(self.row(32, 0), self.row(0, 0), self.row(0, 0)),
           r=("r0_0",), w=("r32_0",), label=f"{tag}_n1")
        op("vector", lambda e: e.tensor_mul(self.row(32, 0), self.row(32, 0), self.row(32, 1)),
           r=("r32_0", "r32_1"), w=("r32_0",), label=f"{tag}_n2")
        op("vector", lambda e: e.tensor_scalar(self.row(0, 1), self.row(32, 0), -0.5, 1.5,
                                               ALU.mult, ALU.add),
           r=("r32_0",), w=("r0_1",), label=f"{tag}_n3")
        op("vector", lambda e: e.tensor_mul(self.row(0, 0), self.row(0, 0), self.row(0, 1)),
           r=("r0_0", "r0_1"), w=("r0_0",), label=f"{tag}_n4")
        op("vector", lambda e: e.tensor_copy(self.row(0, 1), self.row(64, 0)),
           r=("r64_0",), w=("r0_1",), label=f"{tag}_mcp")
        op("vector", lambda e: e.tensor_mul(self.row(0, 1), self.row(0, 0), self.row(0, 1)),
           r=("r0_0", "r0_1"), w=("r0_1",), label=f"{tag}_mrs")
        for j in range(4):
            bk = 1 + j
            op("tensor", lambda e, j=j, bk=bk: e.matmul(
                   self.PS[:, bk * 512:(bk + 1) * 512], self.ones_row_f[:],
                   self.row(0, 0, j * 512, (j + 1) * 512), start=True, stop=True),
               r=("r0_0", "ones_row_f"), w=(f"ps{bk}",), label=f"{tag}_bcr{j}")
        for j in range(4):
            bk = (5 + j) % 8
            op("tensor", lambda e, j=j, bk=bk: e.matmul(
                   self.PS[:, bk * 512:(bk + 1) * 512], self.ones_row_f[:],
                   self.row(0, 1, j * 512, (j + 1) * 512), start=True, stop=True),
               r=("r0_1", "ones_row_f"), w=(f"ps{bk}",), label=f"{tag}_bcm{j}")
            op("scalar", lambda e, j=j, bk=bk: e.copy(
                   self.mrs_sb[:, j * 512:(j + 1) * 512],
                   self.PS[:, bk * 512:(bk + 1) * 512]),
               r=(f"ps{bk}",), w=("mrs_sb",), label=f"{tag}_evm{j}")
        for dc in range(8):
            for j in range(4):
                sl = slice(dc * T + j * 512, dc * T + (j + 1) * 512)
                tsl = slice(j * 512, (j + 1) * 512)
                ttag = f"ln_tmp{j}"
                bk = 1 + j
                op("vector",
                   lambda e, sl=sl, tsl=tsl, bk=bk: e.tensor_mul(
                       self.ln_tmp[:, tsl], self.x[:, sl],
                       self.PS[:, bk * 512:(bk + 1) * 512]),
                   r=self.x_tags(dc) + (f"ps{bk}",), w=(ttag,),
                   label=f"{tag}_ap1_{dc}_{j}")
                op("vector",
                   lambda e, sl=sl, tsl=tsl: e.tensor_sub(
                       self.u[:, sl], self.ln_tmp[:, tsl], self.mrs_sb[:, tsl]),
                   r=(ttag, "mrs_sb"), w=self.u_tags(dc),
                   label=f"{tag}_ap2_{dc}_{j}")
        stL.close()
        self.sched.barrier()

    def prep_l0_ffn_weights(self):
        op, dma = self.op, self.dma
        self.cwcols = self.sb([128, 96], F32, "cwcols")
        self.cbcols = self.sb([128, 32], F32, "cbcols")
        st = ExitStack()
        wacc = self.sb([128, 2], F32, "wp_acc", stack=st)
        wred = self.sb([128, 1], F32, "wp_red", stack=st)
        swcol = self.sb([128, 2], F32, "wp_swcol", stack=st)
        wt = [self.sb([128, 4096], F32, f"wp_t{i}", stack=st) for i in range(2)]
        tern = [self.sb([128, 4096], BF16, f"wp_n{i}", stack=st) for i in range(2)]
        stg = [self.sb([128, 1024], BF16, f"wp_s{i}", stack=st) for i in range(2)]

        for wi, (hbm_name, nrow, ncol, scr, tag) in enumerate([
                ("L0_ffn_w1", 4096, 1024, self.w1qT, "w1"),
                ("L0_ffn_w2", 1024, 4096, self.w2qT, "w2")]):
            wh = self.hbm[hbm_name]
            grp = 4096 // ncol
            nload = (nrow // 128) // grp
            def ld_ap(li, grp=grp, ncol=ncol):
                return bass.AP(wh, li * grp * 128 * ncol,
                               [[ncol, 128], [128 * ncol, grp], [1, ncol]])
            op("vector", lambda e: e.memset(wacc[:, 0:1], 0.0),
               w=("wpacc",), label=f"{tag}_acc0")
            for li in range(nload):
                b = li % 2
                dma("qgp", wt[b][:], ld_ap(li), w=(f"wpt{b}",), label=f"{tag}_ld{li}")
                op("vector", lambda e, b=b: e.reduce_sum(
                       wred[:], wt[b][:], axis=AX.X, apply_absolute_value=True),
                   r=(f"wpt{b}",), w=("wpred",), label=f"{tag}_red{li}")
                op("vector", lambda e: e.tensor_add(wacc[:, 0:1], wacc[:, 0:1], wred[:]),
                   r=("wpred", "wpacc"), w=("wpacc",), label=f"{tag}_acc{li}")
            op("tensor", lambda e: e.transpose(self.PS[0:1, 0:128], wacc[:, 0:1],
                                               self.ident_f[:]),
               r=("wpacc", "ident_f"), w=("ps0",), label=f"{tag}_T")
            op("vector", lambda e: e.tensor_copy(self.row(0, 0, 0, 128), self.PS[0:1, 0:128]),
               r=("ps0",), w=("r0_0",), label=f"{tag}_evT")
            op("vector", lambda e: e.reduce_sum(self.row(0, 0, 0, 1), self.row(0, 0, 0, 128),
                                                axis=AX.X),
               r=("r0_0",), w=("r0_0",), label=f"{tag}_sum")
            N = float(nrow * ncol)
            op("vector", lambda e, N=N: e.tensor_scalar(
                   self.row(0, 0, 1, 2), self.row(0, 0, 0, 1), 1.0 / N, 1e-5,
                   ALU.mult, ALU.max),
               r=("r0_0",), w=("r0_0",), label=f"{tag}_c")
            op("vector", lambda e: e.reciprocal(self.row(0, 1, 0, 1), self.row(0, 0, 1, 2)),
               r=("r0_0",), w=("r0_1",), label=f"{tag}_s")
            op("vector", lambda e, wi=wi: e.tensor_copy(self.tiny[0:1, 1 + wi:2 + wi],
                                                        self.row(0, 0, 1, 2)),
               r=("r0_0",), w=(f"cw{wi}",), label=f"{tag}_csave")
            op("tensor", lambda e: e.matmul(self.PS[:, 512:513], self.ones_row_f[:],
                                            self.row(0, 1, 0, 1), start=True, stop=True),
               r=("r0_1", "ones_row_f"), w=("ps1",), label=f"{tag}_bcs")
            op("scalar", lambda e, wi=wi: e.copy(swcol[:, wi:wi + 1], self.PS[:, 512:513]),
               r=("ps1",), w=(f"swcol{wi}",), label=f"{tag}_evs")
            for li in range(nload):
                b = li % 2
                dma("qgp", wt[b][:], ld_ap(li), w=(f"wpt{b}",), label=f"{tag}_ld2_{li}")
                op("vector", lambda e, b=b, wi=wi: e.tensor_scalar_mul(
                       wt[b][:], wt[b][:], swcol[:, wi:wi + 1]),
                   r=(f"wpt{b}", f"swcol{wi}"), w=(f"wpt{b}",), label=f"{tag}_v{li}")
                op("vector", lambda e, b=b: e.tensor_single_scalar(
                       tern[b][:], wt[b][:], -0.5, ALU.is_le),
                   r=(f"wpt{b}",), w=(f"wptern{b}",), label=f"{tag}_c{li}")
                op("vector", lambda e, b=b: e.scalar_tensor_tensor(
                       tern[b][:], wt[b][:], 0.5, tern[b][:],
                       ALU.is_ge, ALU.subtract),
                   r=(f"wpt{b}", f"wptern{b}"), w=(f"wptern{b}",), label=f"{tag}_t{li}")
                for g0 in range(0, 32, 8):
                    bank = 2 + ((g0 // 8) % 2)
                    psb = self.PS[:, bank * 512:(bank + 1) * 512].bitcast(BF16)
                    for k in range(8):
                        blk = g0 + k
                        op("tensor", lambda e, b=b, blk=blk, psb=psb, k=k: e.transpose(
                               psb[:, k * 128:(k + 1) * 128],
                               tern[b][:, blk * 128:(blk + 1) * 128], self.ident_b[:]),
                           r=(f"wptern{b}", "ident_b"), w=(f"ps{bank}",),
                           label=f"{tag}_T{li}_{blk}")
                    op("scalar", lambda e, psb=psb, b=b: e.copy(
                           stg[b][:], psb[:]),
                       r=(f"ps{bank}",), w=(f"wpstg{b}",), label=f"{tag}_E{li}_{g0}")
                    row0 = (li * 32 + g0) * 128
                    dma("qsync", bass.AP(scr, row0 * 128,
                                         [[128, 128], [128 * 128, 8], [1, 128]]),
                        stg[b][:],
                        r=(f"wpstg{b}",), w=(f"wpout{tag}",),
                        label=f"{tag}_st{li}_{g0}")
        dma("qscal", self.cwcols[:], bass.AP(self.hbm["L0_ffn_conv_w"], 0,
                                             [[3, 128], [384, 32], [1, 3]]),
            w=("cwcols",), label="cw_ld")
        dma("qscal", self.cbcols[:], bass.AP(self.hbm["L0_ffn_conv_b"], 0,
                                             [[1, 128], [128, 32]]),
            w=("cbcols",), label="cb_ld", slow_ok=True)
        st.close()
        self.sched.barrier()

    def _absmax_row(self, tag, src_getter, nchunk, out_row, st):
        op = self.op
        assert nchunk <= 8
        amax16 = self.sb([128, 16], F32, f"{tag}_am16", stack=st)
        am128 = self.sb([16, 128], F32, f"{tag}_am128", stack=st)
        probe_ap, _ = src_getter(0)
        is_bf = probe_ap.dtype == BF16
        for tt in range(16):
            bank = tt % 2 if is_bf else (tt % 2) * 2
            if is_bf:
                psv = self.PS[:, bank * 512:(bank + 1) * 512].bitcast(BF16)
                wtags = (f"ps{bank}",)
            else:
                psv = self.PS[:, bank * 512:(bank + 2) * 512]
                wtags = (f"ps{bank}", f"ps{bank + 1}")
            for c in range(nchunk):
                apc, rtags = src_getter(c)
                ident = self.ident_b if is_bf else self.ident_f
                op("tensor", lambda e, apc=apc, tt=tt, c=c, psv=psv, ident=ident: e.transpose(
                       psv[:, c * 128:(c + 1) * 128],
                       apc[:, tt * 128:(tt + 1) * 128], ident[:]),
                   r=tuple(rtags) + ("ident_b", "ident_f"), w=wtags,
                   label=f"{tag}_amT{tt}_{c}")
            op("vector", lambda e, tt=tt, psv=psv, nchunk=nchunk: e.reduce_max(
                   amax16[:, tt:tt + 1], psv[:, 0:nchunk * 128],
                   axis=AX.X, apply_absolute_value=True),
               r=wtags, w=(f"{tag}_am16",), label=f"{tag}_amR{tt}")
        op("tensor", lambda e: e.transpose(self.PS[0:16, 2048:2176], amax16[:],
                                           self.ident_f[:]),
           r=(f"{tag}_am16", "ident_f"), w=("ps4",), label=f"{tag}_amT2")
        op("vector", lambda e: e.tensor_copy(am128[:], self.PS[0:16, 2048:2176]),
           r=("ps4",), w=(f"{tag}_am128",), label=f"{tag}_amE")
        self.dma("qgp", self.row(0, 0), am128[:],
                 r=(f"{tag}_am128",), w=("r0_0",), label=f"{tag}_amG")

    def _quant_rows(self, tag, wi):
        op = self.op
        op("vector", lambda e: e.tensor_scalar_max(self.row(32, 1), self.row(0, 0), 1e-5),
           r=("r0_0",), w=("r32_1",), label=f"{tag}_clip")
        op("vector", lambda e: e.reciprocal(self.row(0, 0), self.row(32, 1)),
           r=("r32_1",), w=("r0_0",), label=f"{tag}_recip")
        op("vector", lambda e: e.tensor_scalar_mul(self.row(0, 0), self.row(0, 0), 127.0),
           r=("r0_0",), w=("r0_0",), label=f"{tag}_sx")
        op("vector", lambda e: e.tensor_scalar_mul(self.row(0, 1), self.row(32, 1), 1.0 / 127.0),
           r=("r32_1",), w=("r0_1",), label=f"{tag}_inv0")
        op("vector", lambda e, wi=wi: e.tensor_scalar_mul(
               self.row(0, 1), self.row(0, 1), self.tiny[0:1, 1 + wi:2 + wi]),
           r=("r0_1", f"cw{wi}"), w=("r0_1",), label=f"{tag}_inv")

    def _bcast_rows(self, tag, sx_sb=None, inv_sb=None):
        op = self.op
        for j in range(4):
            op("tensor", lambda e, j=j: e.matmul(
                   self.PS[:, j * 512:(j + 1) * 512], self.ones_row_f[:],
                   self.row(0, 0, j * 512, (j + 1) * 512), start=True, stop=True),
               r=("r0_0", "ones_row_f"), w=(f"ps{j}",), label=f"{tag}_bs{j}")
            op("tensor", lambda e, j=j: e.matmul(
                   self.PS[:, 2048 + j * 512: 2048 + (j + 1) * 512],
                   self.ones_row_f[:],
                   self.row(0, 1, j * 512, (j + 1) * 512), start=True, stop=True),
               r=("r0_1", "ones_row_f"), w=(f"ps{4 + j}",), label=f"{tag}_bi{j}")
        if sx_sb is not None:
            op("scalar", lambda e: e.copy(sx_sb[:], self.PS[:, 0:2048]),
               r=("ps0", "ps1", "ps2", "ps3"), w=("ff_sxsb",), label=f"{tag}_esx")
            op("scalar", lambda e: e.copy(inv_sb[:], self.PS[:, 2048:4096]),
               r=("ps4", "ps5", "ps6", "ps7"), w=("ff_invsb",), label=f"{tag}_einv")

    def l0_ffn(self, call):
        op, dma = self.op, self.dma
        tg = f"ffn{call}"
        stC = ExitStack()
        scrA = self.sb([128, T], F32, "ffA", stack=stC)
        scrB = self.sb([128, T], F32, "ffB", stack=stC)
        iqt = self.sb([128, T], I32, "ffI", stack=stC)

        self._absmax_row(tg + "qu",
                         lambda c: (self.u[:, c * T:(c + 1) * T], self.u_tags(c)),
                         8, 0, stC)
        self._quant_rows(tg + "qu", 0)
        self._bcast_rows(tg + "qu")
        sx_ps = [self.PS[:, j * 512:(j + 1) * 512] for j in range(4)]
        inv_ps = [self.PS[:, 2048 + j * 512: 2048 + (j + 1) * 512] for j in range(4)]
        for dc in range(8):
            usl = self.u[:, dc * T:(dc + 1) * T]
            def qv(e, usl=usl):
                ins = None
                for j in range(4):
                    ins = e.tensor_mul(scrA[:, j * 512:(j + 1) * 512],
                                       usl[:, j * 512:(j + 1) * 512], sx_ps[j])
                return ins
            op("vector", qv,
               r=self.u_tags(dc) + ("ps0", "ps1", "ps2", "ps3"), w=("ffA",),
               label=f"{tg}_qv{dc}")
            op("scalar", lambda e, usl=usl: e.activation(scrB[:], usl, AF.Sign),
               r=self.u_tags(dc), w=("ffB",), label=f"{tg}_qs{dc}")
            op("vector", lambda e: e.scalar_tensor_tensor(
                   iqt[:], scrB[:], self.halfkk[:], scrA[:], ALU.mult, ALU.add),
               r=("ffA", "ffB", "halfkk"), w=("ffI",), label=f"{tg}_qi{dc}")
            def qq(e, usl=usl):
                ins = None
                for j in range(4):
                    ins = e.tensor_mul(usl[:, j * 512:(j + 1) * 512],
                                       iqt[:, j * 512:(j + 1) * 512], inv_ps[j])
                return ins
            op("vector", qq,
               r=("ffI", "ps4", "ps5", "ps6", "ps7"), w=self.u_tags(dc),
               label=f"{tg}_qq{dc}")

        st1 = ExitStack()
        h1 = [self.sb([128, T], F32, f"ffh1_{i}", stack=st1) for i in range(2)]
        cacc = [self.sb([128, T], F32, f"ffca_{i}", stack=st1) for i in range(2)]
        hout = [self.sb([128, T], BF16, f"ffho_{i}", stack=st1) for i in range(2)]
        w1l = [self.sb([128, 1024], BF16, f"ffw1l_{i}", stack=st1) for i in range(2)]
        op("vector", lambda e: e.memset(scrB[:], 0.0), w=("ffB",), label=f"{tg}_am0")
        for hc in range(32):
            b = hc % 2
            dma("qgp", w1l[b][:],
                bass.AP(self.w1qT, hc * 8 * 128 * 128,
                        [[128, 128], [128 * 128, 8], [1, 128]]),
                w=(f"ffw1l{b}",), label=f"{tg}_w1l{hc}")
            bk0 = (hc % 2) * 4
            ptags = tuple(f"ps{bk0 + i}" for i in range(4))
            for ts in range(4):
                def mm1(e, hc=hc, ts=ts, b=b, bk0=bk0):
                    ins = None
                    for dc in range(8):
                        ins = e.matmul(
                            self.PS[:, (bk0 + ts) * 512:(bk0 + ts + 1) * 512],
                            w1l[b][:, dc * 128:(dc + 1) * 128],
                            self.u[:, dc * T + ts * 512: dc * T + (ts + 1) * 512],
                            start=(dc == 0), stop=(dc == 7))
                    return ins
                op("tensor", mm1,
                   r=(f"ffw1l{b}",) + self.u_tags(),
                   w=(f"ps{bk0 + ts}",), label=f"{tg}_mm1_{hc}_{ts}")
            op("vector", lambda e, b=b, bk0=bk0: e.tensor_copy(
                   h1[b][:], self.PS[:, bk0 * 512: bk0 * 512 + 2048]),
               r=ptags, w=(f"ffh1{b}",), label=f"{tg}_ev1_{hc}")
            cw = self.cwcols
            op("vector", lambda e, b=b, hc=hc: e.tensor_scalar_mul(
                   cacc[b][:], h1[b][:], cw[:, hc * 3 + 1: hc * 3 + 2]),
               r=(f"ffh1{b}", "cwcols"), w=(f"ffca{b}",), label=f"{tg}_cv1_{hc}")
            op("vector", lambda e, b=b, hc=hc: e.scalar_tensor_tensor(
                   cacc[b][:, 1:T], h1[b][:, 0:T - 1], cw[:, hc * 3: hc * 3 + 1],
                   cacc[b][:, 1:T], ALU.mult, ALU.add),
               r=(f"ffh1{b}", "cwcols", f"ffca{b}"), w=(f"ffca{b}",),
               label=f"{tg}_cv0_{hc}")
            op("vector", lambda e, b=b, hc=hc: e.scalar_tensor_tensor(
                   cacc[b][:, 0:T - 1], h1[b][:, 1:T], cw[:, hc * 3 + 2: hc * 3 + 3],
                   cacc[b][:, 0:T - 1], ALU.mult, ALU.add),
               r=(f"ffh1{b}", "cwcols", f"ffca{b}"), w=(f"ffca{b}",),
               label=f"{tg}_cv2_{hc}")
            op("scalar", lambda e, b=b, hc=hc: e.activation(
                   hout[b][:], cacc[b][:], AF.Silu,
                   bias=self.cbcols[:, hc:hc + 1]),
               r=(f"ffca{b}", "cbcols"), w=(f"ffho{b}",), label=f"{tg}_silu{hc}")
            op("vector", lambda e, b=b: e.scalar_tensor_tensor(
                   scrA[:], hout[b][:], -1.0, hout[b][:], ALU.mult, ALU.max),
               r=(f"ffho{b}",), w=("ffA",), label=f"{tg}_ab{hc}")
            op("vector", lambda e: e.tensor_max(scrB[:], scrB[:], scrA[:]),
               r=("ffA", "ffB"), w=("ffB",), label=f"{tg}_am{hc}")
            dma("qscal", bass.AP(self.h_scr, hc * 128 * T, [[T, 128], [1, T]]),
                hout[b][:],
                r=(f"ffho{b}",), w=(f"ffhs{hc}",), label=f"{tg}_hst{hc}")
        self._absmax_row(tg + "qh", lambda c: (scrB[:], ("ffB",)), 1, 0, st1)
        self._quant_rows(tg + "qh", 1)
        st1.close()
        self.sched.barrier()

        st2 = ExitStack()
        sx_sb = self.sb([128, T], F32, "ffSX", stack=st2)
        inv_sb = self.sb([128, T], F32, "ffIV", stack=st2)
        hbuf = [self.sb([128, T], BF16, f"ffhb_{i}", stack=st2) for i in range(2)]
        w2lbuf = self.sb([128, 8 * 1024], BF16, "ffw2l", stack=st2)
        self._bcast_rows(tg + "qh", sx_sb, inv_sb)
        for g in range(4):
            for k in range(8):
                hc = g * 8 + k
                b = hc % 2
                dma("qgp", hbuf[b][:],
                    bass.AP(self.h_scr, hc * 128 * T, [[T, 128], [1, T]]),
                    r=(f"ffhs{hc}",), w=(f"ffhb{b}",), label=f"{tg}_hld{hc}")
                dma("qsync", w2lbuf[:, k * 1024:(k + 1) * 1024],
                    bass.AP(self.w2qT, hc * 128 * 128,
                            [[128, 128], [32 * 128 * 128, 8], [1, 128]]),
                    w=(f"ffw2l{k}",), label=f"{tg}_w2l{hc}")
                op("vector", lambda e, b=b: e.tensor_mul(scrA[:], hbuf[b][:], sx_sb[:]),
                   r=(f"ffhb{b}", "ff_sxsb"), w=("ffA",), label=f"{tg}_hv{hc}")
                op("scalar", lambda e, b=b: e.activation(scrB[:], hbuf[b][:], AF.Sign),
                   r=(f"ffhb{b}",), w=("ffB",), label=f"{tg}_hs{hc}")
                op("vector", lambda e: e.scalar_tensor_tensor(
                       iqt[:], scrB[:], self.halfkk[:], scrA[:], ALU.mult, ALU.add),
                   r=("ffA", "ffB", "halfkk"), w=("ffI",), label=f"{tg}_hi{hc}")
                op("vector", lambda e, k=k: e.tensor_mul(
                       self.u[:, k * T:(k + 1) * T], iqt[:], inv_sb[:]),
                   r=("ffI", "ff_invsb"), w=self.u_tags(k), label=f"{tg}_hq{hc}")
            for ts in range(4):
                for oc in range(8):
                    def mm2(e, g=g, ts=ts, oc=oc):
                        ins = None
                        for k in range(8):
                            ins = e.matmul(
                                self.PS[:, oc * 512:(oc + 1) * 512],
                                w2lbuf[:, k * 1024 + oc * 128: k * 1024 + (oc + 1) * 128],
                                self.u[:, k * T + ts * 512: k * T + (ts + 1) * 512],
                                start=(k == 0), stop=(k == 7))
                        return ins
                    op("tensor", mm2,
                       r=self.u_tags()
                         + tuple(f"ffw2l{k}" for k in range(8)),
                       w=(f"ps{oc}",), label=f"{tg}_mm2_{g}_{ts}_{oc}")
                for oc in range(8):
                    xs = self.x[:, oc * T + ts * 512: oc * T + (ts + 1) * 512]
                    if g == 0:
                        pass
                    op("vector", lambda e, oc=oc, xs=xs: e.scalar_tensor_tensor(
                           xs, self.PS[:, oc * 512:(oc + 1) * 512], 0.5, xs,
                           ALU.mult, ALU.add),
                       r=(f"ps{oc}",) + self.x_tags(oc, range(ts * 4, ts * 4 + 4)),
                       w=self.x_tags(oc, range(ts * 4, ts * 4 + 4)),
                       label=f"{tg}_res_{g}_{ts}_{oc}")
        st2.close()
        stC.close()
        self.sched.barrier()

    def _rope_apply(self, tag, pre_lo, pre_hi, out32, trig, tmp, st, out_tag):
        op, dma = self.op, self.dma
        cos = trig[:, 0:T]
        sin = trig[:, T:2 * T]
        h1 = self.sb([16, T], BF16, f"{tag}_h1", stack=st)
        h2 = self.sb([16, T], BF16, f"{tag}_h2", stack=st)
        lo_tags = ("ps0", "ps1", "ps2", "ps3")
        hi_tags = ("ps4", "ps5", "ps6", "ps7")
        op("vector", lambda e: e.tensor_mul(tmp[0], pre_lo, cos),
           r=lo_tags + ("at_trig",), w=("r0_0",), label=f"{tag}_a")
        op("vector", lambda e: e.tensor_mul(tmp[1], pre_hi, sin),
           r=hi_tags + ("at_trig",), w=("r0_1",), label=f"{tag}_b")
        op("vector", lambda e: e.tensor_sub(h1[:], tmp[0], tmp[1]),
           r=("r0_0", "r0_1"), w=(f"{tag}_h1",), label=f"{tag}_c")
        op("vector", lambda e: e.tensor_mul(tmp[0], pre_hi, cos),
           r=hi_tags + ("at_trig",), w=("r0_0",), label=f"{tag}_d")
        op("vector", lambda e: e.tensor_mul(tmp[1], pre_lo, sin),
           r=lo_tags + ("at_trig",), w=("r0_1",), label=f"{tag}_e")
        op("vector", lambda e: e.tensor_add(h2[:], tmp[0], tmp[1]),
           r=("r0_0", "r0_1"), w=(f"{tag}_h2",), label=f"{tag}_f")
        dma("qgp", out32[0:16, :], h1[:], r=(f"{tag}_h1",), w=(out_tag,),
            label=f"{tag}_g")
        dma("qgp", out32[16:32, :], h2[:], r=(f"{tag}_h2",), w=(out_tag,),
            label=f"{tag}_h")

    def _build_VT(self, tg, VT, ckv, wvu_b, NH):
        op = self.op
        for tt in range(16):
            bk = (tt % 2) * 2
            for hi in range(2):
                op("tensor", lambda e, tt=tt, hi=hi, bk=bk: e.matmul(
                       self.PS[:, (bk + hi) * 512:(bk + hi + 1) * 512],
                       ckv[:, tt * 128:(tt + 1) * 128],
                       wvu_b[:, hi * 512:(hi + 1) * 512],
                       start=True, stop=True),
                   r=("at_ckv", "at_wvu"), w=(f"ps{bk + hi}",),
                   label=f"{tg}_vt{tt}_{hi}")
            def vev(e, tt=tt, bk=bk):
                ins = None
                for h8 in range(16):
                    src_c = bk * 512 + h8 * 64
                    ins = e.tensor_copy(
                        VT[:, tt * (16 * 65) + h8 * 65: tt * (16 * 65) + h8 * 65 + 64],
                        self.PS[:, src_c:src_c + 64])
                return ins
            op("vector", vev, r=(f"ps{bk}", f"ps{bk + 1}"), w=("at_VT",),
               label=f"{tg}_vte{tt}")
        op("vector", lambda e: e.memset(
               bass.AP(VT, 64, [[16 * 65 * 16, 128], [65, 16 * 16], [1, 1]]), 1.0),
           r=("at_VT",), w=("at_VT",), label=f"{tg}_vt1")


    def l0_attn(self):
        op, dma = self.op, self.dma
        tg = "att"
        NH, HD, DR = 16, 64, 32
        SC = 1.0 / math.sqrt(HD + DR)
        self.constf(SC)
        self.constf(-1.0)
        self.constf(math.pi / 2.0)
        st = ExitStack()
        attn_f = self.sb([128, 8 * T], BF16, "at_out", stack=st)
        ckv = self.sb([128, T], BF16, "at_ckv", stack=st)
        Q97 = self.sb([97, T], BF16, "at_Q97", stack=st)
        K97 = self.sb([97, T], BF16, "at_K97", stack=st)
        PT = [self.sb([128, 512], BF16, f"at_PT{i}", stack=st) for i in range(2)]
        masks = self.sb([128, 4 * 512], BF16, "at_mask", stack=st)
        tmp16 = [self.rows[0:16, 0:T], self.rows[0:16, T:2 * T]]
        it16 = self.rows[0:16, T:2 * T].bitcast(I32)
        wbuf = [self.sb([128, 512], F32, "at_wf", stack=st)] * 2
        wku_b = self.sb([128, 1024], BF16, "at_wku", stack=st)
        wvu_b = self.sb([128, 1024], BF16, "at_wvu", stack=st)
        stW = ExitStack()
        trig = self.sb([16, 2 * T], BF16, "at_trig", stack=stW)
        wkv_b = self.sb([128, 8 * 128], BF16, "at_wkv", stack=stW)
        wkr_b = self.sb([128, 32], BF16, "at_wkr", stack=stW)
        wqr_b = self.sb([128, 8 * 32], BF16, "at_wqr", stack=stW)

        def ldcast(hname, ncols, dst, dtag, ap_dims=None, bi=0):
            dims = ap_dims or [[ncols, 128], [1, ncols]]
            for c0 in range(0, ncols, 512):
                cw_ = min(512, ncols - c0)
                if ap_dims is None:
                    src = bass.AP(self.hbm[hname], c0, [[ncols, 128], [1, cw_]])
                else:
                    assert ncols <= 512
                    src = bass.AP(self.hbm[hname], 0, ap_dims)
                dma("qgp", wbuf[bi][:, 0:cw_], src,
                    w=("at_wf0",), label=f"{tg}_{dtag}ld{c0}")
                op("vector", lambda e, c0=c0, cw_=cw_: e.tensor_copy(
                       dst[:, c0:c0 + cw_], wbuf[bi][:, 0:cw_]),
                   r=("at_wf0",), w=(dtag,), label=f"{tg}_{dtag}c{c0}")
        for g4 in range(2):
            dma("qgp", wbuf[0][:],
                bass.AP(self.hbm["L0_attn_W_kv_down"], g4 * 4 * 128 * 128,
                        [[128, 128], [128 * 128, 4], [1, 128]]),
                w=("at_wf0",), label=f"{tg}_kvld{g4}")
            op("vector", lambda e, g4=g4: e.tensor_copy(
                   wkv_b[:, g4 * 512:(g4 + 1) * 512], wbuf[0][:]),
               r=("at_wf0",), w=("at_wkv",), label=f"{tg}_kvc{g4}")
        ldcast("L0_attn_W_k_up", 1024, wku_b, "at_wku", bi=1)
        ldcast("L0_attn_W_v_up", 1024, wvu_b, "at_wvu", bi=0)
        ldcast("L0_attn_W_k_rope", 32, wkr_b, "at_wkr", bi=1)
        ldcast("L0_attn_W_q_rope", 256, wqr_b, "at_wqr",
               ap_dims=[[32, 128], [4096, 8], [1, 32]], bi=0)

        stM = ExitStack()
        itc = self.sb([128, 512], F32, "at_itc", stack=stM)
        itr = self.sb([128, 512], F32, "at_itr", stack=stM)
        op("gpsimd", lambda e: e.iota(itc[:], [[1, 512]], channel_multiplier=0,
                                      allow_small_or_imprecise_dtypes=True),
           w=("at_itc",), label=f"{tg}_ic")
        op("gpsimd", lambda e: e.iota(itr[:], [[0, 512]], channel_multiplier=1,
                                      allow_small_or_imprecise_dtypes=True),
           w=("at_itr",), label=f"{tg}_ir")
        for di in range(4):
            op("vector", lambda e, di=di: e.scalar_tensor_tensor(
                   masks[:, di * 512:(di + 1) * 512], itr[:], float(di * 128),
                   itc[:], ALU.add, ALU.is_le),
               r=("at_itr", "at_itc"), w=("at_mask",), label=f"{tg}_mk{di}")
        stM.close()
        self.sched.barrier()

        op("gpsimd", lambda e: e.iota(self.row(0, 0, 0, 16), [[1, 16]],
                                      channel_multiplier=0,
                                      allow_small_or_imprecise_dtypes=True),
           w=("r0_0",), label=f"{tg}_fi")
        op("vector", lambda e: e.tensor_scalar_mul(
               self.row(0, 0, 0, 16), self.row(0, 0, 0, 16),
               -math.log(10000.0) / 16.0),
           r=("r0_0",), w=("r0_0",), label=f"{tg}_fs")
        op("scalar", lambda e: e.activation(self.row(0, 0, 0, 16),
                                            self.row(0, 0, 0, 16), AF.Exp),
           r=("r0_0",), w=("r0_0",), label=f"{tg}_fe")
        op("gpsimd", lambda e: e.iota(self.row(0, 1), [[1, T]], channel_multiplier=0,
                                      allow_small_or_imprecise_dtypes=True),
           w=("r0_1",), label=f"{tg}_ti")
        for j in range(4):
            op("tensor", lambda e, j=j: e.matmul(
                   self.PS[0:16, j * 512:(j + 1) * 512],
                   self.row(0, 0, 0, 16), self.row(0, 1, j * 512, (j + 1) * 512),
                   start=True, stop=True),
               r=("r0_0", "r0_1"), w=(f"ps{j}",), label=f"{tg}_th{j}")
        TP = 2.0 * math.pi
        op("vector", lambda e: e.tensor_scalar(
               it16, self.PS[0:16, 0:2048], 1.0 / TP, self.halfkk[0:16, 0:1],
               ALU.mult, ALU.add),
           r=("ps0", "ps1", "ps2", "ps3", "halfkk"), w=("r0_1",), label=f"{tg}_rk")
        op("vector", lambda e: e.tensor_copy(tmp16[0], it16),
           r=("r0_1",), w=("r0_0",), label=f"{tg}_kf")
        op("vector", lambda e: e.scalar_tensor_tensor(
               tmp16[1], tmp16[0], -TP, self.PS[0:16, 0:2048], ALU.mult, ALU.add),
           r=("r0_0", "ps0", "ps1", "ps2", "ps3"), w=("r0_1",), label=f"{tg}_thr")
        op("scalar", lambda e: e.activation(trig[:, T:2 * T], tmp16[1], AF.Sin),
           r=("r0_1",), w=("at_trig",), label=f"{tg}_sin")
        op("vector", lambda e: e.scalar_tensor_tensor(
               tmp16[0], tmp16[1], -1.0, tmp16[1], ALU.mult, ALU.max),
           r=("r0_1",), w=("r0_0",), label=f"{tg}_abs")
        op("scalar", lambda e: e.activation(trig[:, 0:T], tmp16[0], AF.Sin,
                                            scale=-1.0, bias=math.pi / 2.0),
           r=("r0_0",), w=("at_trig",), label=f"{tg}_cos")

        for ts in range(4):
            def mmc(e, ts=ts):
                ins = None
                for dc in range(8):
                    ins = e.matmul(
                        self.PS[:, 2048 + ts * 512: 2048 + (ts + 1) * 512],
                        wkv_b[:, dc * 128:(dc + 1) * 128],
                        self.u[:, dc * T + ts * 512: dc * T + (ts + 1) * 512],
                        start=(dc == 0), stop=(dc == 7))
                return ins
            op("tensor", mmc, r=("at_wkv",) + self.u_tags(),
               w=(f"ps{4 + ts}",), label=f"{tg}_ckv{ts}")
        op("vector", lambda e: e.tensor_copy(ckv[:], self.PS[:, 2048:4096]),
           r=("ps4", "ps5", "ps6", "ps7"), w=("at_ckv",), label=f"{tg}_ckve")

        for hi in range(2):
            for ts in range(4):
                op("tensor", lambda e, hi=hi, ts=ts: e.matmul(
                       self.PS[0:16, hi * 2048 + ts * 512: hi * 2048 + (ts + 1) * 512],
                       wkr_b[:, hi * 16:(hi + 1) * 16],
                       ckv[:, ts * 512:(ts + 1) * 512],
                       start=True, stop=True),
                   r=("at_wkr", "at_ckv"), w=(f"ps{hi * 4 + ts}",),
                   label=f"{tg}_krp{hi}_{ts}")
        self._rope_apply(tg + "kr", self.PS[0:16, 0:2048], self.PS[0:16, 2048:4096],
                         K97[64:96, :], trig, tmp16, stW, "at_K97")
        for hi in range(2):
            for ts in range(4):
                def mmqr(e, hi=hi, ts=ts):
                    ins = None
                    for dc in range(8):
                        ins = e.matmul(
                            self.PS[0:16, hi * 2048 + ts * 512: hi * 2048 + (ts + 1) * 512],
                            wqr_b[:, dc * 32 + hi * 16: dc * 32 + hi * 16 + 16],
                            self.u[:, dc * T + ts * 512: dc * T + (ts + 1) * 512],
                            start=(dc == 0), stop=(dc == 7))
                    return ins
                op("tensor", mmqr, r=("at_wqr",) + self.u_tags(),
                   w=(f"ps{hi * 4 + ts}",), label=f"{tg}_qrp{hi}_{ts}")
        self._rope_apply(tg + "qr", self.PS[0:16, 0:2048], self.PS[0:16, 2048:4096],
                         Q97[64:96, :], trig, tmp16, stW, "at_Q97")

        stW.close()
        self.sched.barrier()
        VT = self.sb([128, 16 * (NH * 65)], BF16, "at_VT", stack=st)
        wq_b = [self.sb([128, 8 * 64], BF16, "at_wq", stack=st)] * 2
        wo_b = [self.sb([128, 8 * 128], BF16, "at_wo", stack=st)] * 2
        rmax = self.sb([128, 16], F32, "at_rmax", stack=st)
        rm128 = self.rows[0:16, 1024:1152]
        osb = self.rows[0:64, 0:256].bitcast(BF16)
        linv = self.rows[0:1, 512:1024]
        self._build_VT(tg, VT, ckv, wvu_b, NH)
        op("vector", lambda e: e.memset(K97[96:97, :], 1.0),
           r=("at_K97",), w=("at_K97",), label=f"{tg}_k97o")

        for h in range(NH):
            hb = h % 2
            dma("qgp", wbuf[hb][:],
                bass.AP(self.hbm["L0_attn_W_q"], h * 64,
                        [[1024, 128], [128 * 1024, 8], [1, 64]]),
                w=("at_wf0",), label=f"{tg}_wq{h}")
            op("vector", lambda e, hb=hb: e.tensor_copy(wq_b[hb][:], wbuf[hb][:, 0:512]),
               r=("at_wf0",), w=("at_wq0",), label=f"{tg}_wqc{h}")
            for ts in range(4):
                def mmq(e, ts=ts, hb=hb):
                    ins = None
                    for dc in range(8):
                        ins = e.matmul(
                            self.PS[0:64, ts * 512:(ts + 1) * 512],
                            wq_b[hb][:, dc * 64:(dc + 1) * 64],
                            self.u[:, dc * T + ts * 512: dc * T + (ts + 1) * 512],
                            start=(dc == 0), stop=(dc == 7))
                    return ins
                op("tensor", mmq, r=("at_wq0",) + self.u_tags(),
                   w=(f"ps{ts}",), label=f"{tg}_q{h}_{ts}")
                op("tensor", lambda e, h=h, ts=ts: e.matmul(
                       self.PS[0:64, 2048 + ts * 512: 2048 + (ts + 1) * 512],
                       wku_b[:, h * 64:(h + 1) * 64],
                       ckv[:, ts * 512:(ts + 1) * 512],
                       start=True, stop=True),
                   r=("at_wku", "at_ckv"), w=(f"ps{4 + ts}",), label=f"{tg}_k{h}_{ts}")
            op("vector", lambda e: e.tensor_copy(Q97[0:64, :], self.PS[0:64, 0:2048]),
               r=("ps0", "ps1", "ps2", "ps3"), w=("at_Q97",), label=f"{tg}_qe{h}")
            op("vector", lambda e: e.tensor_copy(K97[0:64, :], self.PS[0:64, 2048:4096]),
               r=("ps4", "ps5", "ps6", "ps7"), w=("at_K97",), label=f"{tg}_ke{h}")
            for qi in range(16):
                nk = 128 * (qi + 1)
                nsl = (nk + 511) // 512
                def mmp1(e, qi=qi, nk=nk, nsl=nsl):
                    ins = None
                    for sl in range(nsl):
                        n0 = sl * 512
                        n1 = min(nk, n0 + 512)
                        ins = e.matmul(
                            self.PS[:, sl * 512: sl * 512 + (n1 - n0)],
                            Q97[0:96, qi * 128:(qi + 1) * 128],
                            K97[0:96, n0:n1],
                            start=True, stop=True)
                    return ins
                op("tensor", mmp1, r=("at_Q97", "at_K97"),
                   w=tuple(f"ps{b2}" for b2 in range(nsl)),
                   label=f"{tg}_p1_{h}_{qi}")
                op("vector", lambda e, qi=qi, nk=nk, nsl=nsl: e.reduce_max(
                       rmax[:, qi:qi + 1], self.PS[:, 0:nk], axis=AX.X,
                       negate=True),
                   r=tuple(f"ps{b2}" for b2 in range(nsl)),
                   w=("at_rmax",), label=f"{tg}_rx_{h}_{qi}")
            op("tensor", lambda e: e.transpose(self.PS[0:16, 3584:3712], rmax[:],
                                               self.ident_f[:]),
               r=("at_rmax", "ident_f"), w=("ps7",), label=f"{tg}_rxT{h}")
            op("vector", lambda e: e.tensor_copy(rm128[:], self.PS[0:16, 3584:3712]),
               r=("ps7",), w=("r0_0",), label=f"{tg}_rxE{h}")
            dma("qgp", Q97[96:97, :], rm128[:],
                r=("r0_0",), w=("at_Q97",), label=f"{tg}_rxG{h}")
            for qc in range(4):
                nkt = (qc + 1) * 4
                obank = 6
                for kt in range(nkt):
                    pb = kt % 2
                    sbk = 4 + pb
                    op("tensor", lambda e, kt=kt, qc=qc, sbk=sbk: e.matmul(
                           self.PS[:, sbk * 512:(sbk + 1) * 512],
                           K97[:, kt * 128:(kt + 1) * 128],
                           Q97[:, qc * 512:(qc + 1) * 512],
                           start=True, stop=True),
                       r=("at_Q97", "at_K97"), w=(f"ps{sbk}",),
                       label=f"{tg}_p2_{h}_{qc}_{kt}")
                    op("scalar", lambda e, pb=pb, sbk=sbk: e.activation(
                           PT[pb][:], self.PS[:, sbk * 512:(sbk + 1) * 512],
                           AF.Exp, scale=SC),
                       r=(f"ps{sbk}",), w=(f"at_PT{pb}",),
                       label=f"{tg}_ex_{h}_{qc}_{kt}")
                    di = kt - qc * 4
                    if di >= 0:
                        op("vector", lambda e, pb=pb, di=di: e.tensor_mul(
                               PT[pb][:], PT[pb][:],
                               masks[:, di * 512:(di + 1) * 512]),
                           r=(f"at_PT{pb}", "at_mask"), w=(f"at_PT{pb}",),
                           label=f"{tg}_msk_{h}_{qc}_{kt}")
                    op("tensor",
                       lambda e, kt=kt, pb=pb, h=h, nkt=nkt, obank=obank: e.matmul(
                           self.PS[0:65, obank * 512:(obank + 1) * 512],
                           VT[:, kt * (16 * 65) + h * 65: kt * (16 * 65) + (h + 1) * 65],
                           PT[pb][:],
                           start=(kt == 0), stop=(kt == nkt - 1)),
                       r=(f"at_PT{pb}", "at_VT"), w=(f"ps{obank}",),
                       label=f"{tg}_av_{h}_{qc}_{kt}")
                op("vector", lambda e, obank=obank: e.reciprocal(
                       linv[:], self.PS[64:65, obank * 512:(obank + 1) * 512]),
                   r=(f"ps{obank}",), w=("r0_0",), label=f"{tg}_li_{h}_{qc}")
                op("tensor", lambda e: e.matmul(
                       self.PS[0:64, 3584:4096], self.ones_row_f[0:1, 0:64],
                       linv[:], start=True, stop=True),
                   r=("r0_0", "ones_row_f"), w=("ps7",), label=f"{tg}_lb_{h}_{qc}")
                op("scalar", lambda e, obank=obank: e.copy(
                       osb[:], self.PS[0:64, obank * 512:(obank + 1) * 512]),
                   r=(f"ps{obank}",), w=("r32_0",), label=f"{tg}_oe_{h}_{qc}")
                dc0 = h // 2
                r0 = 64 * (h % 2)
                op("vector", lambda e, dc0=dc0, r0=r0, qc=qc: e.tensor_mul(
                       attn_f[r0:r0 + 64, dc0 * T + qc * 512: dc0 * T + (qc + 1) * 512],
                       osb[:], self.PS[0:64, 3584:4096]),
                   r=("r32_0", "ps7"), w=(f"at_o{dc0}",),
                   label=f"{tg}_on_{h}_{qc}")

        for oc in range(8):
            hb = oc % 2
            for g4 in range(2):
                dma("qgp", wbuf[hb][:],
                    bass.AP(self.hbm["L0_attn_W_o"], oc * 128 + g4 * 4 * 128 * 1024,
                            [[1024, 128], [128 * 1024, 4], [1, 128]]),
                    w=("at_wf0",), label=f"{tg}_wo{oc}_{g4}")
                op("vector", lambda e, hb=hb, g4=g4: e.tensor_copy(
                       wo_b[hb][:, g4 * 512:(g4 + 1) * 512], wbuf[hb][:]),
                   r=("at_wf0",), w=("at_wo0",), label=f"{tg}_woc{oc}_{g4}")
            for ts in range(4):
                bk = (oc % 2) * 4 + ts
                def mmo(e, ts=ts, hb=hb, bk=bk):
                    ins = None
                    for dc in range(8):
                        ins = e.matmul(
                            self.PS[:, bk * 512:(bk + 1) * 512],
                            wo_b[hb][:, dc * 128:(dc + 1) * 128],
                            attn_f[:, dc * T + ts * 512: dc * T + (ts + 1) * 512],
                            start=(dc == 0), stop=(dc == 7))
                    return ins
                op("tensor", mmo,
                   r=("at_wo0",) + tuple(f"at_o{d}" for d in range(8)),
                   w=(f"ps{bk}",), label=f"{tg}_mo_{oc}_{ts}")
                op("vector", lambda e, oc=oc, ts=ts, bk=bk: e.tensor_add(
                       self.x[:, oc * T + ts * 512: oc * T + (ts + 1) * 512],
                       self.x[:, oc * T + ts * 512: oc * T + (ts + 1) * 512],
                       self.PS[:, bk * 512:(bk + 1) * 512]),
                   r=(f"ps{bk}",) + self.x_tags(oc, range(ts * 4, ts * 4 + 4)),
                   w=self.x_tags(oc, range(ts * 4, ts * 4 + 4)),
                   label=f"{tg}_res_{oc}_{ts}")
        st.close()
        self.sched.barrier()

    def prep_l1_ffn_weights(self):
        op, dma = self.op, self.dma
        tg = "p1f"
        st = ExitStack()
        wf = [self.sb([128, 2048], F32, f"p1f_wf{i}", stack=st) for i in range(2)]
        wb = [self.sb([128, 2048], BF16, f"p1f_wb{i}", stack=st) for i in range(2)]
        stg = [self.sb([128, 2048], BF16, f"p1f_stg{i}", stack=st) for i in range(2)]
        wc = self.hbm["L1_ffn_w_cproj"]
        for oc in range(8):
            for slab in range(8):
                b = slab % 2
                dma("qgp", wf[b][:],
                    bass.AP(wc, oc * 128 * 16384 + slab * 2048,
                            [[16384, 128], [1, 2048]]),
                    w=(f"p1f_wf{b}",), label=f"{tg}_ld{oc}_{slab}")
                op("vector", lambda e, b=b: e.tensor_copy(wb[b][:], wf[b][:]),
                   r=(f"p1f_wf{b}",), w=(f"p1f_wb{b}",), label=f"{tg}_c{oc}_{slab}")
                for g8 in range(2):
                    bank = 2 + g8
                    psb = self.PS[:, bank * 512:(bank + 1) * 512].bitcast(BF16)
                    for k in range(8):
                        blk = g8 * 8 + k
                        op("tensor", lambda e, b=b, blk=blk, psb=psb, k=k: e.transpose(
                               psb[:, k * 128:(k + 1) * 128],
                               wb[b][:, blk * 128:(blk + 1) * 128], self.ident_b[:]),
                           r=(f"p1f_wb{b}", "ident_b"), w=(f"ps{bank}",),
                           label=f"{tg}_T{oc}_{slab}_{blk}")
                    op("scalar", lambda e, psb=psb, b=b, g8=g8: e.copy(
                           stg[b][:, g8 * 1024:(g8 + 1) * 1024], psb[:]),
                       r=(f"ps{bank}",), w=(f"p1f_stg{b}",),
                       label=f"{tg}_E{oc}_{slab}_{g8}")
                dma("qsync", bass.AP(self.wcT, ((slab * 16) * 8 + oc) * 16384,
                                     [[128, 128], [8 * 16384, 16], [1, 128]]),
                    stg[b][:],
                    r=(f"p1f_stg{b}",), w=("wcT_out",), label=f"{tg}_st{oc}_{slab}")
        st.close()
        self.sched.barrier()
        st2 = ExitStack()
        Cst = self.sb([128, 32 * 256], BF16, "l1_Cstb", stack=st2)
        mk = self.sb([128, 16], F32, "p1f_mk", stack=st2)
        upf = self.sb([128, 16], F32, "p1f_up", stack=st2)
        mkT = self.sb([16, 128], BF16, "p1f_mkT", stack=st2)
        upT = self.sb([16, 128], BF16, "p1f_upT", stack=st2)
        bm1 = self.sb([32, 128], F32, "p1f_bm1", stack=st2)
        bm2 = self.sb([32, 128], F32, "p1f_bm2", stack=st2)
        bdm = self.sb([32, 128], BF16, "p1f_bdm", stack=st2)
        op("gpsimd", lambda e: e.iota(bm1[:], [[0, 128]], channel_multiplier=1,
                                      allow_small_or_imprecise_dtypes=True),
           w=("p1f_bm1",), label=f"{tg}_bi1")
        op("gpsimd", lambda e: e.iota(bm2[:], [[1, 128]], channel_multiplier=0,
                                      allow_small_or_imprecise_dtypes=True),
           w=("p1f_bm2",), label=f"{tg}_bi2")
        op("vector", lambda e: e.tensor_single_scalar(bm1[:], bm1[:], 16.0, ALU.is_ge),
           r=("p1f_bm1",), w=("p1f_bm1",), label=f"{tg}_bg1")
        op("vector", lambda e: e.tensor_single_scalar(bm2[:], bm2[:], 64.0, ALU.is_ge),
           r=("p1f_bm2",), w=("p1f_bm2",), label=f"{tg}_bg2")
        op("vector", lambda e: e.tensor_tensor(bdm[:], bm1[:], bm2[:], ALU.is_equal),
           r=("p1f_bm1", "p1f_bm2"), w=("p1f_bdm",), label=f"{tg}_bdm")
        op("vector", lambda e, Cst=Cst: e.memset(Cst[:], 0.0),
           w=("l1_Cst",), label=f"{tg}_cst0")
        mkh = self.hbm["L1_ffn_rail_maker"]
        uph = self.hbm["L1_ffn_rail_up_proj"]
        for g in range(32):
            h, c = g // 8, g % 8
            dma("qgp", mk[:], bass.AP(mkh, (h * 64 + 8 * c) * 256,
                                      [[16, 128], [1, 16]]),
                w=("p1f_mk",), label=f"{tg}_mk{g}")
            op("tensor", lambda e: e.transpose(self.PS[0:16, 0:128],
                                               mk[:], self.ident_f[:]),
               r=("p1f_mk", "ident_f"), w=("ps0",), label=f"{tg}_mT{g}")
            op("vector", lambda e: e.tensor_copy(mkT[:], self.PS[0:16, 0:128]),
               r=("ps0",), w=("p1f_mkT",), label=f"{tg}_mE{g}")
            for m in range(4):
                dma("qgp", upf[:], bass.AP(uph, (h * 64 + 8 * c + 2 * m) * 1024,
                                           [[16, 128], [1, 16]]),
                    w=("p1f_up",), label=f"{tg}_up{g}_{m}")
                op("tensor", lambda e: e.transpose(self.PS[0:16, 512:640],
                                                   upf[:], self.ident_f[:]),
                   r=("p1f_up", "ident_f"), w=("ps1",), label=f"{tg}_uT{g}_{m}")
                op("vector", lambda e: e.tensor_copy(upT[:], self.PS[0:16, 512:640]),
                   r=("ps1",), w=("p1f_upT",), label=f"{tg}_uE{g}_{m}")
                op("tensor", lambda e, m=m: e.matmul(
                       self.PS[0:32, 1024:1152], mkT[:, m * 32:(m + 1) * 32],
                       upT[:], start=True, stop=True),
                   r=("p1f_mkT", "p1f_upT"), w=("ps2",), label=f"{tg}_cm{g}_{m}")
                if m < 3:
                    dst = Cst[32 * m:32 * (m + 1), g * 256:g * 256 + 128]
                else:
                    dst = Cst[96:128, g * 256 + 128:g * 256 + 256]
                op("vector", lambda e, dst=dst: e.tensor_mul(
                       dst, self.PS[0:32, 1024:1152], bdm[:]),
                   r=("ps2", "p1f_bdm"), w=("l1_Cst",), label=f"{tg}_cw{g}_{m}")
        dma("qsync", self.CsT_hbm.ap(), Cst[:], r=("l1_Cst",), w=("CsT_out",),
            label=f"{tg}_cst_save")
        st2.close()
        self.sched.barrier()

    def l1_ffn(self, call):
        op, dma = self.op, self.dma
        tg = f"l1f{call}"
        st = ExitStack()
        Cst = self.sb([128, 32 * 256], BF16, "l1_Cst", stack=st)
        dma("qgp", Cst[:], self.CsT_hbm.ap(), r=("CsT_out",), w=("l1_Cst",),
            label=f"{tg}_cst_ld")
        hact = self.sb([128, 32 * 512], BF16, "l1_hact", stack=st)
        spill = self.sb([128, 8 * 512], F32, "l1_spill", stack=st)
        wl = [self.sb([128, 1024], BF16, f"l1_wl{i}", stack=st) for i in range(2)]
        for tc in range(4):
            for h in range(4):
                for c in range(8):
                    g = h * 8 + c
                    gl = c * 4
                    bsel = (c % 2) * 4
                    def mmr(e, g=g, c=c, tc=tc, bsel=bsel):
                        ins = None
                        for m in range(4):
                            if m < 3:
                                lh = Cst[32 * m:32 * (m + 1),
                                         g * 256:g * 256 + 128]
                                rh = self.u[32 * m:32 * (m + 1),
                                            c * T + tc * 512:c * T + (tc + 1) * 512]
                            else:
                                lh = Cst[64:128, g * 256 + 128:g * 256 + 256]
                                rh = self.u[64:128,
                                            c * T + tc * 512:c * T + (tc + 1) * 512]
                            ins = e.matmul(
                                self.PS[:, (bsel + m) * 512:(bsel + m + 1) * 512],
                                lh, rh, start=True, stop=True)
                        return ins
                    op("tensor", mmr, r=("l1_Cst",) + self.u_tags(c),
                       w=tuple(f"ps{bsel + m}" for m in range(4)),
                       label=f"{tg}_r{tc}_{g}")
                    op("scalar", lambda e, gl=gl, bsel=bsel: e.activation(
                           hact[:, gl * 512:(gl + 4) * 512],
                           self.PS[:, bsel * 512:(bsel + 4) * 512], AF.Gelu),
                       r=tuple(f"ps{bsel + m}" for m in range(4)),
                       w=("l1_hact",), label=f"{tg}_g{tc}_{g}")
                for i32 in range(32):
                    hc = h * 32 + i32
                    b = i32 % 2
                    dma("qgp", wl[b][:],
                        bass.AP(self.wcT, hc * 8 * 16384,
                                [[128, 128], [16384, 8], [1, 128]]),
                        w=(f"l1_wl{b}",), label=f"{tg}_w{tc}_{hc}")
                    def mmw(e, i32=i32, b=b):
                        ins = None
                        for oc in range(8):
                            ins = e.matmul(
                                self.PS[:, oc * 512:(oc + 1) * 512],
                                wl[b][:, oc * 128:(oc + 1) * 128],
                                hact[:, i32 * 512:(i32 + 1) * 512],
                                start=(i32 == 0), stop=(i32 == 31))
                        return ins
                    op("tensor", mmw, r=(f"l1_wl{b}", "l1_hact"),
                       w=tuple(f"ps{oc}" for oc in range(8)),
                       label=f"{tg}_m{tc}_{hc}")
                for oc in range(8):
                    if h == 0:
                        op("vector", lambda e, oc=oc: e.tensor_copy(
                               spill[:, oc * 512:(oc + 1) * 512],
                               self.PS[:, oc * 512:(oc + 1) * 512]),
                           r=(f"ps{oc}",), w=("l1_spill",), label=f"{tg}_s{tc}_{h}_{oc}")
                    else:
                        op("vector", lambda e, oc=oc: e.tensor_add(
                               spill[:, oc * 512:(oc + 1) * 512],
                               spill[:, oc * 512:(oc + 1) * 512],
                               self.PS[:, oc * 512:(oc + 1) * 512]),
                           r=(f"ps{oc}", "l1_spill"), w=("l1_spill",),
                           label=f"{tg}_s{tc}_{h}_{oc}")
            for oc in range(8):
                xs = self.x[:, oc * T + tc * 512: oc * T + (tc + 1) * 512]
                op("vector", lambda e, oc=oc, xs=xs: e.scalar_tensor_tensor(
                       xs, spill[:, oc * 512:(oc + 1) * 512], 0.5, xs,
                       ALU.mult, ALU.add),
                   r=("l1_spill",) + self.x_tags(oc, range(tc * 4, tc * 4 + 4)),
                   w=self.x_tags(oc, range(tc * 4, tc * 4 + 4)),
                   label=f"{tg}_x{tc}_{oc}")
        st.close()
        self.sched.barrier()

    def prep_l1_attn_weights(self):
        op, dma = self.op, self.dma
        tg = "p1a"
        st = ExitStack()
        wf = [self.sb([128, 2048], F32, f"p1a_wf{i}", stack=st) for i in range(2)]
        wb = [self.sb([128, 2048], BF16, f"p1a_wb{i}", stack=st) for i in range(2)]
        stg = [self.sb([128, 2048], BF16, f"p1a_stg{i}", stack=st) for i in range(2)]

        def tile_pass(hname, scratch, n_tiles, nblk, lbl):
            h = self.hbm[hname]
            ncol = nblk * 128
            for it in range(n_tiles):
                b = it % 2
                dma("qgp", wf[b][:, 0:ncol],
                    bass.AP(h, it * 128 * ncol, [[ncol, 128], [1, ncol]]),
                    w=(f"p1a_wf{b}",), label=f"{lbl}_ld{it}")
                op("vector", lambda e, b=b, ncol=ncol: e.tensor_copy(
                       wb[b][:, 0:ncol], wf[b][:, 0:ncol]),
                   r=(f"p1a_wf{b}",), w=(f"p1a_wb{b}",), label=f"{lbl}_c{it}")
                for g in range(nblk // 8):
                    bank = 2 + g % 2
                    psb = self.PS[:, bank * 512:(bank + 1) * 512].bitcast(BF16)
                    for k in range(8):
                        blk = g * 8 + k
                        op("tensor", lambda e, b=b, blk=blk, psb=psb, k=k:
                           e.transpose(psb[:, k * 128:(k + 1) * 128],
                                       wb[b][:, blk * 128:(blk + 1) * 128],
                                       self.ident_b[:]),
                           r=(f"p1a_wb{b}", "ident_b"), w=(f"ps{bank}",),
                           label=f"{lbl}_T{it}_{blk}")
                    op("scalar", lambda e, psb=psb, b=b, g=g: e.copy(
                           stg[b][:, g * 1024:(g + 1) * 1024], psb[:]),
                       r=(f"ps{bank}",), w=(f"p1a_stg{b}",),
                       label=f"{lbl}_E{it}_{g}")
                dma("qsync",
                    bass.AP(scratch, it * nblk * 16384,
                            [[128, 128], [16384, nblk], [1, 128]]),
                    stg[b][:, 0:ncol],
                    r=(f"p1a_stg{b}",), w=(f"{lbl}_out",), label=f"{lbl}_st{it}")

        tile_pass("L1_attn_W_in", self.winT, 32, 8, "pwi")
        tile_pass("L1_attn_W_x", self.wxT, 16, 16, "pwx")
        tile_pass("L1_attn_W_out", self.woT, 8, 16, "pwo")
        wxh = self.hbm["L1_attn_W_x"]
        dma("qgp", wf[0][0:32, 0:2048],
            bass.AP(wxh, 2048 * 2048, [[2048, 32], [1, 2048]]),
            w=("p1a_wf0",), label=f"{tg}_bcld")
        op("vector", lambda e: e.tensor_copy(wb[0][0:32, 0:2048], wf[0][0:32, 0:2048]),
           r=("p1a_wf0",), w=("p1a_wb0",), label=f"{tg}_bcc")
        psb = self.PS[:, 1024:1536].bitcast(BF16)
        for blk in range(16):
            op("tensor", lambda e, blk=blk, psb=psb: e.transpose(
                   psb[:, blk * 32:(blk + 1) * 32],
                   wb[0][0:32, blk * 128:(blk + 1) * 128],
                   self.ident_b[0:32, 0:32]),
               r=("p1a_wb0", "ident_b"), w=("ps2",), label=f"{tg}_bcT{blk}")
        op("scalar", lambda e, psb=psb: e.copy(stg[0][:, 0:512], psb[:, 0:512]),
           r=("ps2",), w=("p1a_stg0",), label=f"{tg}_bcE")
        dma("qsync",
            bass.AP(self.wxT, 256 * 16384, [[128, 128], [16384, 16], [1, 32]]),
            stg[0][:, 0:512],
            r=("p1a_stg0",), w=("pwx_out",), label=f"{tg}_bcst")
        st.close()
        self.sched.barrier()

    def l1_attn(self):
        op, dma = self.op, self.dma
        tg = "mam"
        L, W, TC = 64, 16 * 65, 512
        one = self.constf(1.0)
        st = ExitStack()
        sel = self.sb([48, 2048], BF16, "mam_sel", stack=st)
        stS = ExitStack()
        ncol = self.sb([48, 2048], F32, "mam_ncol", stack=stS)
        qv = self.sb([48, 4], F32, "mam_qv", stack=stS)
        op("gpsimd", lambda e: e.iota(ncol[:], [[1, 16], [0, 128]],
                                      channel_multiplier=0,
                                      allow_small_or_imprecise_dtypes=True),
           w=("mam_ncol",), label=f"{tg}_in")
        op("gpsimd", lambda e: e.iota(qv[:, 0:1], [[0, 1]], channel_multiplier=1,
                                      allow_small_or_imprecise_dtypes=True),
           w=("mam_qv",), label=f"{tg}_iq")
        op("vector", lambda e: e.tensor_single_scalar(
               qv[:, 1:2], qv[:, 0:1], 16.0, ALU.is_ge),
           r=("mam_qv",), w=("mam_qv",), label=f"{tg}_q1")
        op("vector", lambda e: e.tensor_single_scalar(
               qv[:, 2:3], qv[:, 0:1], 32.0, ALU.is_ge),
           r=("mam_qv",), w=("mam_qv",), label=f"{tg}_q2")
        op("vector", lambda e: e.scalar_tensor_tensor(
               qv[:, 3:4], qv[:, 1:2], -16.0, qv[:, 0:1], ALU.mult, ALU.add),
           r=("mam_qv",), w=("mam_qv",), label=f"{tg}_q3")
        op("vector", lambda e: e.scalar_tensor_tensor(
               qv[:, 3:4], qv[:, 2:3], -16.0, qv[:, 3:4], ALU.mult, ALU.add),
           r=("mam_qv",), w=("mam_qv",), label=f"{tg}_q4")
        op("vector", lambda e: e.tensor_tensor(
               sel[:], ncol[:], bass.AP(qv, 3, [[4, 48], [0, 2048]]),
               ALU.is_equal),
           r=("mam_ncol", "mam_qv"), w=("mam_sel",), label=f"{tg}_sel")
        stS.close()
        self.sched.barrier()
        xy = self.sb([128, 8240], BF16, "mam_xy", stack=st)
        hconv = self.sb([128, 8192], BF16, "mam_hconv", stack=st)
        dl = self.sb([128, 8192], BF16, "mam_dl", stack=st)
        BbS = [self.sb([128, 1024], BF16, f"mam_Bb{i}", stack=st) for i in range(2)]
        CbS = [self.sb([128, 1024], BF16, f"mam_Cb{i}", stack=st) for i in range(2)]
        ab = [self.sb([128, W], BF16, f"mam_ab{i}", stack=st) for i in range(2)]
        bu = [self.sb([128, W], BF16, f"mam_bu{i}", stack=st) for i in range(2)]
        hs = [self.sb([128, W], BF16, f"mam_hs{i}", stack=st) for i in range(2)]
        du = [self.sb([128, 1024], BF16, "mam_du", stack=st)] * 2
        G = [self.sb([128, 1024], BF16, f"mam_G{i}", stack=st) for i in range(2)]
        BCsb = self.sb([48, 512], BF16, "mam_BC", stack=st)
        A_sb = self.sb([128, 256], BF16, "mam_A", stack=st)
        Dcol = self.sb([128, 16], F32, "mam_D", stack=st)
        cw2 = self.sb([128, 64], F32, "mam_cw", stack=st)
        cb2 = self.sb([128, 16], F32, "mam_cb", stack=st)
        carr = self.sb([128, 256], BF16, "mam_carr", stack=st)
        halocp = self.sb([128, 48], BF16, "mam_halo", stack=st)
        ztmp = [self.sb([128, 512], BF16, "mam_zt", stack=st)] * 2
        s2f = [self.sb([128, 512], F32, f"mam_s2f{i}", stack=st) for i in range(2)]
        wbig = self.sb([128, 4096], BF16, "mam_wbig", stack=st)

        def wq(lo, hi):
            return tuple(f"mam_wb{q}" for q in range(lo // 1024, (hi + 1023) // 1024))
        hcs = tuple(f"mam_hc{ec}" for ec in range(16))
        dls = tuple(f"mam_dl{ec}" for ec in range(16))

        dma("qscal", s2f[0][:, 0:256],
            bass.AP(self.hbm["L1_attn_A_log"], 0, [[16, 128], [2048, 16], [1, 16]]),
            w=("mam_s2f0",), label=f"{tg}_ldA", slow_ok=True)
        op("scalar", lambda e: e.activation(s2f[1][:, 0:256], s2f[0][:, 0:256],
                                            AF.Exp),
           r=("mam_s2f0",), w=("mam_s2f1",), label=f"{tg}_expA")
        op("vector", lambda e: e.tensor_scalar_mul(A_sb[:], s2f[1][:, 0:256], -1.0),
           r=("mam_s2f1",), w=("mam_A",), label=f"{tg}_negA")
        dma("qscal", Dcol[:],
            bass.AP(self.hbm["L1_attn_D"], 0, [[1, 128], [128, 16]]),
            w=("mam_D",), label=f"{tg}_ldD", slow_ok=True)
        dma("qscal", cw2[:],
            bass.AP(self.hbm["L1_attn_conv_w"], 0, [[4, 128], [512, 16], [1, 4]]),
            w=("mam_cw",), label=f"{tg}_ldcw", slow_ok=True)
        dma("qscal", cb2[:],
            bass.AP(self.hbm["L1_attn_conv_b"], 0, [[1, 128], [128, 16]]),
            w=("mam_cb",), label=f"{tg}_ldcb", slow_ok=True)
        op("gpsimd", lambda e: e.memset(carr[:], 0.0), w=("mam_carr",),
           label=f"{tg}_carr0")
        op("gpsimd", lambda e: e.memset(halocp[:], 0.0), w=("mam_halo",),
           label=f"{tg}_halo0")
        for i in range(2):
            op("gpsimd", lambda e, i=i: e.memset(
                   bass.AP(ab[i], 0, [[W, 128], [65, 16], [1, 1]]), 0.0),
               w=(f"mam_ab{i}",), label=f"{tg}_abz{i}")

        for tc in range(4):
            t0 = tc * TC
            for jc in range(16):
                b = jc % 2
                dma("qgp", wbig[:, b * 1024:(b + 1) * 1024],
                    bass.AP(self.winT, jc * 8 * 16384,
                            [[128, 128], [16384, 8], [1, 128]]),
                    w=wq(b * 1024, (b + 1) * 1024), label=f"{tg}_wi{tc}_{jc}")
                bank = jc % 6
                def mmxz(e, jc=jc, b=b, bank=bank, t0=t0):
                    ins = None
                    for dc in range(8):
                        ins = e.matmul(
                            self.PS[:, bank * 512:(bank + 1) * 512],
                            wbig[:, b * 1024 + dc * 128: b * 1024 + (dc + 1) * 128],
                            self.u[:, dc * T + t0: dc * T + t0 + TC],
                            start=(dc == 0), stop=(dc == 7))
                    return ins
                op("tensor", mmxz,
                   r=wq(b * 1024, (b + 1) * 1024) + self.u_tags(),
                   w=(f"ps{bank}",), label=f"{tg}_xz{tc}_{jc}")
                op("scalar", lambda e, jc=jc, bank=bank: e.copy(
                       bass.AP(xy, jc * 515 + 3, [[8240, 128], [1, 512]]),
                       self.PS[:, bank * 512:(bank + 1) * 512]),
                   r=(f"ps{bank}",), w=("mam_xy",), label=f"{tg}_xe{tc}_{jc}")
            op("vector", lambda e: e.tensor_copy(
                   bass.AP(xy, 0, [[8240, 128], [515, 16], [1, 3]]),
                   bass.AP(halocp, 0, [[48, 128], [3, 16], [1, 3]])),
               r=("mam_halo",), w=("mam_xy",), label=f"{tg}_hr{tc}")
            for ec in range(16):
                base = ec * 515
                cb_ = ec % 2
                cv = s2f[cb_]
                ctag = f"mam_s2f{cb_}"
                op("vector", lambda e, base=base, cv=cv, ec=ec: e.tensor_scalar_mul(
                       cv[:], bass.AP(xy, base, [[8240, 128], [1, 512]]),
                       cw2[:, ec * 4:ec * 4 + 1]),
                   r=("mam_xy", "mam_cw"), w=(ctag,), label=f"{tg}_cv{tc}_{ec}_0")
                for k in range(1, 4):
                    op("vector", lambda e, base=base, cv=cv, ec=ec, k=k:
                       e.scalar_tensor_tensor(
                           cv[:], bass.AP(xy, base + k, [[8240, 128], [1, 512]]),
                           cw2[:, ec * 4 + k:ec * 4 + k + 1], cv[:],
                           ALU.mult, ALU.add),
                       r=("mam_xy", "mam_cw", ctag), w=(ctag,),
                       label=f"{tg}_cv{tc}_{ec}_{k}")
                op("scalar", lambda e, cv=cv, ec=ec: e.activation(
                       hconv[:, ec * 512:(ec + 1) * 512], cv[:], AF.Silu,
                       bias=cb2[:, ec:ec + 1]),
                   r=(ctag, "mam_cb"), w=(f"mam_hc{ec}",),
                   label=f"{tg}_si{tc}_{ec}")
            op("vector", lambda e: e.tensor_copy(
                   bass.AP(halocp, 0, [[48, 128], [3, 16], [1, 3]]),
                   bass.AP(xy, 512, [[8240, 128], [515, 16], [1, 3]])),
               r=("mam_xy",), w=("mam_halo",), label=f"{tg}_hw{tc}")
            dma("qsync", wbig[:, 2048:2560],
                bass.AP(self.wxT, 256 * 16384, [[128, 128], [16384, 16], [1, 32]]),
                w=wq(2048, 2560), label=f"{tg}_wbc{tc}")
            def mmB(e, col0=0, pcol=0):
                ins = None
                for ec in range(16):
                    ins = e.matmul(
                        self.PS[0:16, pcol:pcol + 512],
                        wbig[:, 2048 + ec * 32 + col0: 2048 + ec * 32 + col0 + 16],
                        hconv[:, ec * 512:(ec + 1) * 512],
                        start=(ec == 0), stop=(ec == 15))
                return ins
            op("tensor", lambda e: mmB(e, 0, 0), r=wq(2048, 2560) + hcs,
               w=("ps0",), label=f"{tg}_mB{tc}")
            op("tensor", lambda e: mmB(e, 16, 512), r=wq(2048, 2560) + hcs,
               w=("ps1",), label=f"{tg}_mC{tc}")
            op("vector", lambda e: e.tensor_copy(BCsb[0:16, :], self.PS[0:16, 0:512]),
               r=("ps0",), w=("mam_BC",), label=f"{tg}_eB{tc}")
            op("vector", lambda e: e.tensor_copy(BCsb[32:48, :], self.PS[0:16, 512:1024]),
               r=("ps1",), w=("mam_BC",), label=f"{tg}_eC{tc}")
            for fc in range(16):
                b = fc % 2
                dma("qsync", wbig[:, b * 2048:(b + 1) * 2048],
                    bass.AP(self.wxT, fc * 16 * 16384,
                            [[128, 128], [16384, 16], [1, 128]]),
                    w=wq(b * 2048, (b + 1) * 2048), label=f"{tg}_wx{tc}_{fc}")
                bank = 2 + fc % 4
                def mmd(e, b=b, bank=bank):
                    ins = None
                    for ec in range(16):
                        ins = e.matmul(
                            self.PS[:, bank * 512:(bank + 1) * 512],
                            wbig[:, b * 2048 + ec * 128: b * 2048 + (ec + 1) * 128],
                            hconv[:, ec * 512:(ec + 1) * 512],
                            start=(ec == 0), stop=(ec == 15))
                    return ins
                op("tensor", mmd, r=wq(b * 2048, (b + 1) * 2048) + hcs,
                   w=(f"ps{bank}",), label=f"{tg}_dbc{tc}_{fc}")
                s2 = s2f[b]
                stag = f"mam_s2f{b}"
                pb = self.PS[:, bank * 512:(bank + 1) * 512]
                op("vector", lambda e, s2=s2, pb=pb: e.tensor_scalar_mul(
                       s2[:], pb, -1.0),
                   r=(f"ps{bank}",), w=(stag,), label=f"{tg}_sp0{tc}_{fc}")
                op("vector", lambda e, s2=s2, pb=pb: e.tensor_tensor(
                       s2[:], s2[:], pb, ALU.min),
                   r=(f"ps{bank}", stag), w=(stag,), label=f"{tg}_sp1{tc}_{fc}")
                op("scalar", lambda e, s2=s2: e.activation(s2[:], s2[:], AF.Exp),
                   r=(stag,), w=(stag,), label=f"{tg}_sp2{tc}_{fc}")
                op("scalar", lambda e, s2=s2: e.activation(s2[:], s2[:], AF.Ln,
                                                           bias=one),
                   r=(stag, "constbuf"), w=(stag,), label=f"{tg}_sp3{tc}_{fc}")
                op("vector", lambda e, s2=s2, pb=pb, fc=fc: e.scalar_tensor_tensor(
                       dl[:, fc * 512:(fc + 1) * 512], pb, 0.0, s2[:],
                       ALU.max, ALU.add),
                   r=(f"ps{bank}", stag), w=(f"mam_dl{fc}",),
                   label=f"{tg}_sp4{tc}_{fc}")
            for s in range(8):
                sb2 = s % 2
                op("vector", lambda e, s=s, sb2=sb2: e.tensor_tensor(
                       du[sb2][:],
                       bass.AP(dl, s * L, [[8192, 128], [512, 16], [1, L]]),
                       bass.AP(hconv, s * L, [[8192, 128], [512, 16], [1, L]]),
                       ALU.mult),
                   r=dls + hcs, w=("mam_du",), label=f"{tg}_du{tc}_{s}")
                def mmbc(e, s=s, prow=0, pcol=1024):
                    ins = None
                    for n in range(16):
                        ins = e.matmul(
                            self.PS[:, pcol + n * 64: pcol + (n + 1) * 64],
                            sel[prow:prow + 16, n * 128:(n + 1) * 128],
                            BCsb[prow:prow + 16, s * L:(s + 1) * L],
                            start=True, stop=True)
                    return ins
                op("tensor", lambda e, s=s: mmbc(e, s, 0, 1024),
                   r=("mam_sel", "mam_BC"), w=("ps2", "ps3"),
                   label=f"{tg}_bb{tc}_{s}")
                op("scalar", lambda e, sb2=sb2: e.copy(BbS[sb2][:],
                                                       self.PS[:, 1024:2048]),
                   r=("ps2", "ps3"), w=(f"mam_Bb{sb2}",), label=f"{tg}_be{tc}_{s}")
                op("tensor", lambda e, s=s: mmbc(e, s, 32, 2048),
                   r=("mam_sel", "mam_BC"), w=("ps4", "ps5"),
                   label=f"{tg}_cb{tc}_{s}")
                op("scalar", lambda e, sb2=sb2: e.copy(CbS[sb2][:],
                                                       self.PS[:, 2048:3072]),
                   r=("ps4", "ps5"), w=(f"mam_Cb{sb2}",), label=f"{tg}_ce{tc}_{s}")
                for ec in range(16):
                    ui = (s * 16 + ec) % 2
                    abt, but, hst, Gt = ab[ui], bu[ui], hs[ui], G[ui]
                    ta, tb, th, tG = (f"mam_ab{ui}", f"mam_bu{ui}",
                                      f"mam_hs{ui}", f"mam_G{ui}")
                    op("gpsimd", lambda e, abt=abt, ec=ec, s=s: e.tensor_tensor(
                           bass.AP(abt, 1, [[W, 128], [65, 16], [1, L]]),
                           bass.AP(dl, ec * 512 + s * L,
                                   [[8192, 128], [0, 16], [1, L]]),
                           bass.AP(A_sb, ec * 16, [[256, 128], [1, 16], [0, L]]),
                           ALU.mult),
                       r=(f"mam_dl{ec}", "mam_A"), w=(ta,),
                       label=f"{tg}_ap{tc}_{s}_{ec}")
                    op("scalar", lambda e, abt=abt: e.activation(
                           bass.AP(abt, 1, [[W, 128], [65, 16], [1, L]]),
                           bass.AP(abt, 1, [[W, 128], [65, 16], [1, L]]),
                           AF.Exp),
                       r=(ta,), w=(ta,), label=f"{tg}_ex{tc}_{s}_{ec}")
                    op("vector", lambda e, but=but, ec=ec, sb2=sb2: e.tensor_tensor(
                           bass.AP(but, 1, [[W, 128], [65, 16], [1, L]]),
                           bass.AP(du[sb2], ec * L, [[1024, 128], [0, 16], [1, L]]),
                           bass.AP(BbS[sb2], 0, [[1024, 128], [64, 16], [1, L]]),
                           ALU.mult),
                       r=("mam_du", f"mam_Bb{sb2}"), w=(tb,),
                       label=f"{tg}_bu{tc}_{s}_{ec}")
                    op("gpsimd", lambda e, but=but, ec=ec: e.tensor_copy(
                           bass.AP(but, 0, [[W, 128], [65, 16], [1, 1]]),
                           bass.AP(carr, ec * 16, [[256, 128], [1, 16], [1, 1]])),
                       r=("mam_carr",), w=(tb,), label=f"{tg}_ci{tc}_{s}_{ec}")
                    op("vector", lambda e, hst=hst, abt=abt, but=but:
                       e.tensor_tensor_scan(hst[:], abt[:], but[:], 0.0,
                                            ALU.mult, ALU.add),
                       r=(ta, tb), w=(th,), label=f"{tg}_sc{tc}_{s}_{ec}")
                    op("gpsimd", lambda e, hst=hst, ec=ec: e.tensor_copy(
                           bass.AP(carr, ec * 16, [[256, 128], [1, 16], [1, 1]]),
                           bass.AP(hst, 64, [[W, 128], [65, 16], [1, 1]])),
                       r=(th,), w=("mam_carr",), label=f"{tg}_co{tc}_{s}_{ec}")
                    op("vector", lambda e, Gt=Gt, hst=hst, sb2=sb2: e.tensor_tensor(
                           Gt[:],
                           bass.AP(hst, 1, [[W, 128], [65, 16], [1, L]]),
                           bass.AP(CbS[sb2], 0, [[1024, 128], [64, 16], [1, L]]),
                           ALU.mult),
                       r=(th, f"mam_Cb{sb2}"), w=(tG,),
                       label=f"{tg}_G{tc}_{s}_{ec}")
                    op("gpsimd", lambda e, Gt=Gt: e.tensor_add(
                           Gt[:, 0:512], Gt[:, 0:512], Gt[:, 512:1024]),
                       r=(tG,), w=(tG,), label=f"{tg}_f1{tc}_{s}_{ec}")
                    op("gpsimd", lambda e, Gt=Gt: e.tensor_add(
                           Gt[:, 0:256], Gt[:, 0:256], Gt[:, 256:512]),
                       r=(tG,), w=(tG,), label=f"{tg}_f2{tc}_{s}_{ec}")
                    op("vector", lambda e, Gt=Gt: e.tensor_add(
                           Gt[:, 0:128], Gt[:, 0:128], Gt[:, 128:256]),
                       r=(tG,), w=(tG,), label=f"{tg}_f3{tc}_{s}_{ec}")
                    op("vector", lambda e, Gt=Gt, ec=ec, s=s: e.tensor_add(
                           bass.AP(xy, ec * 512 + s * L, [[8240, 128], [1, L]]),
                           Gt[:, 0:64], Gt[:, 64:128]),
                       r=(tG,), w=("mam_xy",), label=f"{tg}_f4{tc}_{s}_{ec}")
            for ec in range(16):
                op("vector", lambda e, ec=ec: e.scalar_tensor_tensor(
                       bass.AP(xy, ec * 512, [[8240, 128], [1, 512]]),
                       hconv[:, ec * 512:(ec + 1) * 512],
                       Dcol[:, ec:ec + 1],
                       bass.AP(xy, ec * 512, [[8240, 128], [1, 512]]),
                       ALU.mult, ALU.add),
                   r=(f"mam_hc{ec}", "mam_D", "mam_xy"), w=("mam_xy",),
                   label=f"{tg}_dk{tc}_{ec}")
            for jc2 in range(16):
                b = jc2 % 2
                dma("qgp", wbig[:, b * 1024:(b + 1) * 1024],
                    bass.AP(self.winT, (16 + jc2) * 8 * 16384,
                            [[128, 128], [16384, 8], [1, 128]]),
                    w=wq(b * 1024, (b + 1) * 1024), label=f"{tg}_wz{tc}_{jc2}")
                bank = jc2 % 4
                def mmz(e, jc2=jc2, b=b, bank=bank, t0=t0):
                    ins = None
                    for dc in range(8):
                        ins = e.matmul(
                            self.PS[:, bank * 512:(bank + 1) * 512],
                            wbig[:, b * 1024 + dc * 128: b * 1024 + (dc + 1) * 128],
                            self.u[:, dc * T + t0: dc * T + t0 + TC],
                            start=(dc == 0), stop=(dc == 7))
                    return ins
                op("tensor", mmz,
                   r=wq(b * 1024, (b + 1) * 1024) + self.u_tags(),
                   w=(f"ps{bank}",), label=f"{tg}_z{tc}_{jc2}")
                op("scalar", lambda e, b=b, bank=bank: e.activation(
                       ztmp[b][:], self.PS[:, bank * 512:(bank + 1) * 512],
                       AF.Silu),
                   r=(f"ps{bank}",), w=("mam_zt",), label=f"{tg}_ze{tc}_{jc2}")
                op("vector", lambda e, b=b, jc2=jc2: e.tensor_mul(
                       bass.AP(xy, jc2 * 512, [[8240, 128], [1, 512]]),
                       bass.AP(xy, jc2 * 512, [[8240, 128], [1, 512]]),
                       ztmp[b][:]),
                   r=("mam_zt", "mam_xy"), w=("mam_xy",),
                   label=f"{tg}_gt{tc}_{jc2}")
            for oc in range(8):
                b = oc % 2
                dma("qscal", wbig[:, b * 2048:(b + 1) * 2048],
                    bass.AP(self.woT, oc * 16 * 16384,
                            [[128, 128], [16384, 16], [1, 128]]),
                    w=wq(b * 2048, (b + 1) * 2048), label=f"{tg}_wo{tc}_{oc}")
                def mmo(e, b=b, oc=oc):
                    ins = None
                    for ec in range(16):
                        ins = e.matmul(
                            self.PS[:, oc * 512:(oc + 1) * 512],
                            wbig[:, b * 2048 + ec * 128: b * 2048 + (ec + 1) * 128],
                            bass.AP(xy, ec * 512, [[8240, 128], [1, 512]]),
                            start=(ec == 0), stop=(ec == 15))
                    return ins
                op("tensor", mmo,
                   r=wq(b * 2048, (b + 1) * 2048) + ("mam_xy",),
                   w=(f"ps{oc}",), label=f"{tg}_wm{tc}_{oc}")
                xs_t = self.x_tags(oc, range(tc * 4, tc * 4 + 4))
                op("vector", lambda e, oc=oc, t0=t0: e.tensor_add(
                       self.x[:, oc * T + t0: oc * T + t0 + TC],
                       self.x[:, oc * T + t0: oc * T + t0 + TC],
                       self.PS[:, oc * 512:(oc + 1) * 512]),
                   r=(f"ps{oc}",) + xs_t, w=xs_t, label=f"{tg}_xr{tc}_{oc}")
        st.close()
        self.sched.barrier()

    def store_out(self):
        stio = ExitStack()
        stage = [self.sb([128, D], F32, f"ostage{i}", stack=stio) for i in range(2)]
        for tt in range(16):
            s = stage[tt % 2]
            stag = f"iostage{tt % 2}"
            for dc in range(8):
                pb = self.pool[dc % 4]
                ptag = f"ps{dc % 4}"
                self.op("tensor",
                        lambda e, dc=dc, tt=tt, pb=pb: e.transpose(
                            pb[:, 0:128],
                            self.x[:, dc * T + tt * 128: dc * T + (tt + 1) * 128],
                            self.ident_f[:]),
                        r=(f"x:{dc}:{tt}", "ident_f"), w=(ptag,),
                        label=f"oT{tt}_{dc}")
                self.op("vector",
                        lambda e, dc=dc, pb=pb, s=s: e.tensor_copy(
                            s[:, dc * 128:(dc + 1) * 128], pb[:, 0:128]),
                        r=(ptag,), w=(stag,), label=f"oEv{tt}_{dc}")
            self.dma("qsync",
                     bass.AP(self.out_hbm, tt * 128 * D, [[D, 128], [1, D]]),
                     s[:],
                     r=(stag,), w=(f"osent{tt}",), label=f"ostore{tt}")
        stio.close()



INPUT_META = [
    ("L0_attn_x", (T, D), F32),
    ("L0_attn_W_kv_down", (1024, 128), F32),
    ("L0_attn_W_k_up", (128, 1024), F32),
    ("L0_attn_W_v_up", (128, 1024), F32),
    ("L0_attn_W_q", (1024, 1024), F32),
    ("L0_attn_W_q_rope", (1024, 32), F32),
    ("L0_attn_W_k_rope", (128, 32), F32),
    ("L0_attn_W_o", (1024, 1024), F32),
    ("L0_ffn_w1", (4096, 1024), F32),
    ("L0_ffn_conv_w", (4096, 1, 3), F32),
    ("L0_ffn_conv_b", (4096,), F32),
    ("L0_ffn_w2", (1024, 4096), F32),
    ("L1_attn_W_in", (4096, 1024), F32),
    ("L1_attn_conv_w", (2048, 1, 4), F32),
    ("L1_attn_conv_b", (2048,), F32),
    ("L1_attn_W_x", (2080, 2048), F32),
    ("L1_attn_A_log", (2048, 16), F32),
    ("L1_attn_D", (2048,), F32),
    ("L1_attn_W_out", (1024, 2048), F32),
    ("L1_ffn_rail_maker", (4, 64, 16, 16), F32),
    ("L1_ffn_rail_up_proj", (4, 64, 64, 16), F32),
    ("L1_ffn_w_cproj", (1024, 16384), F32),
]


def build_nc(n_sub=6):
    nc = bass.Bass("TRN2", target_bir_lowering=False)
    es = ExitStack()
    with es:
        b = Builder(nc, es, n_sub=n_sub)
        b.build(INPUT_META)
    return nc


def kernel(**inputs):
    n_sub = int(inputs.pop("_n_sub", 6))
    nc = build_nc(n_sub=n_sub)
    x = np.ascontiguousarray(inputs["L0_attn_x"])
    in_maps = []
    for c in range(NCORE):
        m = {k: np.ascontiguousarray(v) for k, v in inputs.items()}
        m["L0_attn_x"] = np.ascontiguousarray(x[c])
        in_maps.append(m)
    res = run_bass_kernel_spmd(nc, in_maps, core_ids=list(range(NCORE)))
    out = np.stack([r["out"] for r in res.results], axis=0)
    return out


# version 02
# speedup vs baseline: 1.1158x; 1.1158x over previous
import math
from contextlib import ExitStack

import numpy as np
import concourse.bass as bass
import concourse.mybir as mybir
from concourse.bass_utils import run_bass_kernel_spmd

F32 = mybir.dt.float32
BF16 = mybir.dt.bfloat16
I32 = mybir.dt.int32
AF = mybir.ActivationFunctionType
ALU = mybir.AluOpType
AX = mybir.AxisListType

T = 2048
D = 1024
NCORE = 8


COMPUTE_ENGINES = ("tensor", "vector", "scalar", "gpsimd", "sync")
DMA_QUEUES = ("qsync", "qgp", "qscal")
QUEUE_ENGINE = {"qsync": "sync", "qgp": "gpsimd", "qscal": "scalar"}
ALL_STREAMS = COMPUTE_ENGINES + DMA_QUEUES


class Op:
    __slots__ = ("stream", "fn", "reads", "writes", "idx", "label",
                 "sem", "inc_to", "waits", "pre_wait")

    def __init__(self, stream, fn, reads, writes, idx, label=""):
        self.stream = stream
        self.fn = fn
        self.reads = reads
        self.writes = writes
        self.idx = idx
        self.label = label
        self.sem = None
        self.inc_to = 0
        self.waits = {}
        self.pre_wait = None


N_QSEM = 8


class Sched:

    def __init__(self, nc):
        self.nc = nc
        self.ops = []
        self.last_writer = {}
        self.readers = {}
        self.eng_seq = {s: 0 for s in COMPUTE_ENGINES}
        self.q_idx = {q: 0 for q in DMA_QUEUES}
        self.sem_uses = {}
        self.bar = {}

    def barrier(self):
        snap = {}
        for op in self.ops:
            snap[op.sem] = max(snap.get(op.sem, 0), op.inc_to)
        self.bar = snap

    def add(self, stream, fn, r=(), w=(), label=""):
        op = Op(stream, fn, tuple(r), tuple(w), len(self.ops), label)
        if stream in DMA_QUEUES:
            k = self.q_idx[stream] % N_QSEM
            self.q_idx[stream] += 1
            sem = f"{stream}_{k}"
            uses = self.sem_uses.get(sem, 0)
            if uses > 0:
                op.pre_wait = (sem, uses * 16)
            self.sem_uses[sem] = uses + 1
            op.sem = sem
            op.inc_to = (uses + 1) * 16
        else:
            self.eng_seq[stream] += 1
            op.sem = f"e_{stream}"
            op.inc_to = self.eng_seq[stream]
        deps = []
        for t in op.reads:
            lw = self.last_writer.get(t)
            if lw is not None:
                deps.append(lw)
        for t in op.writes:
            lw = self.last_writer.get(t)
            if lw is not None:
                deps.append(lw)
            deps.extend(self.readers.get(t, ()))
        for d in deps:
            if d is op:
                continue
            cur = op.waits.get(d.sem, 0)
            op.waits[d.sem] = max(cur, d.inc_to)
        for sname, v in self.bar.items():
            if op.waits.get(sname, 0) < v:
                op.waits[sname] = v
        for t in op.reads:
            self.readers.setdefault(t, []).append(op)
        for t in op.writes:
            self.last_writer[t] = op
            self.readers[t] = []
        self.ops.append(op)
        return op

    def finalize_and_emit(self, es):
        nc = self.nc
        sem_names = set()
        for op in self.ops:
            sem_names.add(op.sem)
            sem_names.update(op.waits.keys())
            if op.pre_wait:
                sem_names.add(op.pre_wait[0])
        sems = {n: es.enter_context(nc.semaphore(n)) for n in sorted(sem_names)}
        per_stream = {s: [] for s in ALL_STREAMS}
        for op in self.ops:
            per_stream[op.stream].append(op)

        hw_map = {}
        for stream in ALL_STREAMS:
            if per_stream[stream]:
                hw = QUEUE_ENGINE.get(stream, stream)
                hw_map.setdefault(hw, []).append(stream)

        def make_runner(streams):
            ops = sorted([o for s in streams for o in per_stream[s]],
                         key=lambda o: o.idx)

            def runner(eng):
                waited = {}
                for op in ops:
                    for sname, v in sorted(op.waits.items()):
                        if waited.get(sname, 0) < v:
                            eng.wait_ge(sems[sname], v)
                            waited[sname] = v
                    if op.pre_wait is not None:
                        sname, v = op.pre_wait
                        if waited.get(sname, 0) < v:
                            eng.wait_ge(sems[sname], v)
                            waited[sname] = v
                    ins = op.fn(eng)
                    assert ins is not None,                         f"op {op.label} on {op.stream} returned no instruction"
                    ins.then_inc(sems[op.sem],
                                 16 if op.stream in DMA_QUEUES else 1)
            return runner

        with nc.Block() as blk:
            for hw, streams in hw_map.items():
                getattr(blk, hw)(make_runner(streams))

        print(f"[sched] ops={len(self.ops)} sems={len(sem_names)} "
              f"per-stream={ {k: len(v) for k, v in per_stream.items() if v} }")



class Builder:
    def __init__(self, nc, es, n_sub=6):
        self.nc = nc
        self.es = es
        self.sched = Sched(nc)
        self.n_sub = n_sub
        self.name_ctr = 0

    def sb(self, shape, dtype, name=None, stack=None):
        self.name_ctr += 1
        nm = f"{name}_{self.name_ctr}" if name else f"t{self.name_ctr}"
        import concourse.mybir as _mb
        nbytes = int(np.prod(shape[1:])) * _mb.dt.size(dtype) if len(shape) > 1 else _mb.dt.size(dtype)
        self.alloc_log = getattr(self, "alloc_log", [])
        self.alloc_log.append((nm, nbytes))
        return (stack or self.es).enter_context(
            self.nc.sbuf_tensor(nm, list(shape), dtype))

    def op(self, stream, fn, r=(), w=(), label=""):
        return self.sched.add(stream, fn, r, w, label)

    def dma(self, queue, out_ap, in_ap, r=(), w=(), label="dma", transpose=False,
            slow_ok=False):
        if transpose:
            fn = lambda e: e.dma_start_transpose(out_ap, in_ap)
        else:
            fn = lambda e: e.dma_start(out=out_ap, in_=in_ap,
                                       allow_slow_non_contiguous=slow_ok)
        return self.sched.add(queue, fn, r, w, label)

    def build(self, inputs_meta):
        nc = self.nc
        self.hbm = {}
        for name, shape, dt in inputs_meta:
            self.hbm[name] = nc.dram_tensor(name, list(shape), dt,
                                            kind="ExternalInput")
        self.out_hbm = nc.dram_tensor("out", [T, D], F32,
                                      kind="ExternalOutput")

        self.x = self.sb([128, 8 * T], F32, "x_res")
        self.u = self.sb([128, 8 * T], BF16, "u_ln")
        self.ones_col_b = self.sb([128, 1], BF16, "ones_col_b")
        self.ones_row_f = self.sb([1, 128], F32, "ones_row_f")
        self.ident_b = self.sb([128, 128], BF16, "ident_b")
        self.ident_f = self.sb([128, 128], F32, "ident_f")
        self.halfkk = self.sb([128, 1], F32, "halfkk")
        self.tiny = self.sb([128, 8], F32, "tinybuf")
        self.rows = self.sb([97, 2 * T], F32, "rows")
        self.PS = self.es.enter_context(
            self.nc.psum_tensor("PS", [128, 4096], F32))
        self.pool = [self.PS[:, 512 * b:512 * (b + 1)] for b in range(8)]
        self.w1qT = self.nc.dram_tensor("w1qT_scr", [32 * 128, 1024], BF16)
        self.wcT = self.nc.dram_tensor("wcT_scr", [128 * 128, 1024], BF16)
        self.CsT_hbm = self.nc.dram_tensor("CsT_scr", [128, 32 * 256], BF16)
        self.w2qT = self.nc.dram_tensor("w2qT_scr", [32 * 128, 1024], BF16)
        self.h_scr = self.nc.dram_tensor("h_scr", [4096, T], BF16)
        self.winT = self.nc.dram_tensor("winT_scr", [32 * 128, 1024], BF16)
        self.wxT = self.nc.dram_tensor("wxT_scr", [17 * 128, 2048], BF16)
        self.woT = self.nc.dram_tensor("woT_scr", [8 * 128, 2048], BF16)

        self.constbuf = self.sb([128, 16], F32, "constbuf")
        self.n_const = 0
        self.setup_consts()
        self.load_x()
        if self.n_sub >= 1:
            self.prep_l0_ffn_weights()
        if self.n_sub >= 4:
            self.prep_l1_ffn_weights()
        if self.n_sub >= 5:
            self.prep_l1_attn_weights()

        steps = [
            lambda: (self.layernorm("ln0"), self.l0_ffn(0)),
            lambda: (self.layernorm("ln1"), self.l0_attn()),
            lambda: (self.layernorm("ln2"), self.l0_ffn(1)),
            lambda: (self.layernorm("ln3"), self.l1_ffn(0)),
            lambda: (self.layernorm("ln4"), self.l1_attn()),
            lambda: (self.layernorm("ln5"), self.l1_ffn(1)),
        ]
        for i in range(min(self.n_sub, 6)):
            steps[i]()

        self.store_out()
        fin_tags = tuple(f"osent{tt}" for tt in range(16))
        self.op("sync", lambda e: e.nop(), r=fin_tags, label="final")
        self.sched.finalize_and_emit(self.es)

    def setup_consts(self):
        op = self.op
        op("vector", lambda e: e.memset(self.ones_col_b[:], 1.0),
           w=("ones_col_b",), label="ones_col")
        op("vector", lambda e: e.memset(self.ones_row_f[:], 1.0),
           w=("ones_row_f",), label="ones_row")
        stC = ExitStack()
        it_col = self.sb([128, 128], F32, "it_col", stack=stC)
        it_row = self.sb([128, 128], F32, "it_row", stack=stC)
        op("gpsimd", lambda e: e.iota(it_col[:], [[1, 128]], channel_multiplier=0, allow_small_or_imprecise_dtypes=True),
           w=("it_col",), label="iota_col")
        op("gpsimd", lambda e: e.iota(it_row[:], [[0, 128]], channel_multiplier=1, allow_small_or_imprecise_dtypes=True),
           w=("it_row",), label="iota_row")
        op("vector", lambda e: e.tensor_tensor(self.ident_f[:], it_col[:],
                                               it_row[:], ALU.is_equal),
           r=("it_col", "it_row"), w=("ident_f",), label="ident_f")
        op("vector", lambda e: e.tensor_copy(self.ident_b[:], self.ident_f[:]),
           r=("ident_f",), w=("ident_b",), label="ident_b")
        can_i = self.sb([128, 1], I32, "can_i", stack=stC)
        op("vector", lambda e: e.memset(self.tiny[:, 0:1], 0.75), w=("tiny0",))
        op("vector", lambda e: e.tensor_copy(can_i[:], self.tiny[:, 0:1]),
           r=("tiny0",), w=("can_i",), label="canary_cvt")
        op("vector", lambda e: e.tensor_scalar(self.halfkk[:], can_i[:], -0.5, 0.5,
                                               ALU.mult, ALU.add),
           r=("can_i",), w=("halfkk",), label="halfkk")
        self.sched.barrier()
        stC.close()

    def load_x(self):
        xin = self.hbm["L0_attn_x"]
        stio = ExitStack()
        stage = [self.sb([128, D], F32, f"iostage{i}", stack=stio) for i in range(2)]
        for tt in range(16):
            s = stage[tt % 2]
            stag = f"iostage{tt % 2}"
            self.dma("qsync", s[:],
                     bass.AP(xin, tt * 128 * D, [[D, 128], [1, D]]),
                     r=(), w=(stag,), label=f"xload{tt}")
            for dc in range(8):
                pb = self.pool[4 + dc % 4]
                ptag = f"ps{4 + dc % 4}"
                self.op("tensor",
                        lambda e, s=s, dc=dc, pb=pb: e.transpose(
                            pb[:, 0:128], s[:, dc * 128:(dc + 1) * 128],
                            self.ident_f[:]),
                        r=(stag, "ident_f"), w=(ptag,), label=f"xT{tt}_{dc}")
                self.op("scalar",
                        lambda e, pb=pb, dc=dc, tt=tt: e.copy(
                            self.x[:, dc * T + tt * 128: dc * T + (tt + 1) * 128],
                            pb[:, 0:128]),
                        r=(ptag,), w=(f"x:{dc}:{tt}",), label=f"xEv{tt}_{dc}")
        stio.close()
        self.sched.barrier()

    def x_tags(self, dc=None, tts=None):
        dcs = range(8) if dc is None else [dc]
        tl = range(16) if tts is None else tts
        return tuple(f"x:{d}:{t}" for d in dcs for t in tl)

    def u_tags(self, dc=None):
        dcs = range(8) if dc is None else [dc]
        return tuple(f"u:{d}" for d in dcs)

    def constf(self, val):
        key = float(val)
        self.cvals = getattr(self, "cvals", {})
        if key in self.cvals:
            return key
        idx = self.n_const
        self.n_const += 1
        assert idx < 16
        ap = self.constbuf[:, idx:idx + 1]
        self.op("vector", lambda e, ap=ap, key=key: e.memset(ap, key),
                w=("constbuf",), label=f"const_{idx}")
        self.nc.const_aps.register(F32, key, ap)
        self.cvals[key] = True
        return key

    def row(self, p, seg=0, lo=0, hi=T):
        return self.rows[p:p + 1, seg * T + lo: seg * T + hi]

    def layernorm(self, tag):
        op = self.op
        stL = ExitStack()
        self.mrs_sb = self.sb([128, T], F32, "mrs_sb", stack=stL)
        self.ln_tmp = self.sb([128, T], F32, "ln_tmp", stack=stL)
        op("vector", lambda e: e.tensor_copy(self.u[:], self.x[:]),
           r=self.x_tags(), w=self.u_tags(), label=f"{tag}_shadow")
        for j in range(4):
            def mmsx(e, j=j):
                ins = None
                for dc in range(8):
                    ins = e.matmul(
                        self.PS[0:1, j * 512:(j + 1) * 512],
                        self.ones_col_b[:],
                        self.u[:, dc * T + j * 512: dc * T + (j + 1) * 512],
                        start=(dc == 0), stop=(dc == 7))
                return ins
            op("tensor", mmsx, r=self.u_tags() + ("ones_col_b",),
               w=(f"ps{j}",), label=f"{tag}_sx{j}")
        op("vector", lambda e: e.tensor_copy(self.row(32, 0), self.PS[0:1, 0:2048]),
           r=("ps0", "ps1", "ps2", "ps3"), w=("r32_0",), label=f"{tag}_evsx")
        op("scalar", lambda e: e.activation(self.u[:], self.u[:], AF.Square),
           r=self.u_tags(), w=self.u_tags(), label=f"{tag}_sq")
        for j in range(4):
            def mmsq(e, j=j):
                ins = None
                for dc in range(8):
                    ins = e.matmul(
                        self.PS[0:1, 2048 + j * 512: 2048 + (j + 1) * 512],
                        self.ones_col_b[:],
                        self.u[:, dc * T + j * 512: dc * T + (j + 1) * 512],
                        start=(dc == 0), stop=(dc == 7))
                return ins
            op("tensor", mmsq, r=self.u_tags() + ("ones_col_b",),
               w=(f"ps{4 + j}",), label=f"{tag}_sq{j}")
        op("vector", lambda e: e.tensor_copy(self.row(32, 1), self.PS[0:1, 2048:4096]),
           r=("ps4", "ps5", "ps6", "ps7"), w=("r32_1",), label=f"{tag}_evsq")
        op("vector", lambda e: e.tensor_scalar_mul(self.row(64, 0), self.row(32, 0), 1.0 / D),
           r=("r32_0",), w=("r64_0",), label=f"{tag}_m")
        op("vector", lambda e: e.tensor_scalar_mul(self.row(32, 1), self.row(32, 1), 1.0 / D),
           r=("r32_1",), w=("r32_1",), label=f"{tag}_q")
        op("vector", lambda e: e.tensor_mul(self.row(32, 0), self.row(64, 0), self.row(64, 0)),
           r=("r64_0",), w=("r32_0",), label=f"{tag}_mm")
        op("vector", lambda e: e.tensor_sub(self.row(32, 1), self.row(32, 1), self.row(32, 0)),
           r=("r32_1", "r32_0"), w=("r32_1",), label=f"{tag}_var")
        op("vector", lambda e: e.tensor_scalar_add(self.row(32, 1), self.row(32, 1), 1e-5),
           r=("r32_1",), w=("r32_1",), label=f"{tag}_eps")
        op("vector", lambda e: e.reciprocal(self.row(32, 0), self.row(32, 1)),
           r=("r32_1",), w=("r32_0",), label=f"{tag}_recip")
        op("scalar", lambda e: e.activation(self.row(0, 0), self.row(32, 0), AF.Sqrt),
           r=("r32_0",), w=("r0_0",), label=f"{tag}_sqrt")
        op("vector", lambda e: e.tensor_mul(self.row(32, 0), self.row(0, 0), self.row(0, 0)),
           r=("r0_0",), w=("r32_0",), label=f"{tag}_n1")
        op("vector", lambda e: e.tensor_mul(self.row(32, 0), self.row(32, 0), self.row(32, 1)),
           r=("r32_0", "r32_1"), w=("r32_0",), label=f"{tag}_n2")
        op("vector", lambda e: e.tensor_scalar(self.row(0, 1), self.row(32, 0), -0.5, 1.5,
                                               ALU.mult, ALU.add),
           r=("r32_0",), w=("r0_1",), label=f"{tag}_n3")
        op("vector", lambda e: e.tensor_mul(self.row(0, 0), self.row(0, 0), self.row(0, 1)),
           r=("r0_0", "r0_1"), w=("r0_0",), label=f"{tag}_n4")
        op("vector", lambda e: e.tensor_copy(self.row(0, 1), self.row(64, 0)),
           r=("r64_0",), w=("r0_1",), label=f"{tag}_mcp")
        op("vector", lambda e: e.tensor_mul(self.row(0, 1), self.row(0, 0), self.row(0, 1)),
           r=("r0_0", "r0_1"), w=("r0_1",), label=f"{tag}_mrs")
        for j in range(4):
            bk = 1 + j
            op("tensor", lambda e, j=j, bk=bk: e.matmul(
                   self.PS[:, bk * 512:(bk + 1) * 512], self.ones_row_f[:],
                   self.row(0, 0, j * 512, (j + 1) * 512), start=True, stop=True),
               r=("r0_0", "ones_row_f"), w=(f"ps{bk}",), label=f"{tag}_bcr{j}")
        for j in range(4):
            bk = (5 + j) % 8
            op("tensor", lambda e, j=j, bk=bk: e.matmul(
                   self.PS[:, bk * 512:(bk + 1) * 512], self.ones_row_f[:],
                   self.row(0, 1, j * 512, (j + 1) * 512), start=True, stop=True),
               r=("r0_1", "ones_row_f"), w=(f"ps{bk}",), label=f"{tag}_bcm{j}")
            op("scalar", lambda e, j=j, bk=bk: e.copy(
                   self.mrs_sb[:, j * 512:(j + 1) * 512],
                   self.PS[:, bk * 512:(bk + 1) * 512]),
               r=(f"ps{bk}",), w=("mrs_sb",), label=f"{tag}_evm{j}")
        for dc in range(8):
            for j in range(4):
                sl = slice(dc * T + j * 512, dc * T + (j + 1) * 512)
                tsl = slice(j * 512, (j + 1) * 512)
                ttag = f"ln_tmp{j}"
                bk = 1 + j
                op("vector",
                   lambda e, sl=sl, tsl=tsl, bk=bk: e.tensor_mul(
                       self.ln_tmp[:, tsl], self.x[:, sl],
                       self.PS[:, bk * 512:(bk + 1) * 512]),
                   r=self.x_tags(dc) + (f"ps{bk}",), w=(ttag,),
                   label=f"{tag}_ap1_{dc}_{j}")
                op("vector",
                   lambda e, sl=sl, tsl=tsl: e.tensor_sub(
                       self.u[:, sl], self.ln_tmp[:, tsl], self.mrs_sb[:, tsl]),
                   r=(ttag, "mrs_sb"), w=self.u_tags(dc),
                   label=f"{tag}_ap2_{dc}_{j}")
        stL.close()
        self.sched.barrier()

    def prep_l0_ffn_weights(self):
        op, dma = self.op, self.dma
        self.cwcols = self.sb([128, 96], F32, "cwcols")
        self.cbcols = self.sb([128, 32], F32, "cbcols")
        st = ExitStack()
        wacc = self.sb([128, 2], F32, "wp_acc", stack=st)
        wred = self.sb([128, 1], F32, "wp_red", stack=st)
        swcol = self.sb([128, 2], F32, "wp_swcol", stack=st)
        wt = [self.sb([128, 4096], F32, f"wp_t{i}", stack=st) for i in range(2)]
        tern = [self.sb([128, 4096], BF16, f"wp_n{i}", stack=st) for i in range(2)]
        stg = [self.sb([128, 1024], BF16, f"wp_s{i}", stack=st) for i in range(2)]

        for wi, (hbm_name, nrow, ncol, scr, tag) in enumerate([
                ("L0_ffn_w1", 4096, 1024, self.w1qT, "w1"),
                ("L0_ffn_w2", 1024, 4096, self.w2qT, "w2")]):
            wh = self.hbm[hbm_name]
            grp = 4096 // ncol
            nload = (nrow // 128) // grp
            def ld_ap(li, grp=grp, ncol=ncol):
                return bass.AP(wh, li * grp * 128 * ncol,
                               [[ncol, 128], [128 * ncol, grp], [1, ncol]])
            op("vector", lambda e: e.memset(wacc[:, 0:1], 0.0),
               w=("wpacc",), label=f"{tag}_acc0")
            for li in range(nload):
                b = li % 2
                dma("qgp", wt[b][:], ld_ap(li), w=(f"wpt{b}",), label=f"{tag}_ld{li}")
                op("vector", lambda e, b=b: e.reduce_sum(
                       wred[:], wt[b][:], axis=AX.X, apply_absolute_value=True),
                   r=(f"wpt{b}",), w=("wpred",), label=f"{tag}_red{li}")
                op("vector", lambda e: e.tensor_add(wacc[:, 0:1], wacc[:, 0:1], wred[:]),
                   r=("wpred", "wpacc"), w=("wpacc",), label=f"{tag}_acc{li}")
            op("tensor", lambda e: e.transpose(self.PS[0:1, 0:128], wacc[:, 0:1],
                                               self.ident_f[:]),
               r=("wpacc", "ident_f"), w=("ps0",), label=f"{tag}_T")
            op("vector", lambda e: e.tensor_copy(self.row(0, 0, 0, 128), self.PS[0:1, 0:128]),
               r=("ps0",), w=("r0_0",), label=f"{tag}_evT")
            op("vector", lambda e: e.reduce_sum(self.row(0, 0, 0, 1), self.row(0, 0, 0, 128),
                                                axis=AX.X),
               r=("r0_0",), w=("r0_0",), label=f"{tag}_sum")
            N = float(nrow * ncol)
            op("vector", lambda e, N=N: e.tensor_scalar(
                   self.row(0, 0, 1, 2), self.row(0, 0, 0, 1), 1.0 / N, 1e-5,
                   ALU.mult, ALU.max),
               r=("r0_0",), w=("r0_0",), label=f"{tag}_c")
            op("vector", lambda e: e.reciprocal(self.row(0, 1, 0, 1), self.row(0, 0, 1, 2)),
               r=("r0_0",), w=("r0_1",), label=f"{tag}_s")
            op("vector", lambda e, wi=wi: e.tensor_copy(self.tiny[0:1, 1 + wi:2 + wi],
                                                        self.row(0, 0, 1, 2)),
               r=("r0_0",), w=(f"cw{wi}",), label=f"{tag}_csave")
            op("tensor", lambda e: e.matmul(self.PS[:, 512:513], self.ones_row_f[:],
                                            self.row(0, 1, 0, 1), start=True, stop=True),
               r=("r0_1", "ones_row_f"), w=("ps1",), label=f"{tag}_bcs")
            op("scalar", lambda e, wi=wi: e.copy(swcol[:, wi:wi + 1], self.PS[:, 512:513]),
               r=("ps1",), w=(f"swcol{wi}",), label=f"{tag}_evs")
            for li in range(nload):
                b = li % 2
                dma("qgp", wt[b][:], ld_ap(li), w=(f"wpt{b}",), label=f"{tag}_ld2_{li}")
                op("vector", lambda e, b=b, wi=wi: e.tensor_scalar_mul(
                       wt[b][:], wt[b][:], swcol[:, wi:wi + 1]),
                   r=(f"wpt{b}", f"swcol{wi}"), w=(f"wpt{b}",), label=f"{tag}_v{li}")
                op("vector", lambda e, b=b: e.tensor_single_scalar(
                       tern[b][:], wt[b][:], -0.5, ALU.is_le),
                   r=(f"wpt{b}",), w=(f"wptern{b}",), label=f"{tag}_c{li}")
                op("vector", lambda e, b=b: e.scalar_tensor_tensor(
                       tern[b][:], wt[b][:], 0.5, tern[b][:],
                       ALU.is_ge, ALU.subtract),
                   r=(f"wpt{b}", f"wptern{b}"), w=(f"wptern{b}",), label=f"{tag}_t{li}")
                for g0 in range(0, 32, 8):
                    bank = 2 + ((g0 // 8) % 2)
                    psb = self.PS[:, bank * 512:(bank + 1) * 512].bitcast(BF16)
                    for k in range(8):
                        blk = g0 + k
                        op("tensor", lambda e, b=b, blk=blk, psb=psb, k=k: e.transpose(
                               psb[:, k * 128:(k + 1) * 128],
                               tern[b][:, blk * 128:(blk + 1) * 128], self.ident_b[:]),
                           r=(f"wptern{b}", "ident_b"), w=(f"ps{bank}",),
                           label=f"{tag}_T{li}_{blk}")
                    op("scalar", lambda e, psb=psb, b=b: e.copy(
                           stg[b][:], psb[:]),
                       r=(f"ps{bank}",), w=(f"wpstg{b}",), label=f"{tag}_E{li}_{g0}")
                    if wi == 0:
                        dst = bass.AP(scr, (li * 4 + g0 // 8) * 128 * 1024,
                                      [[1024, 128], [1, 1024]])
                    else:
                        dst = bass.AP(scr, g0 * 128 * 1024 + li * 128,
                                      [[1024, 128], [128 * 1024, 8], [1, 128]])
                    dma("qsync", dst, stg[b][:],
                        r=(f"wpstg{b}",), w=(f"wpout{tag}",),
                        label=f"{tag}_st{li}_{g0}")
        dma("qscal", self.cwcols[:], bass.AP(self.hbm["L0_ffn_conv_w"], 0,
                                             [[3, 128], [384, 32], [1, 3]]),
            w=("cwcols",), label="cw_ld")
        dma("qscal", self.cbcols[:], bass.AP(self.hbm["L0_ffn_conv_b"], 0,
                                             [[1, 128], [128, 32]]),
            w=("cbcols",), label="cb_ld", slow_ok=True)
        st.close()
        self.sched.barrier()

    def _absmax_row(self, tag, src_getter, nchunk, out_row, st):
        op = self.op
        assert nchunk <= 8
        amax16 = self.sb([128, 16], F32, f"{tag}_am16", stack=st)
        am128 = self.sb([16, 128], F32, f"{tag}_am128", stack=st)
        probe_ap, _ = src_getter(0)
        is_bf = probe_ap.dtype == BF16
        for tt in range(16):
            bank = tt % 2 if is_bf else (tt % 2) * 2
            if is_bf:
                psv = self.PS[:, bank * 512:(bank + 1) * 512].bitcast(BF16)
                wtags = (f"ps{bank}",)
            else:
                psv = self.PS[:, bank * 512:(bank + 2) * 512]
                wtags = (f"ps{bank}", f"ps{bank + 1}")
            for c in range(nchunk):
                apc, rtags = src_getter(c)
                ident = self.ident_b if is_bf else self.ident_f
                op("tensor", lambda e, apc=apc, tt=tt, c=c, psv=psv, ident=ident: e.transpose(
                       psv[:, c * 128:(c + 1) * 128],
                       apc[:, tt * 128:(tt + 1) * 128], ident[:]),
                   r=tuple(rtags) + ("ident_b", "ident_f"), w=wtags,
                   label=f"{tag}_amT{tt}_{c}")
            op("vector", lambda e, tt=tt, psv=psv, nchunk=nchunk: e.reduce_max(
                   amax16[:, tt:tt + 1], psv[:, 0:nchunk * 128],
                   axis=AX.X, apply_absolute_value=True),
               r=wtags, w=(f"{tag}_am16",), label=f"{tag}_amR{tt}")
        op("tensor", lambda e: e.transpose(self.PS[0:16, 2048:2176], amax16[:],
                                           self.ident_f[:]),
           r=(f"{tag}_am16", "ident_f"), w=("ps4",), label=f"{tag}_amT2")
        op("vector", lambda e: e.tensor_copy(am128[:], self.PS[0:16, 2048:2176]),
           r=("ps4",), w=(f"{tag}_am128",), label=f"{tag}_amE")
        self.dma("qgp", self.row(0, 0), am128[:],
                 r=(f"{tag}_am128",), w=("r0_0",), label=f"{tag}_amG")

    def _quant_rows(self, tag, wi):
        op = self.op
        op("vector", lambda e: e.tensor_scalar_max(self.row(32, 1), self.row(0, 0), 1e-5),
           r=("r0_0",), w=("r32_1",), label=f"{tag}_clip")
        op("vector", lambda e: e.reciprocal(self.row(0, 0), self.row(32, 1)),
           r=("r32_1",), w=("r0_0",), label=f"{tag}_recip")
        op("vector", lambda e: e.tensor_scalar_mul(self.row(0, 0), self.row(0, 0), 127.0),
           r=("r0_0",), w=("r0_0",), label=f"{tag}_sx")
        op("vector", lambda e: e.tensor_scalar_mul(self.row(0, 1), self.row(32, 1), 1.0 / 127.0),
           r=("r32_1",), w=("r0_1",), label=f"{tag}_inv0")
        op("vector", lambda e, wi=wi: e.tensor_scalar_mul(
               self.row(0, 1), self.row(0, 1), self.tiny[0:1, 1 + wi:2 + wi]),
           r=("r0_1", f"cw{wi}"), w=("r0_1",), label=f"{tag}_inv")

    def _bcast_rows(self, tag, sx_sb=None, inv_sb=None):
        op = self.op
        for j in range(4):
            op("tensor", lambda e, j=j: e.matmul(
                   self.PS[:, j * 512:(j + 1) * 512], self.ones_row_f[:],
                   self.row(0, 0, j * 512, (j + 1) * 512), start=True, stop=True),
               r=("r0_0", "ones_row_f"), w=(f"ps{j}",), label=f"{tag}_bs{j}")
            op("tensor", lambda e, j=j: e.matmul(
                   self.PS[:, 2048 + j * 512: 2048 + (j + 1) * 512],
                   self.ones_row_f[:],
                   self.row(0, 1, j * 512, (j + 1) * 512), start=True, stop=True),
               r=("r0_1", "ones_row_f"), w=(f"ps{4 + j}",), label=f"{tag}_bi{j}")
        if sx_sb is not None:
            op("scalar", lambda e: e.copy(sx_sb[:], self.PS[:, 0:2048]),
               r=("ps0", "ps1", "ps2", "ps3"), w=("ff_sxsb",), label=f"{tag}_esx")
            op("scalar", lambda e: e.copy(inv_sb[:], self.PS[:, 2048:4096]),
               r=("ps4", "ps5", "ps6", "ps7"), w=("ff_invsb",), label=f"{tag}_einv")

    def l0_ffn(self, call):
        op, dma = self.op, self.dma
        tg = f"ffn{call}"
        stC = ExitStack()
        scrA = self.sb([128, T], F32, "ffA", stack=stC)
        scrB = self.sb([128, T], F32, "ffB", stack=stC)
        iqt = self.sb([128, T], I32, "ffI", stack=stC)

        self._absmax_row(tg + "qu",
                         lambda c: (self.u[:, c * T:(c + 1) * T], self.u_tags(c)),
                         8, 0, stC)
        self._quant_rows(tg + "qu", 0)
        self._bcast_rows(tg + "qu")
        sx_ps = [self.PS[:, j * 512:(j + 1) * 512] for j in range(4)]
        inv_ps = [self.PS[:, 2048 + j * 512: 2048 + (j + 1) * 512] for j in range(4)]
        for dc in range(8):
            usl = self.u[:, dc * T:(dc + 1) * T]
            def qv(e, usl=usl):
                ins = None
                for j in range(4):
                    ins = e.tensor_mul(scrA[:, j * 512:(j + 1) * 512],
                                       usl[:, j * 512:(j + 1) * 512], sx_ps[j])
                return ins
            op("vector", qv,
               r=self.u_tags(dc) + ("ps0", "ps1", "ps2", "ps3"), w=("ffA",),
               label=f"{tg}_qv{dc}")
            op("scalar", lambda e, usl=usl: e.activation(scrB[:], usl, AF.Sign),
               r=self.u_tags(dc), w=("ffB",), label=f"{tg}_qs{dc}")
            op("vector", lambda e: e.scalar_tensor_tensor(
                   iqt[:], scrB[:], self.halfkk[:], scrA[:], ALU.mult, ALU.add),
               r=("ffA", "ffB", "halfkk"), w=("ffI",), label=f"{tg}_qi{dc}")
            def qq(e, usl=usl):
                ins = None
                for j in range(4):
                    ins = e.tensor_mul(usl[:, j * 512:(j + 1) * 512],
                                       iqt[:, j * 512:(j + 1) * 512], inv_ps[j])
                return ins
            op("vector", qq,
               r=("ffI", "ps4", "ps5", "ps6", "ps7"), w=self.u_tags(dc),
               label=f"{tg}_qq{dc}")

        st1 = ExitStack()
        h1 = [self.sb([128, T], F32, f"ffh1_{i}", stack=st1) for i in range(2)]
        cacc = [self.sb([128, T], F32, f"ffca_{i}", stack=st1) for i in range(2)]
        hout = [self.sb([128, T], BF16, f"ffho_{i}", stack=st1) for i in range(2)]
        w1l = [self.sb([128, 1024], BF16, f"ffw1l_{i}", stack=st1) for i in range(2)]
        op("vector", lambda e: e.memset(scrB[:], 0.0), w=("ffB",), label=f"{tg}_am0")
        for hc in range(32):
            b = hc % 2
            dma("qgp", w1l[b][:],
                bass.AP(self.w1qT, hc * 128 * 1024, [[1024, 128], [1, 1024]]),
                w=(f"ffw1l{b}",), label=f"{tg}_w1l{hc}")
            bk0 = (hc % 2) * 4
            ptags = tuple(f"ps{bk0 + i}" for i in range(4))
            for ts in range(4):
                def mm1(e, hc=hc, ts=ts, b=b, bk0=bk0):
                    ins = None
                    for dc in range(8):
                        ins = e.matmul(
                            self.PS[:, (bk0 + ts) * 512:(bk0 + ts + 1) * 512],
                            w1l[b][:, dc * 128:(dc + 1) * 128],
                            self.u[:, dc * T + ts * 512: dc * T + (ts + 1) * 512],
                            start=(dc == 0), stop=(dc == 7))
                    return ins
                op("tensor", mm1,
                   r=(f"ffw1l{b}",) + self.u_tags(),
                   w=(f"ps{bk0 + ts}",), label=f"{tg}_mm1_{hc}_{ts}")
            op("vector", lambda e, b=b, bk0=bk0: e.tensor_copy(
                   h1[b][:], self.PS[:, bk0 * 512: bk0 * 512 + 2048]),
               r=ptags, w=(f"ffh1{b}",), label=f"{tg}_ev1_{hc}")
            cw = self.cwcols
            op("vector", lambda e, b=b, hc=hc: e.tensor_scalar_mul(
                   cacc[b][:], h1[b][:], cw[:, hc * 3 + 1: hc * 3 + 2]),
               r=(f"ffh1{b}", "cwcols"), w=(f"ffca{b}",), label=f"{tg}_cv1_{hc}")
            op("vector", lambda e, b=b, hc=hc: e.scalar_tensor_tensor(
                   cacc[b][:, 1:T], h1[b][:, 0:T - 1], cw[:, hc * 3: hc * 3 + 1],
                   cacc[b][:, 1:T], ALU.mult, ALU.add),
               r=(f"ffh1{b}", "cwcols", f"ffca{b}"), w=(f"ffca{b}",),
               label=f"{tg}_cv0_{hc}")
            op("vector", lambda e, b=b, hc=hc: e.scalar_tensor_tensor(
                   cacc[b][:, 0:T - 1], h1[b][:, 1:T], cw[:, hc * 3 + 2: hc * 3 + 3],
                   cacc[b][:, 0:T - 1], ALU.mult, ALU.add),
               r=(f"ffh1{b}", "cwcols", f"ffca{b}"), w=(f"ffca{b}",),
               label=f"{tg}_cv2_{hc}")
            op("scalar", lambda e, b=b, hc=hc: e.activation(
                   hout[b][:], cacc[b][:], AF.Silu,
                   bias=self.cbcols[:, hc:hc + 1]),
               r=(f"ffca{b}", "cbcols"), w=(f"ffho{b}",), label=f"{tg}_silu{hc}")
            op("vector", lambda e, b=b: e.scalar_tensor_tensor(
                   scrA[:], hout[b][:], -1.0, hout[b][:], ALU.mult, ALU.max),
               r=(f"ffho{b}",), w=("ffA",), label=f"{tg}_ab{hc}")
            op("vector", lambda e: e.tensor_max(scrB[:], scrB[:], scrA[:]),
               r=("ffA", "ffB"), w=("ffB",), label=f"{tg}_am{hc}")
            dma("qscal", bass.AP(self.h_scr, hc * 128 * T, [[T, 128], [1, T]]),
                hout[b][:],
                r=(f"ffho{b}",), w=(f"ffhs{hc}",), label=f"{tg}_hst{hc}")
        self._absmax_row(tg + "qh", lambda c: (scrB[:], ("ffB",)), 1, 0, st1)
        self._quant_rows(tg + "qh", 1)
        st1.close()
        self.sched.barrier()

        st2 = ExitStack()
        sx_sb = self.sb([128, T], F32, "ffSX", stack=st2)
        inv_sb = self.sb([128, T], F32, "ffIV", stack=st2)
        hbuf = [self.sb([128, T], BF16, f"ffhb_{i}", stack=st2) for i in range(2)]
        w2lbuf = self.sb([128, 8 * 1024], BF16, "ffw2l", stack=st2)
        self._bcast_rows(tg + "qh", sx_sb, inv_sb)
        for g in range(4):
            for k in range(8):
                hc = g * 8 + k
                b = hc % 2
                dma("qgp", hbuf[b][:],
                    bass.AP(self.h_scr, hc * 128 * T, [[T, 128], [1, T]]),
                    r=(f"ffhs{hc}",), w=(f"ffhb{b}",), label=f"{tg}_hld{hc}")
                dma("qsync", w2lbuf[:, k * 1024:(k + 1) * 1024],
                    bass.AP(self.w2qT, hc * 128 * 1024, [[1024, 128], [1, 1024]]),
                    w=(f"ffw2l{k}",), label=f"{tg}_w2l{hc}")
                op("vector", lambda e, b=b: e.tensor_mul(scrA[:], hbuf[b][:], sx_sb[:]),
                   r=(f"ffhb{b}", "ff_sxsb"), w=("ffA",), label=f"{tg}_hv{hc}")
                op("scalar", lambda e, b=b: e.activation(scrB[:], hbuf[b][:], AF.Sign),
                   r=(f"ffhb{b}",), w=("ffB",), label=f"{tg}_hs{hc}")
                op("vector", lambda e: e.scalar_tensor_tensor(
                       iqt[:], scrB[:], self.halfkk[:], scrA[:], ALU.mult, ALU.add),
                   r=("ffA", "ffB", "halfkk"), w=("ffI",), label=f"{tg}_hi{hc}")
                op("vector", lambda e, k=k: e.tensor_mul(
                       self.u[:, k * T:(k + 1) * T], iqt[:], inv_sb[:]),
                   r=("ffI", "ff_invsb"), w=self.u_tags(k), label=f"{tg}_hq{hc}")
            for ts in range(4):
                for oc in range(8):
                    def mm2(e, g=g, ts=ts, oc=oc):
                        ins = None
                        for k in range(8):
                            ins = e.matmul(
                                self.PS[:, oc * 512:(oc + 1) * 512],
                                w2lbuf[:, k * 1024 + oc * 128: k * 1024 + (oc + 1) * 128],
                                self.u[:, k * T + ts * 512: k * T + (ts + 1) * 512],
                                start=(k == 0), stop=(k == 7))
                        return ins
                    op("tensor", mm2,
                       r=self.u_tags()
                         + tuple(f"ffw2l{k}" for k in range(8)),
                       w=(f"ps{oc}",), label=f"{tg}_mm2_{g}_{ts}_{oc}")
                for oc in range(8):
                    xs = self.x[:, oc * T + ts * 512: oc * T + (ts + 1) * 512]
                    if g == 0:
                        pass
                    op("vector", lambda e, oc=oc, xs=xs: e.scalar_tensor_tensor(
                           xs, self.PS[:, oc * 512:(oc + 1) * 512], 0.5, xs,
                           ALU.mult, ALU.add),
                       r=(f"ps{oc}",) + self.x_tags(oc, range(ts * 4, ts * 4 + 4)),
                       w=self.x_tags(oc, range(ts * 4, ts * 4 + 4)),
                       label=f"{tg}_res_{g}_{ts}_{oc}")
        st2.close()
        stC.close()
        self.sched.barrier()

    def _rope_apply(self, tag, pre_lo, pre_hi, out32, trig, tmp, st, out_tag):
        op, dma = self.op, self.dma
        cos = trig[:, 0:T]
        sin = trig[:, T:2 * T]
        h1 = self.sb([16, T], BF16, f"{tag}_h1", stack=st)
        h2 = self.sb([16, T], BF16, f"{tag}_h2", stack=st)
        lo_tags = ("ps0", "ps1", "ps2", "ps3")
        hi_tags = ("ps4", "ps5", "ps6", "ps7")
        op("vector", lambda e: e.tensor_mul(tmp[0], pre_lo, cos),
           r=lo_tags + ("at_trig",), w=("r0_0",), label=f"{tag}_a")
        op("vector", lambda e: e.tensor_mul(tmp[1], pre_hi, sin),
           r=hi_tags + ("at_trig",), w=("r0_1",), label=f"{tag}_b")
        op("vector", lambda e: e.tensor_sub(h1[:], tmp[0], tmp[1]),
           r=("r0_0", "r0_1"), w=(f"{tag}_h1",), label=f"{tag}_c")
        op("vector", lambda e: e.tensor_mul(tmp[0], pre_hi, cos),
           r=hi_tags + ("at_trig",), w=("r0_0",), label=f"{tag}_d")
        op("vector", lambda e: e.tensor_mul(tmp[1], pre_lo, sin),
           r=lo_tags + ("at_trig",), w=("r0_1",), label=f"{tag}_e")
        op("vector", lambda e: e.tensor_add(h2[:], tmp[0], tmp[1]),
           r=("r0_0", "r0_1"), w=(f"{tag}_h2",), label=f"{tag}_f")
        dma("qgp", out32[0:16, :], h1[:], r=(f"{tag}_h1",), w=(out_tag,),
            label=f"{tag}_g")
        dma("qgp", out32[16:32, :], h2[:], r=(f"{tag}_h2",), w=(out_tag,),
            label=f"{tag}_h")

    def _build_VT(self, tg, VT, ckv, wvu_b, NH):
        op = self.op
        for tt in range(16):
            bk = (tt % 2) * 2
            for hi in range(2):
                op("tensor", lambda e, tt=tt, hi=hi, bk=bk: e.matmul(
                       self.PS[:, (bk + hi) * 512:(bk + hi + 1) * 512],
                       ckv[:, tt * 128:(tt + 1) * 128],
                       wvu_b[:, hi * 512:(hi + 1) * 512],
                       start=True, stop=True),
                   r=("at_ckv", "at_wvu"), w=(f"ps{bk + hi}",),
                   label=f"{tg}_vt{tt}_{hi}")
            def vev(e, tt=tt, bk=bk):
                ins = None
                for h8 in range(16):
                    src_c = bk * 512 + h8 * 64
                    ins = e.tensor_copy(
                        VT[:, tt * (16 * 65) + h8 * 65: tt * (16 * 65) + h8 * 65 + 64],
                        self.PS[:, src_c:src_c + 64])
                return ins
            op("vector", vev, r=(f"ps{bk}", f"ps{bk + 1}"), w=("at_VT",),
               label=f"{tg}_vte{tt}")
        op("vector", lambda e: e.memset(
               bass.AP(VT, 64, [[16 * 65 * 16, 128], [65, 16 * 16], [1, 1]]), 1.0),
           r=("at_VT",), w=("at_VT",), label=f"{tg}_vt1")


    def l0_attn(self):
        op, dma = self.op, self.dma
        tg = "att"
        NH, HD, DR = 16, 64, 32
        SC = 1.0 / math.sqrt(HD + DR)
        self.constf(SC)
        self.constf(-1.0)
        self.constf(math.pi / 2.0)
        st = ExitStack()
        attn_f = self.sb([128, 8 * T], BF16, "at_out", stack=st)
        ckv = self.sb([128, T], BF16, "at_ckv", stack=st)
        Q97 = self.sb([97, T], BF16, "at_Q97", stack=st)
        K97 = self.sb([97, T], BF16, "at_K97", stack=st)
        PTb = self.sb([128, 1024], BF16, "at_PT", stack=st)
        PT = [PTb[:, 0:512], PTb[:, 512:1024]]
        masks = self.sb([128, 4 * 512], BF16, "at_mask", stack=st)
        tmp16 = [self.rows[0:16, 0:T], self.rows[0:16, T:2 * T]]
        it16 = self.rows[0:16, T:2 * T].bitcast(I32)
        wbuf = [self.sb([128, 512], F32, "at_wf", stack=st)] * 2
        wku_b = self.sb([128, 1024], BF16, "at_wku", stack=st)
        wvu_b = self.sb([128, 1024], BF16, "at_wvu", stack=st)
        stW = ExitStack()
        trig = self.sb([16, 2 * T], BF16, "at_trig", stack=stW)
        wkv_b = self.sb([128, 8 * 128], BF16, "at_wkv", stack=stW)
        wkr_b = self.sb([128, 32], BF16, "at_wkr", stack=stW)
        wqr_b = self.sb([128, 8 * 32], BF16, "at_wqr", stack=stW)

        def ldcast(hname, ncols, dst, dtag, ap_dims=None, bi=0):
            dims = ap_dims or [[ncols, 128], [1, ncols]]
            for c0 in range(0, ncols, 512):
                cw_ = min(512, ncols - c0)
                if ap_dims is None:
                    src = bass.AP(self.hbm[hname], c0, [[ncols, 128], [1, cw_]])
                else:
                    assert ncols <= 512
                    src = bass.AP(self.hbm[hname], 0, ap_dims)
                dma("qgp", wbuf[bi][:, 0:cw_], src,
                    w=("at_wf0",), label=f"{tg}_{dtag}ld{c0}")
                op("vector", lambda e, c0=c0, cw_=cw_: e.tensor_copy(
                       dst[:, c0:c0 + cw_], wbuf[bi][:, 0:cw_]),
                   r=("at_wf0",), w=(dtag,), label=f"{tg}_{dtag}c{c0}")
        for g4 in range(2):
            dma("qgp", wbuf[0][:],
                bass.AP(self.hbm["L0_attn_W_kv_down"], g4 * 4 * 128 * 128,
                        [[128, 128], [128 * 128, 4], [1, 128]]),
                w=("at_wf0",), label=f"{tg}_kvld{g4}")
            op("vector", lambda e, g4=g4: e.tensor_copy(
                   wkv_b[:, g4 * 512:(g4 + 1) * 512], wbuf[0][:]),
               r=("at_wf0",), w=("at_wkv",), label=f"{tg}_kvc{g4}")
        ldcast("L0_attn_W_k_up", 1024, wku_b, "at_wku", bi=1)
        ldcast("L0_attn_W_v_up", 1024, wvu_b, "at_wvu", bi=0)
        ldcast("L0_attn_W_k_rope", 32, wkr_b, "at_wkr", bi=1)
        ldcast("L0_attn_W_q_rope", 256, wqr_b, "at_wqr",
               ap_dims=[[32, 128], [4096, 8], [1, 32]], bi=0)

        stM = ExitStack()
        itc = self.sb([128, 512], F32, "at_itc", stack=stM)
        itr = self.sb([128, 512], F32, "at_itr", stack=stM)
        op("gpsimd", lambda e: e.iota(itc[:], [[1, 512]], channel_multiplier=0,
                                      allow_small_or_imprecise_dtypes=True),
           w=("at_itc",), label=f"{tg}_ic")
        op("gpsimd", lambda e: e.iota(itr[:], [[0, 512]], channel_multiplier=1,
                                      allow_small_or_imprecise_dtypes=True),
           w=("at_itr",), label=f"{tg}_ir")
        for di in range(4):
            op("vector", lambda e, di=di: e.scalar_tensor_tensor(
                   masks[:, di * 512:(di + 1) * 512], itr[:], float(di * 128),
                   itc[:], ALU.add, ALU.is_le),
               r=("at_itr", "at_itc"), w=("at_mask",), label=f"{tg}_mk{di}")
        stM.close()
        self.sched.barrier()

        op("gpsimd", lambda e: e.iota(self.row(0, 0, 0, 16), [[1, 16]],
                                      channel_multiplier=0,
                                      allow_small_or_imprecise_dtypes=True),
           w=("r0_0",), label=f"{tg}_fi")
        op("vector", lambda e: e.tensor_scalar_mul(
               self.row(0, 0, 0, 16), self.row(0, 0, 0, 16),
               -math.log(10000.0) / 16.0),
           r=("r0_0",), w=("r0_0",), label=f"{tg}_fs")
        op("scalar", lambda e: e.activation(self.row(0, 0, 0, 16),
                                            self.row(0, 0, 0, 16), AF.Exp),
           r=("r0_0",), w=("r0_0",), label=f"{tg}_fe")
        op("gpsimd", lambda e: e.iota(self.row(0, 1), [[1, T]], channel_multiplier=0,
                                      allow_small_or_imprecise_dtypes=True),
           w=("r0_1",), label=f"{tg}_ti")
        for j in range(4):
            op("tensor", lambda e, j=j: e.matmul(
                   self.PS[0:16, j * 512:(j + 1) * 512],
                   self.row(0, 0, 0, 16), self.row(0, 1, j * 512, (j + 1) * 512),
                   start=True, stop=True),
               r=("r0_0", "r0_1"), w=(f"ps{j}",), label=f"{tg}_th{j}")
        TP = 2.0 * math.pi
        op("vector", lambda e: e.tensor_scalar(
               it16, self.PS[0:16, 0:2048], 1.0 / TP, self.halfkk[0:16, 0:1],
               ALU.mult, ALU.add),
           r=("ps0", "ps1", "ps2", "ps3", "halfkk"), w=("r0_1",), label=f"{tg}_rk")
        op("vector", lambda e: e.tensor_copy(tmp16[0], it16),
           r=("r0_1",), w=("r0_0",), label=f"{tg}_kf")
        op("vector", lambda e: e.scalar_tensor_tensor(
               tmp16[1], tmp16[0], -TP, self.PS[0:16, 0:2048], ALU.mult, ALU.add),
           r=("r0_0", "ps0", "ps1", "ps2", "ps3"), w=("r0_1",), label=f"{tg}_thr")
        op("scalar", lambda e: e.activation(trig[:, T:2 * T], tmp16[1], AF.Sin),
           r=("r0_1",), w=("at_trig",), label=f"{tg}_sin")
        op("vector", lambda e: e.scalar_tensor_tensor(
               tmp16[0], tmp16[1], -1.0, tmp16[1], ALU.mult, ALU.max),
           r=("r0_1",), w=("r0_0",), label=f"{tg}_abs")
        op("scalar", lambda e: e.activation(trig[:, 0:T], tmp16[0], AF.Sin,
                                            scale=-1.0, bias=math.pi / 2.0),
           r=("r0_0",), w=("at_trig",), label=f"{tg}_cos")

        for ts in range(4):
            def mmc(e, ts=ts):
                ins = None
                for dc in range(8):
                    ins = e.matmul(
                        self.PS[:, 2048 + ts * 512: 2048 + (ts + 1) * 512],
                        wkv_b[:, dc * 128:(dc + 1) * 128],
                        self.u[:, dc * T + ts * 512: dc * T + (ts + 1) * 512],
                        start=(dc == 0), stop=(dc == 7))
                return ins
            op("tensor", mmc, r=("at_wkv",) + self.u_tags(),
               w=(f"ps{4 + ts}",), label=f"{tg}_ckv{ts}")
        op("vector", lambda e: e.tensor_copy(ckv[:], self.PS[:, 2048:4096]),
           r=("ps4", "ps5", "ps6", "ps7"), w=("at_ckv",), label=f"{tg}_ckve")

        for hi in range(2):
            for ts in range(4):
                op("tensor", lambda e, hi=hi, ts=ts: e.matmul(
                       self.PS[0:16, hi * 2048 + ts * 512: hi * 2048 + (ts + 1) * 512],
                       wkr_b[:, hi * 16:(hi + 1) * 16],
                       ckv[:, ts * 512:(ts + 1) * 512],
                       start=True, stop=True),
                   r=("at_wkr", "at_ckv"), w=(f"ps{hi * 4 + ts}",),
                   label=f"{tg}_krp{hi}_{ts}")
        self._rope_apply(tg + "kr", self.PS[0:16, 0:2048], self.PS[0:16, 2048:4096],
                         K97[64:96, :], trig, tmp16, stW, "at_K97")
        for hi in range(2):
            for ts in range(4):
                def mmqr(e, hi=hi, ts=ts):
                    ins = None
                    for dc in range(8):
                        ins = e.matmul(
                            self.PS[0:16, hi * 2048 + ts * 512: hi * 2048 + (ts + 1) * 512],
                            wqr_b[:, dc * 32 + hi * 16: dc * 32 + hi * 16 + 16],
                            self.u[:, dc * T + ts * 512: dc * T + (ts + 1) * 512],
                            start=(dc == 0), stop=(dc == 7))
                    return ins
                op("tensor", mmqr, r=("at_wqr",) + self.u_tags(),
                   w=(f"ps{hi * 4 + ts}",), label=f"{tg}_qrp{hi}_{ts}")
        self._rope_apply(tg + "qr", self.PS[0:16, 0:2048], self.PS[0:16, 2048:4096],
                         Q97[64:96, :], trig, tmp16, stW, "at_Q97")

        stW.close()
        self.sched.barrier()
        VT = self.sb([128, 16 * (NH * 65)], BF16, "at_VT", stack=st)
        wq_b = [self.sb([128, 8 * 64], BF16, "at_wq", stack=st)] * 2
        wo_b = [self.sb([128, 8 * 128], BF16, "at_wo", stack=st)] * 2
        rmax = self.sb([128, 16], F32, "at_rmax", stack=st)
        rm128 = self.rows[0:16, 1024:1152]
        osb = self.rows[0:64, 0:256].bitcast(BF16)
        linv = self.rows[0:1, 512:1024]
        self._build_VT(tg, VT, ckv, wvu_b, NH)
        op("vector", lambda e: e.memset(K97[96:97, :], 1.0),
           r=("at_K97",), w=("at_K97",), label=f"{tg}_k97o")

        for h in range(NH):
            hb = h % 2
            dma("qgp", wbuf[hb][:],
                bass.AP(self.hbm["L0_attn_W_q"], h * 64,
                        [[1024, 128], [128 * 1024, 8], [1, 64]]),
                w=("at_wf0",), label=f"{tg}_wq{h}")
            op("vector", lambda e, hb=hb: e.tensor_copy(wq_b[hb][:], wbuf[hb][:, 0:512]),
               r=("at_wf0",), w=("at_wq0",), label=f"{tg}_wqc{h}")
            for ts in range(4):
                def mmq(e, ts=ts, hb=hb):
                    ins = None
                    for dc in range(8):
                        ins = e.matmul(
                            self.PS[0:64, ts * 512:(ts + 1) * 512],
                            wq_b[hb][:, dc * 64:(dc + 1) * 64],
                            self.u[:, dc * T + ts * 512: dc * T + (ts + 1) * 512],
                            start=(dc == 0), stop=(dc == 7))
                    return ins
                op("tensor", mmq, r=("at_wq0",) + self.u_tags(),
                   w=(f"ps{ts}",), label=f"{tg}_q{h}_{ts}")
                op("tensor", lambda e, h=h, ts=ts: e.matmul(
                       self.PS[0:64, 2048 + ts * 512: 2048 + (ts + 1) * 512],
                       wku_b[:, h * 64:(h + 1) * 64],
                       ckv[:, ts * 512:(ts + 1) * 512],
                       start=True, stop=True),
                   r=("at_wku", "at_ckv"), w=(f"ps{4 + ts}",), label=f"{tg}_k{h}_{ts}")
            op("vector", lambda e: e.tensor_copy(Q97[0:64, :], self.PS[0:64, 0:2048]),
               r=("ps0", "ps1", "ps2", "ps3"), w=("at_Q97",), label=f"{tg}_qe{h}")
            op("vector", lambda e: e.tensor_copy(K97[0:64, :], self.PS[0:64, 2048:4096]),
               r=("ps4", "ps5", "ps6", "ps7"), w=("at_K97",), label=f"{tg}_ke{h}")
            for qi in range(16):
                nk = 128 * (qi + 1)
                nsl = (nk + 511) // 512
                def mmp1(e, qi=qi, nk=nk, nsl=nsl):
                    ins = None
                    for sl in range(nsl):
                        n0 = sl * 512
                        n1 = min(nk, n0 + 512)
                        ins = e.matmul(
                            self.PS[:, sl * 512: sl * 512 + (n1 - n0)],
                            Q97[0:96, qi * 128:(qi + 1) * 128],
                            K97[0:96, n0:n1],
                            start=True, stop=True)
                    return ins
                op("tensor", mmp1, r=("at_Q97", "at_K97"),
                   w=tuple(f"ps{b2}" for b2 in range(nsl)),
                   label=f"{tg}_p1_{h}_{qi}")
                op("vector", lambda e, qi=qi, nk=nk, nsl=nsl: e.reduce_max(
                       rmax[:, qi:qi + 1], self.PS[:, 0:nk], axis=AX.X,
                       negate=True),
                   r=tuple(f"ps{b2}" for b2 in range(nsl)),
                   w=("at_rmax",), label=f"{tg}_rx_{h}_{qi}")
            op("tensor", lambda e: e.transpose(self.PS[0:16, 3584:3712], rmax[:],
                                               self.ident_f[:]),
               r=("at_rmax", "ident_f"), w=("ps7",), label=f"{tg}_rxT{h}")
            op("vector", lambda e: e.tensor_copy(rm128[:], self.PS[0:16, 3584:3712]),
               r=("ps7",), w=("r0_0",), label=f"{tg}_rxE{h}")
            dma("qgp", Q97[96:97, :], rm128[:],
                r=("r0_0",), w=("at_Q97",), label=f"{tg}_rxG{h}")
            for qc in range(4):
                nkt = (qc + 1) * 4
                obank = 6
                for kt0 in range(0, nkt, 2):
                    for kt in (kt0, kt0 + 1):
                        sbk = 4 + kt % 2
                        op("tensor", lambda e, kt=kt, qc=qc, sbk=sbk: e.matmul(
                               self.PS[:, sbk * 512:(sbk + 1) * 512],
                               K97[:, kt * 128:(kt + 1) * 128],
                               Q97[:, qc * 512:(qc + 1) * 512],
                               start=True, stop=True),
                           r=("at_Q97", "at_K97"), w=(f"ps{sbk}",),
                           label=f"{tg}_p2_{h}_{qc}_{kt}")
                    op("scalar", lambda e: e.activation(
                           PTb[:], self.PS[:, 2048:3072], AF.Exp, scale=SC),
                       r=("ps4", "ps5"), w=("at_PT",),
                       label=f"{tg}_ex_{h}_{qc}_{kt0}")
                    for kt in (kt0, kt0 + 1):
                        di = kt - qc * 4
                        if di >= 0:
                            op("vector", lambda e, pb=kt % 2, di=di: e.tensor_mul(
                                   PT[pb], PT[pb],
                                   masks[:, di * 512:(di + 1) * 512]),
                               r=("at_PT", "at_mask"), w=("at_PT",),
                               label=f"{tg}_msk_{h}_{qc}_{kt}")
                    for kt in (kt0, kt0 + 1):
                        op("tensor",
                           lambda e, kt=kt, pb=kt % 2, h=h, nkt=nkt, obank=obank:
                           e.matmul(
                               self.PS[0:65, obank * 512:(obank + 1) * 512],
                               VT[:, kt * (16 * 65) + h * 65: kt * (16 * 65) + (h + 1) * 65],
                               PT[pb],
                               start=(kt == 0), stop=(kt == nkt - 1)),
                           r=("at_PT", "at_VT"), w=(f"ps{obank}",),
                           label=f"{tg}_av_{h}_{qc}_{kt}")
                op("scalar", lambda e, obank=obank: e.activation(
                       linv[:], self.PS[64:65, obank * 512:(obank + 1) * 512],
                       AF.Ln),
                   r=(f"ps{obank}",), w=("r0_0",), label=f"{tg}_li_{h}_{qc}")
                op("scalar", lambda e: e.activation(
                       linv[:], linv[:], AF.Exp, scale=-1.0),
                   r=("r0_0",), w=("r0_0",), label=f"{tg}_li2_{h}_{qc}")
                op("tensor", lambda e: e.matmul(
                       self.PS[0:64, 3584:4096], self.ones_row_f[0:1, 0:64],
                       linv[:], start=True, stop=True),
                   r=("r0_0", "ones_row_f"), w=("ps7",), label=f"{tg}_lb_{h}_{qc}")
                op("scalar", lambda e, obank=obank: e.copy(
                       osb[:], self.PS[0:64, obank * 512:(obank + 1) * 512]),
                   r=(f"ps{obank}",), w=("r32_0",), label=f"{tg}_oe_{h}_{qc}")
                dc0 = h // 2
                r0 = 64 * (h % 2)
                op("vector", lambda e, dc0=dc0, r0=r0, qc=qc: e.tensor_mul(
                       attn_f[r0:r0 + 64, dc0 * T + qc * 512: dc0 * T + (qc + 1) * 512],
                       osb[:], self.PS[0:64, 3584:4096]),
                   r=("r32_0", "ps7"), w=(f"at_o{dc0}",),
                   label=f"{tg}_on_{h}_{qc}")

        for oc in range(8):
            hb = oc % 2
            for g4 in range(2):
                dma("qgp", wbuf[hb][:],
                    bass.AP(self.hbm["L0_attn_W_o"], oc * 128 + g4 * 4 * 128 * 1024,
                            [[1024, 128], [128 * 1024, 4], [1, 128]]),
                    w=("at_wf0",), label=f"{tg}_wo{oc}_{g4}")
                op("vector", lambda e, hb=hb, g4=g4: e.tensor_copy(
                       wo_b[hb][:, g4 * 512:(g4 + 1) * 512], wbuf[hb][:]),
                   r=("at_wf0",), w=("at_wo0",), label=f"{tg}_woc{oc}_{g4}")
            for ts in range(4):
                bk = (oc % 2) * 4 + ts
                def mmo(e, ts=ts, hb=hb, bk=bk):
                    ins = None
                    for dc in range(8):
                        ins = e.matmul(
                            self.PS[:, bk * 512:(bk + 1) * 512],
                            wo_b[hb][:, dc * 128:(dc + 1) * 128],
                            attn_f[:, dc * T + ts * 512: dc * T + (ts + 1) * 512],
                            start=(dc == 0), stop=(dc == 7))
                    return ins
                op("tensor", mmo,
                   r=("at_wo0",) + tuple(f"at_o{d}" for d in range(8)),
                   w=(f"ps{bk}",), label=f"{tg}_mo_{oc}_{ts}")
                op("vector", lambda e, oc=oc, ts=ts, bk=bk: e.tensor_add(
                       self.x[:, oc * T + ts * 512: oc * T + (ts + 1) * 512],
                       self.x[:, oc * T + ts * 512: oc * T + (ts + 1) * 512],
                       self.PS[:, bk * 512:(bk + 1) * 512]),
                   r=(f"ps{bk}",) + self.x_tags(oc, range(ts * 4, ts * 4 + 4)),
                   w=self.x_tags(oc, range(ts * 4, ts * 4 + 4)),
                   label=f"{tg}_res_{oc}_{ts}")
        st.close()
        self.sched.barrier()

    def prep_l1_ffn_weights(self):
        op, dma = self.op, self.dma
        tg = "p1f"
        st = ExitStack()
        wf = [self.sb([128, 2048], F32, f"p1f_wf{i}", stack=st) for i in range(2)]
        wb = [self.sb([128, 2048], BF16, f"p1f_wb{i}", stack=st) for i in range(2)]
        stg = [self.sb([128, 2048], BF16, f"p1f_stg{i}", stack=st) for i in range(2)]
        wc = self.hbm["L1_ffn_w_cproj"]
        for oc in range(8):
            for slab in range(8):
                b = slab % 2
                dma("qgp", wf[b][:],
                    bass.AP(wc, oc * 128 * 16384 + slab * 2048,
                            [[16384, 128], [1, 2048]]),
                    w=(f"p1f_wf{b}",), label=f"{tg}_ld{oc}_{slab}")
                op("vector", lambda e, b=b: e.tensor_copy(wb[b][:], wf[b][:]),
                   r=(f"p1f_wf{b}",), w=(f"p1f_wb{b}",), label=f"{tg}_c{oc}_{slab}")
                for g8 in range(2):
                    bank = 2 + g8
                    psb = self.PS[:, bank * 512:(bank + 1) * 512].bitcast(BF16)
                    for k in range(8):
                        blk = g8 * 8 + k
                        op("tensor", lambda e, b=b, blk=blk, psb=psb, k=k: e.transpose(
                               psb[:, k * 128:(k + 1) * 128],
                               wb[b][:, blk * 128:(blk + 1) * 128], self.ident_b[:]),
                           r=(f"p1f_wb{b}", "ident_b"), w=(f"ps{bank}",),
                           label=f"{tg}_T{oc}_{slab}_{blk}")
                    op("scalar", lambda e, psb=psb, b=b, g8=g8: e.copy(
                           stg[b][:, g8 * 1024:(g8 + 1) * 1024], psb[:]),
                       r=(f"ps{bank}",), w=(f"p1f_stg{b}",),
                       label=f"{tg}_E{oc}_{slab}_{g8}")
                dma("qsync", bass.AP(self.wcT,
                                     slab * 16 * 128 * 1024 + oc * 128,
                                     [[1024, 128], [128 * 1024, 16], [1, 128]]),
                    stg[b][:],
                    r=(f"p1f_stg{b}",), w=("wcT_out",), label=f"{tg}_st{oc}_{slab}")
        st.close()
        self.sched.barrier()
        st2 = ExitStack()
        Cst = self.sb([128, 32 * 256], BF16, "l1_Cstb", stack=st2)
        mk = self.sb([128, 16], F32, "p1f_mk", stack=st2)
        upf = self.sb([128, 16], F32, "p1f_up", stack=st2)
        mkT = self.sb([16, 128], BF16, "p1f_mkT", stack=st2)
        upT = self.sb([16, 128], BF16, "p1f_upT", stack=st2)
        bm1 = self.sb([32, 128], F32, "p1f_bm1", stack=st2)
        bm2 = self.sb([32, 128], F32, "p1f_bm2", stack=st2)
        bdm = self.sb([32, 128], BF16, "p1f_bdm", stack=st2)
        op("gpsimd", lambda e: e.iota(bm1[:], [[0, 128]], channel_multiplier=1,
                                      allow_small_or_imprecise_dtypes=True),
           w=("p1f_bm1",), label=f"{tg}_bi1")
        op("gpsimd", lambda e: e.iota(bm2[:], [[1, 128]], channel_multiplier=0,
                                      allow_small_or_imprecise_dtypes=True),
           w=("p1f_bm2",), label=f"{tg}_bi2")
        op("vector", lambda e: e.tensor_single_scalar(bm1[:], bm1[:], 16.0, ALU.is_ge),
           r=("p1f_bm1",), w=("p1f_bm1",), label=f"{tg}_bg1")
        op("vector", lambda e: e.tensor_single_scalar(bm2[:], bm2[:], 64.0, ALU.is_ge),
           r=("p1f_bm2",), w=("p1f_bm2",), label=f"{tg}_bg2")
        op("vector", lambda e: e.tensor_tensor(bdm[:], bm1[:], bm2[:], ALU.is_equal),
           r=("p1f_bm1", "p1f_bm2"), w=("p1f_bdm",), label=f"{tg}_bdm")
        op("vector", lambda e, Cst=Cst: e.memset(Cst[:], 0.0),
           w=("l1_Cst",), label=f"{tg}_cst0")
        mkh = self.hbm["L1_ffn_rail_maker"]
        uph = self.hbm["L1_ffn_rail_up_proj"]
        for g in range(32):
            h, c = g // 8, g % 8
            dma("qgp", mk[:], bass.AP(mkh, (h * 64 + 8 * c) * 256,
                                      [[16, 128], [1, 16]]),
                w=("p1f_mk",), label=f"{tg}_mk{g}")
            op("tensor", lambda e: e.transpose(self.PS[0:16, 0:128],
                                               mk[:], self.ident_f[:]),
               r=("p1f_mk", "ident_f"), w=("ps0",), label=f"{tg}_mT{g}")
            op("vector", lambda e: e.tensor_copy(mkT[:], self.PS[0:16, 0:128]),
               r=("ps0",), w=("p1f_mkT",), label=f"{tg}_mE{g}")
            for m in range(4):
                dma("qgp", upf[:], bass.AP(uph, (h * 64 + 8 * c + 2 * m) * 1024,
                                           [[16, 128], [1, 16]]),
                    w=("p1f_up",), label=f"{tg}_up{g}_{m}")
                op("tensor", lambda e: e.transpose(self.PS[0:16, 512:640],
                                                   upf[:], self.ident_f[:]),
                   r=("p1f_up", "ident_f"), w=("ps1",), label=f"{tg}_uT{g}_{m}")
                op("vector", lambda e: e.tensor_copy(upT[:], self.PS[0:16, 512:640]),
                   r=("ps1",), w=("p1f_upT",), label=f"{tg}_uE{g}_{m}")
                op("tensor", lambda e, m=m: e.matmul(
                       self.PS[0:32, 1024:1152], mkT[:, m * 32:(m + 1) * 32],
                       upT[:], start=True, stop=True),
                   r=("p1f_mkT", "p1f_upT"), w=("ps2",), label=f"{tg}_cm{g}_{m}")
                if m < 3:
                    dst = Cst[32 * m:32 * (m + 1), g * 256:g * 256 + 128]
                else:
                    dst = Cst[96:128, g * 256 + 128:g * 256 + 256]
                op("vector", lambda e, dst=dst: e.tensor_mul(
                       dst, self.PS[0:32, 1024:1152], bdm[:]),
                   r=("ps2", "p1f_bdm"), w=("l1_Cst",), label=f"{tg}_cw{g}_{m}")
        dma("qsync", self.CsT_hbm.ap(), Cst[:], r=("l1_Cst",), w=("CsT_out",),
            label=f"{tg}_cst_save")
        st2.close()
        self.sched.barrier()

    def l1_ffn(self, call):
        op, dma = self.op, self.dma
        tg = f"l1f{call}"
        st = ExitStack()
        Cst = self.sb([128, 32 * 256], BF16, "l1_Cst", stack=st)
        dma("qgp", Cst[:], self.CsT_hbm.ap(), r=("CsT_out",), w=("l1_Cst",),
            label=f"{tg}_cst_ld")
        hact = self.sb([128, 32 * 512], BF16, "l1_hact", stack=st)
        spill = self.sb([128, 8 * 512], F32, "l1_spill", stack=st)
        wl = [self.sb([128, 2048], BF16, f"l1_wl{i}", stack=st) for i in range(2)]
        for tc in range(4):
            for h in range(4):
                for c in range(8):
                    g = h * 8 + c
                    gl = c * 4
                    bsel = (c % 2) * 4
                    def mmr(e, g=g, c=c, tc=tc, bsel=bsel):
                        ins = None
                        for m in range(4):
                            if m < 3:
                                lh = Cst[32 * m:32 * (m + 1),
                                         g * 256:g * 256 + 128]
                                rh = self.u[32 * m:32 * (m + 1),
                                            c * T + tc * 512:c * T + (tc + 1) * 512]
                            else:
                                lh = Cst[64:128, g * 256 + 128:g * 256 + 256]
                                rh = self.u[64:128,
                                            c * T + tc * 512:c * T + (tc + 1) * 512]
                            ins = e.matmul(
                                self.PS[:, (bsel + m) * 512:(bsel + m + 1) * 512],
                                lh, rh, start=True, stop=True)
                        return ins
                    op("tensor", mmr, r=("l1_Cst",) + self.u_tags(c),
                       w=tuple(f"ps{bsel + m}" for m in range(4)),
                       label=f"{tg}_r{tc}_{g}")
                    op("scalar", lambda e, gl=gl, bsel=bsel: e.activation(
                           hact[:, gl * 512:(gl + 4) * 512],
                           self.PS[:, bsel * 512:(bsel + 4) * 512], AF.Gelu),
                       r=tuple(f"ps{bsel + m}" for m in range(4)),
                       w=("l1_hact",), label=f"{tg}_g{tc}_{g}")
                for i32 in range(32):
                    hc = h * 32 + i32
                    b = (i32 // 2) % 2
                    if i32 % 2 == 0:
                        dma("qgp", wl[b][:],
                            bass.AP(self.wcT, hc * 128 * 1024,
                                    [[1024, 128], [131072, 2], [1, 1024]]),
                            w=(f"l1_wl{b}",), label=f"{tg}_w{tc}_{hc}")
                    half = (i32 % 2) * 1024
                    def mmw(e, i32=i32, b=b, half=half):
                        ins = None
                        for oc in range(8):
                            ins = e.matmul(
                                self.PS[:, oc * 512:(oc + 1) * 512],
                                wl[b][:, half + oc * 128: half + (oc + 1) * 128],
                                hact[:, i32 * 512:(i32 + 1) * 512],
                                start=(i32 == 0), stop=(i32 == 31))
                        return ins
                    op("tensor", mmw, r=(f"l1_wl{b}", "l1_hact"),
                       w=tuple(f"ps{oc}" for oc in range(8)),
                       label=f"{tg}_m{tc}_{hc}")
                for oc in range(8):
                    if h == 0:
                        op("vector", lambda e, oc=oc: e.tensor_copy(
                               spill[:, oc * 512:(oc + 1) * 512],
                               self.PS[:, oc * 512:(oc + 1) * 512]),
                           r=(f"ps{oc}",), w=("l1_spill",), label=f"{tg}_s{tc}_{h}_{oc}")
                    else:
                        op("vector", lambda e, oc=oc: e.tensor_add(
                               spill[:, oc * 512:(oc + 1) * 512],
                               spill[:, oc * 512:(oc + 1) * 512],
                               self.PS[:, oc * 512:(oc + 1) * 512]),
                           r=(f"ps{oc}", "l1_spill"), w=("l1_spill",),
                           label=f"{tg}_s{tc}_{h}_{oc}")
            for oc in range(8):
                xs = self.x[:, oc * T + tc * 512: oc * T + (tc + 1) * 512]
                op("vector", lambda e, oc=oc, xs=xs: e.scalar_tensor_tensor(
                       xs, spill[:, oc * 512:(oc + 1) * 512], 0.5, xs,
                       ALU.mult, ALU.add),
                   r=("l1_spill",) + self.x_tags(oc, range(tc * 4, tc * 4 + 4)),
                   w=self.x_tags(oc, range(tc * 4, tc * 4 + 4)),
                   label=f"{tg}_x{tc}_{oc}")
        st.close()
        self.sched.barrier()

    def prep_l1_attn_weights(self):
        op, dma = self.op, self.dma
        tg = "p1a"
        st = ExitStack()
        wf = [self.sb([128, 2048], F32, f"p1a_wf{i}", stack=st) for i in range(2)]
        wb = [self.sb([128, 2048], BF16, f"p1a_wb{i}", stack=st) for i in range(2)]
        stg = [self.sb([128, 2048], BF16, f"p1a_stg{i}", stack=st) for i in range(2)]

        def tile_pass(hname, scratch, n_tiles, nblk, lbl):
            h = self.hbm[hname]
            ncol = nblk * 128
            for it in range(n_tiles):
                b = it % 2
                dma("qgp", wf[b][:, 0:ncol],
                    bass.AP(h, it * 128 * ncol, [[ncol, 128], [1, ncol]]),
                    w=(f"p1a_wf{b}",), label=f"{lbl}_ld{it}")
                op("vector", lambda e, b=b, ncol=ncol: e.tensor_copy(
                       wb[b][:, 0:ncol], wf[b][:, 0:ncol]),
                   r=(f"p1a_wf{b}",), w=(f"p1a_wb{b}",), label=f"{lbl}_c{it}")
                for g in range(nblk // 8):
                    bank = 2 + g % 2
                    psb = self.PS[:, bank * 512:(bank + 1) * 512].bitcast(BF16)
                    for k in range(8):
                        blk = g * 8 + k
                        op("tensor", lambda e, b=b, blk=blk, psb=psb, k=k:
                           e.transpose(psb[:, k * 128:(k + 1) * 128],
                                       wb[b][:, blk * 128:(blk + 1) * 128],
                                       self.ident_b[:]),
                           r=(f"p1a_wb{b}", "ident_b"), w=(f"ps{bank}",),
                           label=f"{lbl}_T{it}_{blk}")
                    op("scalar", lambda e, psb=psb, b=b, g=g: e.copy(
                           stg[b][:, g * 1024:(g + 1) * 1024], psb[:]),
                       r=(f"ps{bank}",), w=(f"p1a_stg{b}",),
                       label=f"{lbl}_E{it}_{g}")
                dma("qsync",
                    bass.AP(scratch, it * 128 * ncol, [[ncol, 128], [1, ncol]]),
                    stg[b][:, 0:ncol],
                    r=(f"p1a_stg{b}",), w=(f"{lbl}_out",), label=f"{lbl}_st{it}")

        tile_pass("L1_attn_W_in", self.winT, 32, 8, "pwi")
        tile_pass("L1_attn_W_x", self.wxT, 16, 16, "pwx")
        tile_pass("L1_attn_W_out", self.woT, 8, 16, "pwo")
        wxh = self.hbm["L1_attn_W_x"]
        dma("qgp", wf[0][0:32, 0:2048],
            bass.AP(wxh, 2048 * 2048, [[2048, 32], [1, 2048]]),
            w=("p1a_wf0",), label=f"{tg}_bcld")
        op("vector", lambda e: e.tensor_copy(wb[0][0:32, 0:2048], wf[0][0:32, 0:2048]),
           r=("p1a_wf0",), w=("p1a_wb0",), label=f"{tg}_bcc")
        psb = self.PS[:, 1024:1536].bitcast(BF16)
        for blk in range(16):
            op("tensor", lambda e, blk=blk, psb=psb: e.transpose(
                   psb[:, blk * 32:(blk + 1) * 32],
                   wb[0][0:32, blk * 128:(blk + 1) * 128],
                   self.ident_b[0:32, 0:32]),
               r=("p1a_wb0", "ident_b"), w=("ps2",), label=f"{tg}_bcT{blk}")
        op("scalar", lambda e, psb=psb: e.copy(stg[0][:, 0:512], psb[:, 0:512]),
           r=("ps2",), w=("p1a_stg0",), label=f"{tg}_bcE")
        dma("qsync",
            bass.AP(self.wxT, 16 * 128 * 2048, [[2048, 128], [1, 512]]),
            stg[0][:, 0:512],
            r=("p1a_stg0",), w=("pwx_out",), label=f"{tg}_bcst")
        st.close()
        self.sched.barrier()

    def l1_attn(self):
        op, dma = self.op, self.dma
        tg = "mam"
        L, W, TC = 64, 16 * 65, 512
        one = self.constf(1.0)
        st = ExitStack()
        sel = self.sb([48, 2048], BF16, "mam_sel", stack=st)
        stS = ExitStack()
        ncol = self.sb([48, 2048], F32, "mam_ncol", stack=stS)
        qv = self.sb([48, 4], F32, "mam_qv", stack=stS)
        op("gpsimd", lambda e: e.iota(ncol[:], [[1, 16], [0, 128]],
                                      channel_multiplier=0,
                                      allow_small_or_imprecise_dtypes=True),
           w=("mam_ncol",), label=f"{tg}_in")
        op("gpsimd", lambda e: e.iota(qv[:, 0:1], [[0, 1]], channel_multiplier=1,
                                      allow_small_or_imprecise_dtypes=True),
           w=("mam_qv",), label=f"{tg}_iq")
        op("vector", lambda e: e.tensor_single_scalar(
               qv[:, 1:2], qv[:, 0:1], 16.0, ALU.is_ge),
           r=("mam_qv",), w=("mam_qv",), label=f"{tg}_q1")
        op("vector", lambda e: e.tensor_single_scalar(
               qv[:, 2:3], qv[:, 0:1], 32.0, ALU.is_ge),
           r=("mam_qv",), w=("mam_qv",), label=f"{tg}_q2")
        op("vector", lambda e: e.scalar_tensor_tensor(
               qv[:, 3:4], qv[:, 1:2], -16.0, qv[:, 0:1], ALU.mult, ALU.add),
           r=("mam_qv",), w=("mam_qv",), label=f"{tg}_q3")
        op("vector", lambda e: e.scalar_tensor_tensor(
               qv[:, 3:4], qv[:, 2:3], -16.0, qv[:, 3:4], ALU.mult, ALU.add),
           r=("mam_qv",), w=("mam_qv",), label=f"{tg}_q4")
        op("vector", lambda e: e.tensor_tensor(
               sel[:], ncol[:], bass.AP(qv, 3, [[4, 48], [0, 2048]]),
               ALU.is_equal),
           r=("mam_ncol", "mam_qv"), w=("mam_sel",), label=f"{tg}_sel")
        stS.close()
        self.sched.barrier()
        xy = self.sb([128, 8240], BF16, "mam_xy", stack=st)
        hconv = self.sb([128, 8192], BF16, "mam_hconv", stack=st)
        dl = self.sb([128, 8192], BF16, "mam_dl", stack=st)
        BbS = [self.sb([128, 1024], BF16, f"mam_Bb{i}", stack=st) for i in range(2)]
        CbS = [self.sb([128, 1024], BF16, f"mam_Cb{i}", stack=st) for i in range(2)]
        ab = [self.sb([128, W], BF16, f"mam_ab{i}", stack=st) for i in range(2)]
        bu = [self.sb([128, W], BF16, f"mam_bu{i}", stack=st) for i in range(2)]
        hs = [self.sb([128, W], BF16, f"mam_hs{i}", stack=st) for i in range(2)]
        du = [self.sb([128, 1024], BF16, "mam_du", stack=st)] * 2
        G = [self.sb([128, 1024], BF16, f"mam_G{i}", stack=st) for i in range(2)]
        BCsb = self.sb([48, 512], BF16, "mam_BC", stack=st)
        A_sb = self.sb([128, 256], BF16, "mam_A", stack=st)
        Dcol = self.sb([128, 16], F32, "mam_D", stack=st)
        cw2 = self.sb([128, 64], F32, "mam_cw", stack=st)
        cb2 = self.sb([128, 16], F32, "mam_cb", stack=st)
        carr = self.sb([128, 256], BF16, "mam_carr", stack=st)
        halocp = self.sb([128, 48], BF16, "mam_halo", stack=st)
        ztmp = [self.sb([128, 512], BF16, "mam_zt", stack=st)] * 2
        s2f = [self.sb([128, 512], F32, f"mam_s2f{i}", stack=st) for i in range(2)]
        wbig = self.sb([128, 4096], BF16, "mam_wbig", stack=st)

        def wq(lo, hi):
            return tuple(f"mam_wb{q}" for q in range(lo // 1024, (hi + 1023) // 1024))
        hcs = tuple(f"mam_hc{ec}" for ec in range(16))
        dls = tuple(f"mam_dl{ec}" for ec in range(16))

        dma("qscal", s2f[0][:, 0:256],
            bass.AP(self.hbm["L1_attn_A_log"], 0, [[16, 128], [2048, 16], [1, 16]]),
            w=("mam_s2f0",), label=f"{tg}_ldA", slow_ok=True)
        op("scalar", lambda e: e.activation(s2f[1][:, 0:256], s2f[0][:, 0:256],
                                            AF.Exp),
           r=("mam_s2f0",), w=("mam_s2f1",), label=f"{tg}_expA")
        op("vector", lambda e: e.tensor_scalar_mul(A_sb[:], s2f[1][:, 0:256], -1.0),
           r=("mam_s2f1",), w=("mam_A",), label=f"{tg}_negA")
        dma("qscal", Dcol[:],
            bass.AP(self.hbm["L1_attn_D"], 0, [[1, 128], [128, 16]]),
            w=("mam_D",), label=f"{tg}_ldD", slow_ok=True)
        dma("qscal", cw2[:],
            bass.AP(self.hbm["L1_attn_conv_w"], 0, [[4, 128], [512, 16], [1, 4]]),
            w=("mam_cw",), label=f"{tg}_ldcw", slow_ok=True)
        dma("qscal", cb2[:],
            bass.AP(self.hbm["L1_attn_conv_b"], 0, [[1, 128], [128, 16]]),
            w=("mam_cb",), label=f"{tg}_ldcb", slow_ok=True)
        op("gpsimd", lambda e: e.memset(carr[:], 0.0), w=("mam_carr",),
           label=f"{tg}_carr0")
        op("gpsimd", lambda e: e.memset(halocp[:], 0.0), w=("mam_halo",),
           label=f"{tg}_halo0")
        for i in range(2):
            op("gpsimd", lambda e, i=i: e.memset(
                   bass.AP(ab[i], 0, [[W, 128], [65, 16], [1, 1]]), 0.0),
               w=(f"mam_ab{i}",), label=f"{tg}_abz{i}")

        for tc in range(4):
            t0 = tc * TC
            for jc in range(16):
                b = jc % 2
                dma("qgp", wbig[:, b * 1024:(b + 1) * 1024],
                    bass.AP(self.winT, jc * 128 * 1024, [[1024, 128], [1, 1024]]),
                    w=wq(b * 1024, (b + 1) * 1024), label=f"{tg}_wi{tc}_{jc}")
                bank = jc % 6
                def mmxz(e, jc=jc, b=b, bank=bank, t0=t0):
                    ins = None
                    for dc in range(8):
                        ins = e.matmul(
                            self.PS[:, bank * 512:(bank + 1) * 512],
                            wbig[:, b * 1024 + dc * 128: b * 1024 + (dc + 1) * 128],
                            self.u[:, dc * T + t0: dc * T + t0 + TC],
                            start=(dc == 0), stop=(dc == 7))
                    return ins
                op("tensor", mmxz,
                   r=wq(b * 1024, (b + 1) * 1024) + self.u_tags(),
                   w=(f"ps{bank}",), label=f"{tg}_xz{tc}_{jc}")
                op("scalar", lambda e, jc=jc, bank=bank: e.copy(
                       bass.AP(xy, jc * 515 + 3, [[8240, 128], [1, 512]]),
                       self.PS[:, bank * 512:(bank + 1) * 512]),
                   r=(f"ps{bank}",), w=("mam_xy",), label=f"{tg}_xe{tc}_{jc}")
            op("vector", lambda e: e.tensor_copy(
                   bass.AP(xy, 0, [[8240, 128], [515, 16], [1, 3]]),
                   bass.AP(halocp, 0, [[48, 128], [3, 16], [1, 3]])),
               r=("mam_halo",), w=("mam_xy",), label=f"{tg}_hr{tc}")
            for ec in range(16):
                base = ec * 515
                cb_ = ec % 2
                cv = s2f[cb_]
                ctag = f"mam_s2f{cb_}"
                op("vector", lambda e, base=base, cv=cv, ec=ec: e.tensor_scalar_mul(
                       cv[:], bass.AP(xy, base, [[8240, 128], [1, 512]]),
                       cw2[:, ec * 4:ec * 4 + 1]),
                   r=("mam_xy", "mam_cw"), w=(ctag,), label=f"{tg}_cv{tc}_{ec}_0")
                for k in range(1, 4):
                    op("vector", lambda e, base=base, cv=cv, ec=ec, k=k:
                       e.scalar_tensor_tensor(
                           cv[:], bass.AP(xy, base + k, [[8240, 128], [1, 512]]),
                           cw2[:, ec * 4 + k:ec * 4 + k + 1], cv[:],
                           ALU.mult, ALU.add),
                       r=("mam_xy", "mam_cw", ctag), w=(ctag,),
                       label=f"{tg}_cv{tc}_{ec}_{k}")
                op("scalar", lambda e, cv=cv, ec=ec: e.activation(
                       hconv[:, ec * 512:(ec + 1) * 512], cv[:], AF.Silu,
                       bias=cb2[:, ec:ec + 1]),
                   r=(ctag, "mam_cb"), w=(f"mam_hc{ec}",),
                   label=f"{tg}_si{tc}_{ec}")
            op("vector", lambda e: e.tensor_copy(
                   bass.AP(halocp, 0, [[48, 128], [3, 16], [1, 3]]),
                   bass.AP(xy, 512, [[8240, 128], [515, 16], [1, 3]])),
               r=("mam_xy",), w=("mam_halo",), label=f"{tg}_hw{tc}")
            dma("qsync", wbig[:, 2048:2560],
                bass.AP(self.wxT, 16 * 128 * 2048, [[2048, 128], [1, 512]]),
                w=wq(2048, 2560), label=f"{tg}_wbc{tc}")
            def mmB(e, col0=0, pcol=0):
                ins = None
                for ec in range(16):
                    ins = e.matmul(
                        self.PS[0:16, pcol:pcol + 512],
                        wbig[:, 2048 + ec * 32 + col0: 2048 + ec * 32 + col0 + 16],
                        hconv[:, ec * 512:(ec + 1) * 512],
                        start=(ec == 0), stop=(ec == 15))
                return ins
            op("tensor", lambda e: mmB(e, 0, 0), r=wq(2048, 2560) + hcs,
               w=("ps0",), label=f"{tg}_mB{tc}")
            op("tensor", lambda e: mmB(e, 16, 512), r=wq(2048, 2560) + hcs,
               w=("ps1",), label=f"{tg}_mC{tc}")
            op("vector", lambda e: e.tensor_copy(BCsb[0:16, :], self.PS[0:16, 0:512]),
               r=("ps0",), w=("mam_BC",), label=f"{tg}_eB{tc}")
            op("vector", lambda e: e.tensor_copy(BCsb[32:48, :], self.PS[0:16, 512:1024]),
               r=("ps1",), w=("mam_BC",), label=f"{tg}_eC{tc}")
            for fc in range(16):
                b = fc % 2
                dma("qsync", wbig[:, b * 2048:(b + 1) * 2048],
                    bass.AP(self.wxT, fc * 128 * 2048, [[2048, 128], [1, 2048]]),
                    w=wq(b * 2048, (b + 1) * 2048), label=f"{tg}_wx{tc}_{fc}")
                bank = 2 + fc % 4
                def mmd(e, b=b, bank=bank):
                    ins = None
                    for ec in range(16):
                        ins = e.matmul(
                            self.PS[:, bank * 512:(bank + 1) * 512],
                            wbig[:, b * 2048 + ec * 128: b * 2048 + (ec + 1) * 128],
                            hconv[:, ec * 512:(ec + 1) * 512],
                            start=(ec == 0), stop=(ec == 15))
                    return ins
                op("tensor", mmd, r=wq(b * 2048, (b + 1) * 2048) + hcs,
                   w=(f"ps{bank}",), label=f"{tg}_dbc{tc}_{fc}")
                s2 = s2f[b]
                stag = f"mam_s2f{b}"
                pb = self.PS[:, bank * 512:(bank + 1) * 512]
                op("vector", lambda e, s2=s2, pb=pb: e.tensor_scalar_mul(
                       s2[:], pb, -1.0),
                   r=(f"ps{bank}",), w=(stag,), label=f"{tg}_sp0{tc}_{fc}")
                op("vector", lambda e, s2=s2, pb=pb: e.tensor_tensor(
                       s2[:], s2[:], pb, ALU.min),
                   r=(f"ps{bank}", stag), w=(stag,), label=f"{tg}_sp1{tc}_{fc}")
                op("scalar", lambda e, s2=s2: e.activation(s2[:], s2[:], AF.Exp),
                   r=(stag,), w=(stag,), label=f"{tg}_sp2{tc}_{fc}")
                op("scalar", lambda e, s2=s2: e.activation(s2[:], s2[:], AF.Ln,
                                                           bias=one),
                   r=(stag, "constbuf"), w=(stag,), label=f"{tg}_sp3{tc}_{fc}")
                op("vector", lambda e, s2=s2, pb=pb, fc=fc: e.scalar_tensor_tensor(
                       dl[:, fc * 512:(fc + 1) * 512], pb, 0.0, s2[:],
                       ALU.max, ALU.add),
                   r=(f"ps{bank}", stag), w=(f"mam_dl{fc}",),
                   label=f"{tg}_sp4{tc}_{fc}")
            for s in range(8):
                sb2 = s % 2
                op("vector", lambda e, s=s, sb2=sb2: e.tensor_tensor(
                       du[sb2][:],
                       bass.AP(dl, s * L, [[8192, 128], [512, 16], [1, L]]),
                       bass.AP(hconv, s * L, [[8192, 128], [512, 16], [1, L]]),
                       ALU.mult),
                   r=dls + hcs, w=("mam_du",), label=f"{tg}_du{tc}_{s}")
                def mmbc(e, s=s, prow=0, pcol=1024):
                    ins = None
                    for n in range(16):
                        ins = e.matmul(
                            self.PS[:, pcol + n * 64: pcol + (n + 1) * 64],
                            sel[prow:prow + 16, n * 128:(n + 1) * 128],
                            BCsb[prow:prow + 16, s * L:(s + 1) * L],
                            start=True, stop=True)
                    return ins
                op("tensor", lambda e, s=s: mmbc(e, s, 0, 1024),
                   r=("mam_sel", "mam_BC"), w=("ps2", "ps3"),
                   label=f"{tg}_bb{tc}_{s}")
                op("scalar", lambda e, sb2=sb2: e.copy(BbS[sb2][:],
                                                       self.PS[:, 1024:2048]),
                   r=("ps2", "ps3"), w=(f"mam_Bb{sb2}",), label=f"{tg}_be{tc}_{s}")
                op("tensor", lambda e, s=s: mmbc(e, s, 32, 2048),
                   r=("mam_sel", "mam_BC"), w=("ps4", "ps5"),
                   label=f"{tg}_cb{tc}_{s}")
                op("scalar", lambda e, sb2=sb2: e.copy(CbS[sb2][:],
                                                       self.PS[:, 2048:3072]),
                   r=("ps4", "ps5"), w=(f"mam_Cb{sb2}",), label=f"{tg}_ce{tc}_{s}")
                for ec in range(16):
                    ui = (s * 16 + ec) % 2
                    abt, but, hst, Gt = ab[ui], bu[ui], hs[ui], G[ui]
                    ta, tb, th, tG = (f"mam_ab{ui}", f"mam_bu{ui}",
                                      f"mam_hs{ui}", f"mam_G{ui}")
                    op("gpsimd", lambda e, abt=abt, ec=ec, s=s: e.tensor_tensor(
                           bass.AP(abt, 1, [[W, 128], [65, 16], [1, L]]),
                           bass.AP(dl, ec * 512 + s * L,
                                   [[8192, 128], [0, 16], [1, L]]),
                           bass.AP(A_sb, ec * 16, [[256, 128], [1, 16], [0, L]]),
                           ALU.mult),
                       r=(f"mam_dl{ec}", "mam_A"), w=(ta,),
                       label=f"{tg}_ap{tc}_{s}_{ec}")
                    op("scalar", lambda e, abt=abt: e.activation(
                           bass.AP(abt, 1, [[W, 128], [65, 16], [1, L]]),
                           bass.AP(abt, 1, [[W, 128], [65, 16], [1, L]]),
                           AF.Exp),
                       r=(ta,), w=(ta,), label=f"{tg}_ex{tc}_{s}_{ec}")
                    op("vector", lambda e, but=but, ec=ec, sb2=sb2: e.tensor_tensor(
                           bass.AP(but, 1, [[W, 128], [65, 16], [1, L]]),
                           bass.AP(du[sb2], ec * L, [[1024, 128], [0, 16], [1, L]]),
                           bass.AP(BbS[sb2], 0, [[1024, 128], [64, 16], [1, L]]),
                           ALU.mult),
                       r=("mam_du", f"mam_Bb{sb2}"), w=(tb,),
                       label=f"{tg}_bu{tc}_{s}_{ec}")
                    op("gpsimd", lambda e, but=but, ec=ec: e.tensor_copy(
                           bass.AP(but, 0, [[W, 128], [65, 16], [1, 1]]),
                           bass.AP(carr, ec * 16, [[256, 128], [1, 16], [1, 1]])),
                       r=("mam_carr",), w=(tb,), label=f"{tg}_ci{tc}_{s}_{ec}")
                    op("vector", lambda e, hst=hst, abt=abt, but=but:
                       e.tensor_tensor_scan(hst[:], abt[:], but[:], 0.0,
                                            ALU.mult, ALU.add),
                       r=(ta, tb), w=(th,), label=f"{tg}_sc{tc}_{s}_{ec}")
                    op("gpsimd", lambda e, hst=hst, ec=ec: e.tensor_copy(
                           bass.AP(carr, ec * 16, [[256, 128], [1, 16], [1, 1]]),
                           bass.AP(hst, 64, [[W, 128], [65, 16], [1, 1]])),
                       r=(th,), w=("mam_carr",), label=f"{tg}_co{tc}_{s}_{ec}")
                    op("vector", lambda e, Gt=Gt, hst=hst, sb2=sb2: e.tensor_tensor(
                           Gt[:],
                           bass.AP(hst, 1, [[W, 128], [65, 16], [1, L]]),
                           bass.AP(CbS[sb2], 0, [[1024, 128], [64, 16], [1, L]]),
                           ALU.mult),
                       r=(th, f"mam_Cb{sb2}"), w=(tG,),
                       label=f"{tg}_G{tc}_{s}_{ec}")
                    op("gpsimd", lambda e, Gt=Gt, but=but: e.tensor_add(
                           but[:, 0:512], Gt[:, 0:512], Gt[:, 512:1024]),
                       r=(tG, tb), w=(tb,), label=f"{tg}_f1{tc}_{s}_{ec}")
                    op("gpsimd", lambda e, but=but: e.tensor_add(
                           but[:, 512:768], but[:, 0:256], but[:, 256:512]),
                       r=(tb,), w=(tb,), label=f"{tg}_f2{tc}_{s}_{ec}")
                    op("vector", lambda e, but=but: e.tensor_add(
                           but[:, 768:896], but[:, 512:640], but[:, 640:768]),
                       r=(tb,), w=(tb,), label=f"{tg}_f3{tc}_{s}_{ec}")
                    op("vector", lambda e, but=but, ec=ec, s=s: e.tensor_add(
                           bass.AP(xy, ec * 512 + s * L, [[8240, 128], [1, L]]),
                           but[:, 768:832], but[:, 832:896]),
                       r=(tb,), w=("mam_xy",), label=f"{tg}_f4{tc}_{s}_{ec}")
            for ec in range(16):
                op("vector", lambda e, ec=ec: e.scalar_tensor_tensor(
                       bass.AP(xy, ec * 512, [[8240, 128], [1, 512]]),
                       hconv[:, ec * 512:(ec + 1) * 512],
                       Dcol[:, ec:ec + 1],
                       bass.AP(xy, ec * 512, [[8240, 128], [1, 512]]),
                       ALU.mult, ALU.add),
                   r=(f"mam_hc{ec}", "mam_D", "mam_xy"), w=("mam_xy",),
                   label=f"{tg}_dk{tc}_{ec}")
            for jc2 in range(16):
                b = jc2 % 2
                dma("qgp", wbig[:, b * 1024:(b + 1) * 1024],
                    bass.AP(self.winT, (16 + jc2) * 128 * 1024,
                            [[1024, 128], [1, 1024]]),
                    w=wq(b * 1024, (b + 1) * 1024), label=f"{tg}_wz{tc}_{jc2}")
                bank = jc2 % 4
                def mmz(e, jc2=jc2, b=b, bank=bank, t0=t0):
                    ins = None
                    for dc in range(8):
                        ins = e.matmul(
                            self.PS[:, bank * 512:(bank + 1) * 512],
                            wbig[:, b * 1024 + dc * 128: b * 1024 + (dc + 1) * 128],
                            self.u[:, dc * T + t0: dc * T + t0 + TC],
                            start=(dc == 0), stop=(dc == 7))
                    return ins
                op("tensor", mmz,
                   r=wq(b * 1024, (b + 1) * 1024) + self.u_tags(),
                   w=(f"ps{bank}",), label=f"{tg}_z{tc}_{jc2}")
                op("scalar", lambda e, b=b, bank=bank: e.activation(
                       ztmp[b][:], self.PS[:, bank * 512:(bank + 1) * 512],
                       AF.Silu),
                   r=(f"ps{bank}",), w=("mam_zt",), label=f"{tg}_ze{tc}_{jc2}")
                op("vector", lambda e, b=b, jc2=jc2: e.tensor_mul(
                       bass.AP(xy, jc2 * 512, [[8240, 128], [1, 512]]),
                       bass.AP(xy, jc2 * 512, [[8240, 128], [1, 512]]),
                       ztmp[b][:]),
                   r=("mam_zt", "mam_xy"), w=("mam_xy",),
                   label=f"{tg}_gt{tc}_{jc2}")
            for oc in range(8):
                b = oc % 2
                dma("qscal", wbig[:, b * 2048:(b + 1) * 2048],
                    bass.AP(self.woT, oc * 128 * 2048, [[2048, 128], [1, 2048]]),
                    w=wq(b * 2048, (b + 1) * 2048), label=f"{tg}_wo{tc}_{oc}")
                def mmo(e, b=b, oc=oc):
                    ins = None
                    for ec in range(16):
                        ins = e.matmul(
                            self.PS[:, oc * 512:(oc + 1) * 512],
                            wbig[:, b * 2048 + ec * 128: b * 2048 + (ec + 1) * 128],
                            bass.AP(xy, ec * 512, [[8240, 128], [1, 512]]),
                            start=(ec == 0), stop=(ec == 15))
                    return ins
                op("tensor", mmo,
                   r=wq(b * 2048, (b + 1) * 2048) + ("mam_xy",),
                   w=(f"ps{oc}",), label=f"{tg}_wm{tc}_{oc}")
                xs_t = self.x_tags(oc, range(tc * 4, tc * 4 + 4))
                op("vector", lambda e, oc=oc, t0=t0: e.tensor_add(
                       self.x[:, oc * T + t0: oc * T + t0 + TC],
                       self.x[:, oc * T + t0: oc * T + t0 + TC],
                       self.PS[:, oc * 512:(oc + 1) * 512]),
                   r=(f"ps{oc}",) + xs_t, w=xs_t, label=f"{tg}_xr{tc}_{oc}")
        st.close()
        self.sched.barrier()

    def store_out(self):
        stio = ExitStack()
        stage = [self.sb([128, D], F32, f"ostage{i}", stack=stio) for i in range(2)]
        for tt in range(16):
            s = stage[tt % 2]
            stag = f"iostage{tt % 2}"
            for dc in range(8):
                pb = self.pool[dc % 4]
                ptag = f"ps{dc % 4}"
                self.op("tensor",
                        lambda e, dc=dc, tt=tt, pb=pb: e.transpose(
                            pb[:, 0:128],
                            self.x[:, dc * T + tt * 128: dc * T + (tt + 1) * 128],
                            self.ident_f[:]),
                        r=(f"x:{dc}:{tt}", "ident_f"), w=(ptag,),
                        label=f"oT{tt}_{dc}")
                self.op("vector",
                        lambda e, dc=dc, pb=pb, s=s: e.tensor_copy(
                            s[:, dc * 128:(dc + 1) * 128], pb[:, 0:128]),
                        r=(ptag,), w=(stag,), label=f"oEv{tt}_{dc}")
            self.dma("qsync",
                     bass.AP(self.out_hbm, tt * 128 * D, [[D, 128], [1, D]]),
                     s[:],
                     r=(stag,), w=(f"osent{tt}",), label=f"ostore{tt}")
        stio.close()



INPUT_META = [
    ("L0_attn_x", (T, D), F32),
    ("L0_attn_W_kv_down", (1024, 128), F32),
    ("L0_attn_W_k_up", (128, 1024), F32),
    ("L0_attn_W_v_up", (128, 1024), F32),
    ("L0_attn_W_q", (1024, 1024), F32),
    ("L0_attn_W_q_rope", (1024, 32), F32),
    ("L0_attn_W_k_rope", (128, 32), F32),
    ("L0_attn_W_o", (1024, 1024), F32),
    ("L0_ffn_w1", (4096, 1024), F32),
    ("L0_ffn_conv_w", (4096, 1, 3), F32),
    ("L0_ffn_conv_b", (4096,), F32),
    ("L0_ffn_w2", (1024, 4096), F32),
    ("L1_attn_W_in", (4096, 1024), F32),
    ("L1_attn_conv_w", (2048, 1, 4), F32),
    ("L1_attn_conv_b", (2048,), F32),
    ("L1_attn_W_x", (2080, 2048), F32),
    ("L1_attn_A_log", (2048, 16), F32),
    ("L1_attn_D", (2048,), F32),
    ("L1_attn_W_out", (1024, 2048), F32),
    ("L1_ffn_rail_maker", (4, 64, 16, 16), F32),
    ("L1_ffn_rail_up_proj", (4, 64, 64, 16), F32),
    ("L1_ffn_w_cproj", (1024, 16384), F32),
]


def build_nc(n_sub=6):
    nc = bass.Bass("TRN2", target_bir_lowering=False)
    es = ExitStack()
    with es:
        b = Builder(nc, es, n_sub=n_sub)
        b.build(INPUT_META)
    return nc


def kernel(**inputs):
    n_sub = int(inputs.pop("_n_sub", 6))
    nc = build_nc(n_sub=n_sub)
    x = np.ascontiguousarray(inputs["L0_attn_x"])
    in_maps = []
    for c in range(NCORE):
        m = {k: np.ascontiguousarray(v) for k, v in inputs.items()}
        m["L0_attn_x"] = np.ascontiguousarray(x[c])
        in_maps.append(m)
    res = run_bass_kernel_spmd(nc, in_maps, core_ids=list(range(NCORE)))
    out = np.stack([r["out"] for r in res.results], axis=0)
    return out


# version 03
# speedup vs baseline: 1.1511x; 1.0316x over previous
import math
from contextlib import ExitStack

import numpy as np
import concourse.bass as bass
import concourse.mybir as mybir
from concourse.bass_utils import run_bass_kernel_spmd

F32 = mybir.dt.float32
BF16 = mybir.dt.bfloat16
I32 = mybir.dt.int32
AF = mybir.ActivationFunctionType
ALU = mybir.AluOpType
AX = mybir.AxisListType

T = 2048
D = 1024
NCORE = 8


COMPUTE_ENGINES = ("tensor", "vector", "scalar", "gpsimd", "sync")
DMA_QUEUES = ("qsync", "qgp", "qscal")
QUEUE_ENGINE = {"qsync": "sync", "qgp": "gpsimd", "qscal": "scalar"}
ALL_STREAMS = COMPUTE_ENGINES + DMA_QUEUES


class Op:
    __slots__ = ("stream", "fn", "reads", "writes", "idx", "label",
                 "sem", "inc_to", "waits", "pre_wait")

    def __init__(self, stream, fn, reads, writes, idx, label=""):
        self.stream = stream
        self.fn = fn
        self.reads = reads
        self.writes = writes
        self.idx = idx
        self.label = label
        self.sem = None
        self.inc_to = 0
        self.waits = {}
        self.pre_wait = None


N_QSEM = 8


class Sched:

    def __init__(self, nc):
        self.nc = nc
        self.ops = []
        self.last_writer = {}
        self.readers = {}
        self.eng_seq = {s: 0 for s in COMPUTE_ENGINES}
        self.q_idx = {q: 0 for q in DMA_QUEUES}
        self.sem_uses = {}
        self.bar = {}

    def barrier(self):
        snap = {}
        for op in self.ops:
            snap[op.sem] = max(snap.get(op.sem, 0), op.inc_to)
        self.bar = snap

    def add(self, stream, fn, r=(), w=(), label=""):
        op = Op(stream, fn, tuple(r), tuple(w), len(self.ops), label)
        if stream in DMA_QUEUES:
            k = self.q_idx[stream] % N_QSEM
            self.q_idx[stream] += 1
            sem = f"{stream}_{k}"
            uses = self.sem_uses.get(sem, 0)
            if uses > 0:
                op.pre_wait = (sem, uses * 16)
            self.sem_uses[sem] = uses + 1
            op.sem = sem
            op.inc_to = (uses + 1) * 16
        else:
            self.eng_seq[stream] += 1
            op.sem = f"e_{stream}"
            op.inc_to = self.eng_seq[stream]
        deps = []
        for t in op.reads:
            lw = self.last_writer.get(t)
            if lw is not None:
                deps.append(lw)
        for t in op.writes:
            lw = self.last_writer.get(t)
            if lw is not None:
                deps.append(lw)
            deps.extend(self.readers.get(t, ()))
        for d in deps:
            if d is op:
                continue
            cur = op.waits.get(d.sem, 0)
            op.waits[d.sem] = max(cur, d.inc_to)
        for sname, v in self.bar.items():
            if op.waits.get(sname, 0) < v:
                op.waits[sname] = v
        for t in op.reads:
            self.readers.setdefault(t, []).append(op)
        for t in op.writes:
            self.last_writer[t] = op
            self.readers[t] = []
        self.ops.append(op)
        return op

    def finalize_and_emit(self, es):
        nc = self.nc
        sem_names = set()
        for op in self.ops:
            sem_names.add(op.sem)
            sem_names.update(op.waits.keys())
            if op.pre_wait:
                sem_names.add(op.pre_wait[0])
        sems = {n: es.enter_context(nc.semaphore(n)) for n in sorted(sem_names)}
        per_stream = {s: [] for s in ALL_STREAMS}
        for op in self.ops:
            per_stream[op.stream].append(op)

        hw_map = {}
        for stream in ALL_STREAMS:
            if per_stream[stream]:
                hw = QUEUE_ENGINE.get(stream, stream)
                hw_map.setdefault(hw, []).append(stream)

        def make_runner(streams):
            ops = sorted([o for s in streams for o in per_stream[s]],
                         key=lambda o: o.idx)

            def runner(eng):
                waited = {}
                for op in ops:
                    for sname, v in sorted(op.waits.items()):
                        if waited.get(sname, 0) < v:
                            eng.wait_ge(sems[sname], v)
                            waited[sname] = v
                    if op.pre_wait is not None:
                        sname, v = op.pre_wait
                        if waited.get(sname, 0) < v:
                            eng.wait_ge(sems[sname], v)
                            waited[sname] = v
                    ins = op.fn(eng)
                    assert ins is not None,                         f"op {op.label} on {op.stream} returned no instruction"
                    ins.then_inc(sems[op.sem],
                                 16 if op.stream in DMA_QUEUES else 1)
            return runner

        with nc.Block() as blk:
            for hw, streams in hw_map.items():
                getattr(blk, hw)(make_runner(streams))

        print(f"[sched] ops={len(self.ops)} sems={len(sem_names)} "
              f"per-stream={ {k: len(v) for k, v in per_stream.items() if v} }")



class Builder:
    def __init__(self, nc, es, n_sub=6):
        self.nc = nc
        self.es = es
        self.sched = Sched(nc)
        self.n_sub = n_sub
        self.name_ctr = 0

    def sb(self, shape, dtype, name=None, stack=None):
        self.name_ctr += 1
        nm = f"{name}_{self.name_ctr}" if name else f"t{self.name_ctr}"
        import concourse.mybir as _mb
        nbytes = int(np.prod(shape[1:])) * _mb.dt.size(dtype) if len(shape) > 1 else _mb.dt.size(dtype)
        self.alloc_log = getattr(self, "alloc_log", [])
        self.alloc_log.append((nm, nbytes))
        return (stack or self.es).enter_context(
            self.nc.sbuf_tensor(nm, list(shape), dtype))

    def op(self, stream, fn, r=(), w=(), label=""):
        return self.sched.add(stream, fn, r, w, label)

    def dma(self, queue, out_ap, in_ap, r=(), w=(), label="dma", transpose=False,
            slow_ok=False):
        if transpose:
            fn = lambda e: e.dma_start_transpose(out_ap, in_ap)
        else:
            fn = lambda e: e.dma_start(out=out_ap, in_=in_ap,
                                       allow_slow_non_contiguous=slow_ok)
        return self.sched.add(queue, fn, r, w, label)

    def build(self, inputs_meta):
        nc = self.nc
        self.hbm = {}
        for name, shape, dt in inputs_meta:
            self.hbm[name] = nc.dram_tensor(name, list(shape), dt,
                                            kind="ExternalInput")
        self.out_hbm = nc.dram_tensor("out", [T, D], F32,
                                      kind="ExternalOutput")

        self.x = self.sb([128, 8 * T], F32, "x_res")
        self.u = self.sb([128, 8 * T], BF16, "u_ln")
        self.ones_col_b = self.sb([128, 1], BF16, "ones_col_b")
        self.ones_row_f = self.sb([1, 128], F32, "ones_row_f")
        self.ident_b = self.sb([128, 128], BF16, "ident_b")
        self.ident_f = self.sb([128, 128], F32, "ident_f")
        self.halfkk = self.sb([128, 1], F32, "halfkk")
        self.tiny = self.sb([128, 8], F32, "tinybuf")
        self.rows = self.sb([97, 2 * T], F32, "rows")
        self.PS = self.es.enter_context(
            self.nc.psum_tensor("PS", [128, 4096], F32))
        self.pool = [self.PS[:, 512 * b:512 * (b + 1)] for b in range(8)]
        self.w1qT = self.nc.dram_tensor("w1qT_scr", [32 * 128, 1024], BF16)
        self.wcT = self.nc.dram_tensor("wcT_scr", [128 * 128, 1024], BF16)
        self.CsT_hbm = self.nc.dram_tensor("CsT_scr", [128, 32 * 256], BF16)
        self.w2qT = self.nc.dram_tensor("w2qT_scr", [32 * 128, 1024], BF16)
        self.h_scr = self.nc.dram_tensor("h_scr", [4096, T], BF16)
        self.winT = self.nc.dram_tensor("winT_scr", [32 * 128, 1024], BF16)
        self.wxT = self.nc.dram_tensor("wxT_scr", [17 * 128, 2048], BF16)
        self.woT = self.nc.dram_tensor("woT_scr", [8 * 128, 2048], BF16)

        self.constbuf = self.sb([128, 16], F32, "constbuf")
        self.n_const = 0
        self.setup_consts()
        self.load_x()
        if self.n_sub >= 1:
            self.prep_l0_ffn_weights()
        if self.n_sub >= 4:
            self.prep_l1_ffn_weights()
        if self.n_sub >= 5:
            self.prep_l1_attn_weights()

        steps = [
            lambda: (self.layernorm("ln0"), self.l0_ffn(0)),
            lambda: (self.layernorm("ln1"), self.l0_attn()),
            lambda: (self.layernorm("ln2"), self.l0_ffn(1)),
            lambda: (self.layernorm("ln3"), self.l1_ffn(0)),
            lambda: (self.layernorm("ln4"), self.l1_attn()),
            lambda: (self.layernorm("ln5"), self.l1_ffn(1)),
        ]
        for i in range(min(self.n_sub, 6)):
            steps[i]()

        self.store_out()
        fin_tags = tuple(f"osent{tt}" for tt in range(16))
        self.op("sync", lambda e: e.nop(), r=fin_tags, label="final")
        self.sched.finalize_and_emit(self.es)

    def setup_consts(self):
        op = self.op
        op("vector", lambda e: e.memset(self.ones_col_b[:], 1.0),
           w=("ones_col_b",), label="ones_col")
        op("vector", lambda e: e.memset(self.ones_row_f[:], 1.0),
           w=("ones_row_f",), label="ones_row")
        stC = ExitStack()
        it_col = self.sb([128, 128], F32, "it_col", stack=stC)
        it_row = self.sb([128, 128], F32, "it_row", stack=stC)
        op("gpsimd", lambda e: e.iota(it_col[:], [[1, 128]], channel_multiplier=0, allow_small_or_imprecise_dtypes=True),
           w=("it_col",), label="iota_col")
        op("gpsimd", lambda e: e.iota(it_row[:], [[0, 128]], channel_multiplier=1, allow_small_or_imprecise_dtypes=True),
           w=("it_row",), label="iota_row")
        op("vector", lambda e: e.tensor_tensor(self.ident_f[:], it_col[:],
                                               it_row[:], ALU.is_equal),
           r=("it_col", "it_row"), w=("ident_f",), label="ident_f")
        op("vector", lambda e: e.tensor_copy(self.ident_b[:], self.ident_f[:]),
           r=("ident_f",), w=("ident_b",), label="ident_b")
        can_i = self.sb([128, 1], I32, "can_i", stack=stC)
        op("vector", lambda e: e.memset(self.tiny[:, 0:1], 0.75), w=("tiny0",))
        op("vector", lambda e: e.tensor_copy(can_i[:], self.tiny[:, 0:1]),
           r=("tiny0",), w=("can_i",), label="canary_cvt")
        op("vector", lambda e: e.tensor_scalar(self.halfkk[:], can_i[:], -0.5, 0.5,
                                               ALU.mult, ALU.add),
           r=("can_i",), w=("halfkk",), label="halfkk")
        self.sched.barrier()
        stC.close()

    def load_x(self):
        xin = self.hbm["L0_attn_x"]
        stio = ExitStack()
        stage = [self.sb([128, D], F32, f"iostage{i}", stack=stio) for i in range(2)]
        for tt in range(16):
            s = stage[tt % 2]
            stag = f"iostage{tt % 2}"
            self.dma("qsync", s[:],
                     bass.AP(xin, tt * 128 * D, [[D, 128], [1, D]]),
                     r=(), w=(stag,), label=f"xload{tt}")
            for dc in range(8):
                pb = self.pool[4 + dc % 4]
                ptag = f"ps{4 + dc % 4}"
                self.op("tensor",
                        lambda e, s=s, dc=dc, pb=pb: e.transpose(
                            pb[:, 0:128], s[:, dc * 128:(dc + 1) * 128],
                            self.ident_f[:]),
                        r=(stag, "ident_f"), w=(ptag,), label=f"xT{tt}_{dc}")
                self.op("scalar",
                        lambda e, pb=pb, dc=dc, tt=tt: e.copy(
                            self.x[:, dc * T + tt * 128: dc * T + (tt + 1) * 128],
                            pb[:, 0:128]),
                        r=(ptag,), w=(f"x:{dc}:{tt}",), label=f"xEv{tt}_{dc}")
        stio.close()
        self.sched.barrier()

    def x_tags(self, dc=None, tts=None):
        dcs = range(8) if dc is None else [dc]
        tl = range(16) if tts is None else tts
        return tuple(f"x:{d}:{t}" for d in dcs for t in tl)

    def u_tags(self, dc=None):
        dcs = range(8) if dc is None else [dc]
        return tuple(f"u:{d}" for d in dcs)

    def constf(self, val):
        key = float(val)
        self.cvals = getattr(self, "cvals", {})
        if key in self.cvals:
            return key
        idx = self.n_const
        self.n_const += 1
        assert idx < 16
        ap = self.constbuf[:, idx:idx + 1]
        self.op("vector", lambda e, ap=ap, key=key: e.memset(ap, key),
                w=("constbuf",), label=f"const_{idx}")
        self.nc.const_aps.register(F32, key, ap)
        self.cvals[key] = True
        return key

    def row(self, p, seg=0, lo=0, hi=T):
        return self.rows[p:p + 1, seg * T + lo: seg * T + hi]

    def layernorm(self, tag):
        op = self.op
        stL = ExitStack()
        self.mrs_sb = self.sb([128, T], F32, "mrs_sb", stack=stL)
        self.ln_tmp = self.sb([128, T], F32, "ln_tmp", stack=stL)
        op("vector", lambda e: e.tensor_copy(self.u[:], self.x[:]),
           r=self.x_tags(), w=self.u_tags(), label=f"{tag}_shadow")
        for j in range(4):
            def mmsx(e, j=j):
                ins = None
                for dc in range(8):
                    ins = e.matmul(
                        self.PS[0:1, j * 512:(j + 1) * 512],
                        self.ones_col_b[:],
                        self.u[:, dc * T + j * 512: dc * T + (j + 1) * 512],
                        start=(dc == 0), stop=(dc == 7))
                return ins
            op("tensor", mmsx, r=self.u_tags() + ("ones_col_b",),
               w=(f"ps{j}",), label=f"{tag}_sx{j}")
        op("vector", lambda e: e.tensor_copy(self.row(32, 0), self.PS[0:1, 0:2048]),
           r=("ps0", "ps1", "ps2", "ps3"), w=("r32_0",), label=f"{tag}_evsx")
        op("scalar", lambda e: e.activation(self.u[:], self.u[:], AF.Square),
           r=self.u_tags(), w=self.u_tags(), label=f"{tag}_sq")
        for j in range(4):
            def mmsq(e, j=j):
                ins = None
                for dc in range(8):
                    ins = e.matmul(
                        self.PS[0:1, 2048 + j * 512: 2048 + (j + 1) * 512],
                        self.ones_col_b[:],
                        self.u[:, dc * T + j * 512: dc * T + (j + 1) * 512],
                        start=(dc == 0), stop=(dc == 7))
                return ins
            op("tensor", mmsq, r=self.u_tags() + ("ones_col_b",),
               w=(f"ps{4 + j}",), label=f"{tag}_sq{j}")
        op("vector", lambda e: e.tensor_copy(self.row(32, 1), self.PS[0:1, 2048:4096]),
           r=("ps4", "ps5", "ps6", "ps7"), w=("r32_1",), label=f"{tag}_evsq")
        op("vector", lambda e: e.tensor_scalar_mul(self.row(64, 0), self.row(32, 0), 1.0 / D),
           r=("r32_0",), w=("r64_0",), label=f"{tag}_m")
        op("vector", lambda e: e.tensor_scalar_mul(self.row(32, 1), self.row(32, 1), 1.0 / D),
           r=("r32_1",), w=("r32_1",), label=f"{tag}_q")
        op("vector", lambda e: e.tensor_mul(self.row(32, 0), self.row(64, 0), self.row(64, 0)),
           r=("r64_0",), w=("r32_0",), label=f"{tag}_mm")
        op("vector", lambda e: e.tensor_sub(self.row(32, 1), self.row(32, 1), self.row(32, 0)),
           r=("r32_1", "r32_0"), w=("r32_1",), label=f"{tag}_var")
        op("vector", lambda e: e.tensor_scalar_add(self.row(32, 1), self.row(32, 1), 1e-5),
           r=("r32_1",), w=("r32_1",), label=f"{tag}_eps")
        op("vector", lambda e: e.reciprocal(self.row(32, 0), self.row(32, 1)),
           r=("r32_1",), w=("r32_0",), label=f"{tag}_recip")
        op("scalar", lambda e: e.activation(self.row(0, 0), self.row(32, 0), AF.Sqrt),
           r=("r32_0",), w=("r0_0",), label=f"{tag}_sqrt")
        op("vector", lambda e: e.tensor_mul(self.row(32, 0), self.row(0, 0), self.row(0, 0)),
           r=("r0_0",), w=("r32_0",), label=f"{tag}_n1")
        op("vector", lambda e: e.tensor_mul(self.row(32, 0), self.row(32, 0), self.row(32, 1)),
           r=("r32_0", "r32_1"), w=("r32_0",), label=f"{tag}_n2")
        op("vector", lambda e: e.tensor_scalar(self.row(0, 1), self.row(32, 0), -0.5, 1.5,
                                               ALU.mult, ALU.add),
           r=("r32_0",), w=("r0_1",), label=f"{tag}_n3")
        op("vector", lambda e: e.tensor_mul(self.row(0, 0), self.row(0, 0), self.row(0, 1)),
           r=("r0_0", "r0_1"), w=("r0_0",), label=f"{tag}_n4")
        op("vector", lambda e: e.tensor_copy(self.row(0, 1), self.row(64, 0)),
           r=("r64_0",), w=("r0_1",), label=f"{tag}_mcp")
        op("vector", lambda e: e.tensor_mul(self.row(0, 1), self.row(0, 0), self.row(0, 1)),
           r=("r0_0", "r0_1"), w=("r0_1",), label=f"{tag}_mrs")
        for j in range(4):
            bk = 1 + j
            op("tensor", lambda e, j=j, bk=bk: e.matmul(
                   self.PS[:, bk * 512:(bk + 1) * 512], self.ones_row_f[:],
                   self.row(0, 0, j * 512, (j + 1) * 512), start=True, stop=True),
               r=("r0_0", "ones_row_f"), w=(f"ps{bk}",), label=f"{tag}_bcr{j}")
        for j in range(4):
            bk = (5 + j) % 8
            op("tensor", lambda e, j=j, bk=bk: e.matmul(
                   self.PS[:, bk * 512:(bk + 1) * 512], self.ones_row_f[:],
                   self.row(0, 1, j * 512, (j + 1) * 512), start=True, stop=True),
               r=("r0_1", "ones_row_f"), w=(f"ps{bk}",), label=f"{tag}_bcm{j}")
            op("scalar", lambda e, j=j, bk=bk: e.copy(
                   self.mrs_sb[:, j * 512:(j + 1) * 512],
                   self.PS[:, bk * 512:(bk + 1) * 512]),
               r=(f"ps{bk}",), w=("mrs_sb",), label=f"{tag}_evm{j}")
        for dc in range(8):
            for j in range(4):
                sl = slice(dc * T + j * 512, dc * T + (j + 1) * 512)
                tsl = slice(j * 512, (j + 1) * 512)
                ttag = f"ln_tmp{j}"
                bk = 1 + j
                op("vector",
                   lambda e, sl=sl, tsl=tsl, bk=bk: e.tensor_mul(
                       self.ln_tmp[:, tsl], self.x[:, sl],
                       self.PS[:, bk * 512:(bk + 1) * 512]),
                   r=self.x_tags(dc) + (f"ps{bk}",), w=(ttag,),
                   label=f"{tag}_ap1_{dc}_{j}")
                op("vector",
                   lambda e, sl=sl, tsl=tsl: e.tensor_sub(
                       self.u[:, sl], self.ln_tmp[:, tsl], self.mrs_sb[:, tsl]),
                   r=(ttag, "mrs_sb"), w=self.u_tags(dc),
                   label=f"{tag}_ap2_{dc}_{j}")
        stL.close()
        self.sched.barrier()

    def prep_l0_ffn_weights(self):
        op, dma = self.op, self.dma
        self.cwcols = self.sb([128, 96], F32, "cwcols")
        self.cbcols = self.sb([128, 32], F32, "cbcols")
        st = ExitStack()
        wacc = self.sb([128, 2], F32, "wp_acc", stack=st)
        wred = self.sb([128, 1], F32, "wp_red", stack=st)
        swcol = self.sb([128, 2], F32, "wp_swcol", stack=st)
        wt = [self.sb([128, 4096], F32, f"wp_t{i}", stack=st) for i in range(2)]
        tern = [self.sb([128, 4096], BF16, f"wp_n{i}", stack=st) for i in range(2)]
        stg = [self.sb([128, 1024], BF16, f"wp_s{i}", stack=st) for i in range(2)]

        for wi, (hbm_name, nrow, ncol, scr, tag) in enumerate([
                ("L0_ffn_w1", 4096, 1024, self.w1qT, "w1"),
                ("L0_ffn_w2", 1024, 4096, self.w2qT, "w2")]):
            wh = self.hbm[hbm_name]
            grp = 4096 // ncol
            nload = (nrow // 128) // grp
            def ld_ap(li, grp=grp, ncol=ncol):
                return bass.AP(wh, li * grp * 128 * ncol,
                               [[ncol, 128], [128 * ncol, grp], [1, ncol]])
            op("vector", lambda e: e.memset(wacc[:, 0:1], 0.0),
               w=("wpacc",), label=f"{tag}_acc0")
            for li in range(nload):
                b = li % 2
                dma("qgp", wt[b][:], ld_ap(li), w=(f"wpt{b}",), label=f"{tag}_ld{li}")
                op("vector", lambda e, b=b: e.reduce_sum(
                       wred[:], wt[b][:], axis=AX.X, apply_absolute_value=True),
                   r=(f"wpt{b}",), w=("wpred",), label=f"{tag}_red{li}")
                op("vector", lambda e: e.tensor_add(wacc[:, 0:1], wacc[:, 0:1], wred[:]),
                   r=("wpred", "wpacc"), w=("wpacc",), label=f"{tag}_acc{li}")
            op("tensor", lambda e: e.transpose(self.PS[0:1, 0:128], wacc[:, 0:1],
                                               self.ident_f[:]),
               r=("wpacc", "ident_f"), w=("ps0",), label=f"{tag}_T")
            op("vector", lambda e: e.tensor_copy(self.row(0, 0, 0, 128), self.PS[0:1, 0:128]),
               r=("ps0",), w=("r0_0",), label=f"{tag}_evT")
            op("vector", lambda e: e.reduce_sum(self.row(0, 0, 0, 1), self.row(0, 0, 0, 128),
                                                axis=AX.X),
               r=("r0_0",), w=("r0_0",), label=f"{tag}_sum")
            N = float(nrow * ncol)
            op("vector", lambda e, N=N: e.tensor_scalar(
                   self.row(0, 0, 1, 2), self.row(0, 0, 0, 1), 1.0 / N, 1e-5,
                   ALU.mult, ALU.max),
               r=("r0_0",), w=("r0_0",), label=f"{tag}_c")
            op("vector", lambda e: e.reciprocal(self.row(0, 1, 0, 1), self.row(0, 0, 1, 2)),
               r=("r0_0",), w=("r0_1",), label=f"{tag}_s")
            op("vector", lambda e, wi=wi: e.tensor_copy(self.tiny[0:1, 1 + wi:2 + wi],
                                                        self.row(0, 0, 1, 2)),
               r=("r0_0",), w=(f"cw{wi}",), label=f"{tag}_csave")
            op("tensor", lambda e: e.matmul(self.PS[:, 512:513], self.ones_row_f[:],
                                            self.row(0, 1, 0, 1), start=True, stop=True),
               r=("r0_1", "ones_row_f"), w=("ps1",), label=f"{tag}_bcs")
            op("scalar", lambda e, wi=wi: e.copy(swcol[:, wi:wi + 1], self.PS[:, 512:513]),
               r=("ps1",), w=(f"swcol{wi}",), label=f"{tag}_evs")
            for li in range(nload):
                b = li % 2
                dma("qgp", wt[b][:], ld_ap(li), w=(f"wpt{b}",), label=f"{tag}_ld2_{li}")
                op("vector", lambda e, b=b, wi=wi: e.tensor_scalar_mul(
                       wt[b][:], wt[b][:], swcol[:, wi:wi + 1]),
                   r=(f"wpt{b}", f"swcol{wi}"), w=(f"wpt{b}",), label=f"{tag}_v{li}")
                op("vector", lambda e, b=b: e.tensor_single_scalar(
                       tern[b][:], wt[b][:], -0.5, ALU.is_le),
                   r=(f"wpt{b}",), w=(f"wptern{b}",), label=f"{tag}_c{li}")
                op("vector", lambda e, b=b: e.scalar_tensor_tensor(
                       tern[b][:], wt[b][:], 0.5, tern[b][:],
                       ALU.is_ge, ALU.subtract),
                   r=(f"wpt{b}", f"wptern{b}"), w=(f"wptern{b}",), label=f"{tag}_t{li}")
                for g0 in range(0, 32, 8):
                    bank = 2 + ((g0 // 8) % 2)
                    psb = self.PS[:, bank * 512:(bank + 1) * 512].bitcast(BF16)
                    for k in range(8):
                        blk = g0 + k
                        op("tensor", lambda e, b=b, blk=blk, psb=psb, k=k: e.transpose(
                               psb[:, k * 128:(k + 1) * 128],
                               tern[b][:, blk * 128:(blk + 1) * 128], self.ident_b[:]),
                           r=(f"wptern{b}", "ident_b"), w=(f"ps{bank}",),
                           label=f"{tag}_T{li}_{blk}")
                    op("scalar", lambda e, psb=psb, b=b: e.copy(
                           stg[b][:], psb[:]),
                       r=(f"ps{bank}",), w=(f"wpstg{b}",), label=f"{tag}_E{li}_{g0}")
                    if wi == 0:
                        dst = bass.AP(scr, (li * 4 + g0 // 8) * 128 * 1024,
                                      [[1024, 128], [1, 1024]])
                    else:
                        dst = bass.AP(scr, g0 * 128 * 1024 + li * 128,
                                      [[1024, 128], [128 * 1024, 8], [1, 128]])
                    dma("qsync", dst, stg[b][:],
                        r=(f"wpstg{b}",), w=(f"wpout{tag}",),
                        label=f"{tag}_st{li}_{g0}")
        dma("qscal", self.cwcols[:], bass.AP(self.hbm["L0_ffn_conv_w"], 0,
                                             [[3, 128], [384, 32], [1, 3]]),
            w=("cwcols",), label="cw_ld")
        dma("qscal", self.cbcols[:], bass.AP(self.hbm["L0_ffn_conv_b"], 0,
                                             [[1, 128], [128, 32]]),
            w=("cbcols",), label="cb_ld", slow_ok=True)
        st.close()
        self.sched.barrier()

    def _absmax_row(self, tag, src_getter, nchunk, out_row, st):
        op = self.op
        assert nchunk <= 8
        amax16 = self.sb([128, 16], F32, f"{tag}_am16", stack=st)
        am128 = self.sb([16, 128], F32, f"{tag}_am128", stack=st)
        probe_ap, _ = src_getter(0)
        is_bf = probe_ap.dtype == BF16
        for tt in range(16):
            bank = tt % 2 if is_bf else (tt % 2) * 2
            if is_bf:
                psv = self.PS[:, bank * 512:(bank + 1) * 512].bitcast(BF16)
                wtags = (f"ps{bank}",)
            else:
                psv = self.PS[:, bank * 512:(bank + 2) * 512]
                wtags = (f"ps{bank}", f"ps{bank + 1}")
            for c in range(nchunk):
                apc, rtags = src_getter(c)
                ident = self.ident_b if is_bf else self.ident_f
                op("tensor", lambda e, apc=apc, tt=tt, c=c, psv=psv, ident=ident: e.transpose(
                       psv[:, c * 128:(c + 1) * 128],
                       apc[:, tt * 128:(tt + 1) * 128], ident[:]),
                   r=tuple(rtags) + ("ident_b", "ident_f"), w=wtags,
                   label=f"{tag}_amT{tt}_{c}")
            op("vector", lambda e, tt=tt, psv=psv, nchunk=nchunk: e.reduce_max(
                   amax16[:, tt:tt + 1], psv[:, 0:nchunk * 128],
                   axis=AX.X, apply_absolute_value=True),
               r=wtags, w=(f"{tag}_am16",), label=f"{tag}_amR{tt}")
        op("tensor", lambda e: e.transpose(self.PS[0:16, 2048:2176], amax16[:],
                                           self.ident_f[:]),
           r=(f"{tag}_am16", "ident_f"), w=("ps4",), label=f"{tag}_amT2")
        op("vector", lambda e: e.tensor_copy(am128[:], self.PS[0:16, 2048:2176]),
           r=("ps4",), w=(f"{tag}_am128",), label=f"{tag}_amE")
        self.dma("qgp", self.row(0, 0), am128[:],
                 r=(f"{tag}_am128",), w=("r0_0",), label=f"{tag}_amG")

    def _quant_rows(self, tag, wi):
        op = self.op
        op("vector", lambda e: e.tensor_scalar_max(self.row(32, 1), self.row(0, 0), 1e-5),
           r=("r0_0",), w=("r32_1",), label=f"{tag}_clip")
        op("vector", lambda e: e.reciprocal(self.row(0, 0), self.row(32, 1)),
           r=("r32_1",), w=("r0_0",), label=f"{tag}_recip")
        op("vector", lambda e: e.tensor_scalar_mul(self.row(0, 0), self.row(0, 0), 127.0),
           r=("r0_0",), w=("r0_0",), label=f"{tag}_sx")
        op("vector", lambda e: e.tensor_scalar_mul(self.row(0, 1), self.row(32, 1), 1.0 / 127.0),
           r=("r32_1",), w=("r0_1",), label=f"{tag}_inv0")
        op("vector", lambda e, wi=wi: e.tensor_scalar_mul(
               self.row(0, 1), self.row(0, 1), self.tiny[0:1, 1 + wi:2 + wi]),
           r=("r0_1", f"cw{wi}"), w=("r0_1",), label=f"{tag}_inv")

    def _bcast_rows(self, tag, sx_sb=None, inv_sb=None):
        op = self.op
        for j in range(4):
            op("tensor", lambda e, j=j: e.matmul(
                   self.PS[:, j * 512:(j + 1) * 512], self.ones_row_f[:],
                   self.row(0, 0, j * 512, (j + 1) * 512), start=True, stop=True),
               r=("r0_0", "ones_row_f"), w=(f"ps{j}",), label=f"{tag}_bs{j}")
            op("tensor", lambda e, j=j: e.matmul(
                   self.PS[:, 2048 + j * 512: 2048 + (j + 1) * 512],
                   self.ones_row_f[:],
                   self.row(0, 1, j * 512, (j + 1) * 512), start=True, stop=True),
               r=("r0_1", "ones_row_f"), w=(f"ps{4 + j}",), label=f"{tag}_bi{j}")
        if sx_sb is not None:
            op("scalar", lambda e: e.copy(sx_sb[:], self.PS[:, 0:2048]),
               r=("ps0", "ps1", "ps2", "ps3"), w=("ff_sxsb",), label=f"{tag}_esx")
            op("scalar", lambda e: e.copy(inv_sb[:], self.PS[:, 2048:4096]),
               r=("ps4", "ps5", "ps6", "ps7"), w=("ff_invsb",), label=f"{tag}_einv")

    def l0_ffn(self, call):
        op, dma = self.op, self.dma
        tg = f"ffn{call}"
        stC = ExitStack()
        scrA = self.sb([128, T], F32, "ffA", stack=stC)
        scrB = self.sb([128, T], F32, "ffB", stack=stC)
        iqt = self.sb([128, T], I32, "ffI", stack=stC)

        self._absmax_row(tg + "qu",
                         lambda c: (self.u[:, c * T:(c + 1) * T], self.u_tags(c)),
                         8, 0, stC)
        self._quant_rows(tg + "qu", 0)
        self._bcast_rows(tg + "qu")
        sx_ps = [self.PS[:, j * 512:(j + 1) * 512] for j in range(4)]
        inv_ps = [self.PS[:, 2048 + j * 512: 2048 + (j + 1) * 512] for j in range(4)]
        for dc in range(8):
            usl = self.u[:, dc * T:(dc + 1) * T]
            def qv(e, usl=usl):
                ins = None
                for j in range(4):
                    ins = e.tensor_mul(scrA[:, j * 512:(j + 1) * 512],
                                       usl[:, j * 512:(j + 1) * 512], sx_ps[j])
                return ins
            op("vector", qv,
               r=self.u_tags(dc) + ("ps0", "ps1", "ps2", "ps3"), w=("ffA",),
               label=f"{tg}_qv{dc}")
            op("scalar", lambda e, usl=usl: e.activation(scrB[:], usl, AF.Sign),
               r=self.u_tags(dc), w=("ffB",), label=f"{tg}_qs{dc}")
            op("vector", lambda e: e.scalar_tensor_tensor(
                   iqt[:], scrB[:], self.halfkk[:], scrA[:], ALU.mult, ALU.add),
               r=("ffA", "ffB", "halfkk"), w=("ffI",), label=f"{tg}_qi{dc}")
            def qq(e, usl=usl):
                ins = None
                for j in range(4):
                    ins = e.tensor_mul(usl[:, j * 512:(j + 1) * 512],
                                       iqt[:, j * 512:(j + 1) * 512], inv_ps[j])
                return ins
            op("vector", qq,
               r=("ffI", "ps4", "ps5", "ps6", "ps7"), w=self.u_tags(dc),
               label=f"{tg}_qq{dc}")

        st1 = ExitStack()
        h1 = [self.sb([128, T], F32, f"ffh1_{i}", stack=st1) for i in range(2)]
        cacc = [self.sb([128, T], F32, f"ffca_{i}", stack=st1) for i in range(2)]
        hout = [self.sb([128, T], BF16, f"ffho_{i}", stack=st1) for i in range(2)]
        w1l = [self.sb([128, 1024], BF16, f"ffw1l_{i}", stack=st1) for i in range(2)]
        op("vector", lambda e: e.memset(scrB[:], 0.0), w=("ffB",), label=f"{tg}_am0")
        for hc in range(32):
            b = hc % 2
            dma("qgp", w1l[b][:],
                bass.AP(self.w1qT, hc * 128 * 1024, [[1024, 128], [1, 1024]]),
                w=(f"ffw1l{b}",), label=f"{tg}_w1l{hc}")
            bk0 = (hc % 2) * 4
            ptags = tuple(f"ps{bk0 + i}" for i in range(4))
            for ts in range(4):
                def mm1(e, hc=hc, ts=ts, b=b, bk0=bk0):
                    ins = None
                    for dc in range(8):
                        ins = e.matmul(
                            self.PS[:, (bk0 + ts) * 512:(bk0 + ts + 1) * 512],
                            w1l[b][:, dc * 128:(dc + 1) * 128],
                            self.u[:, dc * T + ts * 512: dc * T + (ts + 1) * 512],
                            start=(dc == 0), stop=(dc == 7))
                    return ins
                op("tensor", mm1,
                   r=(f"ffw1l{b}",) + self.u_tags(),
                   w=(f"ps{bk0 + ts}",), label=f"{tg}_mm1_{hc}_{ts}")
            op("vector", lambda e, b=b, bk0=bk0: e.tensor_copy(
                   h1[b][:], self.PS[:, bk0 * 512: bk0 * 512 + 2048]),
               r=ptags, w=(f"ffh1{b}",), label=f"{tg}_ev1_{hc}")
            cw = self.cwcols
            op("vector", lambda e, b=b, hc=hc: e.tensor_scalar_mul(
                   cacc[b][:], h1[b][:], cw[:, hc * 3 + 1: hc * 3 + 2]),
               r=(f"ffh1{b}", "cwcols"), w=(f"ffca{b}",), label=f"{tg}_cv1_{hc}")
            op("vector", lambda e, b=b, hc=hc: e.scalar_tensor_tensor(
                   cacc[b][:, 1:T], h1[b][:, 0:T - 1], cw[:, hc * 3: hc * 3 + 1],
                   cacc[b][:, 1:T], ALU.mult, ALU.add),
               r=(f"ffh1{b}", "cwcols", f"ffca{b}"), w=(f"ffca{b}",),
               label=f"{tg}_cv0_{hc}")
            op("vector", lambda e, b=b, hc=hc: e.scalar_tensor_tensor(
                   cacc[b][:, 0:T - 1], h1[b][:, 1:T], cw[:, hc * 3 + 2: hc * 3 + 3],
                   cacc[b][:, 0:T - 1], ALU.mult, ALU.add),
               r=(f"ffh1{b}", "cwcols", f"ffca{b}"), w=(f"ffca{b}",),
               label=f"{tg}_cv2_{hc}")
            op("scalar", lambda e, b=b, hc=hc: e.activation(
                   hout[b][:], cacc[b][:], AF.Silu,
                   bias=self.cbcols[:, hc:hc + 1]),
               r=(f"ffca{b}", "cbcols"), w=(f"ffho{b}",), label=f"{tg}_silu{hc}")
            op("vector", lambda e, b=b: e.scalar_tensor_tensor(
                   scrA[:], hout[b][:], -1.0, hout[b][:], ALU.mult, ALU.max),
               r=(f"ffho{b}",), w=("ffA",), label=f"{tg}_ab{hc}")
            op("vector", lambda e: e.tensor_max(scrB[:], scrB[:], scrA[:]),
               r=("ffA", "ffB"), w=("ffB",), label=f"{tg}_am{hc}")
            dma("qscal", bass.AP(self.h_scr, hc * 128 * T, [[T, 128], [1, T]]),
                hout[b][:],
                r=(f"ffho{b}",), w=(f"ffhs{hc}",), label=f"{tg}_hst{hc}")
        self._absmax_row(tg + "qh", lambda c: (scrB[:], ("ffB",)), 1, 0, st1)
        self._quant_rows(tg + "qh", 1)
        st1.close()
        self.sched.barrier()

        st2 = ExitStack()
        sx_sb = self.sb([128, T], F32, "ffSX", stack=st2)
        inv_sb = self.sb([128, T], F32, "ffIV", stack=st2)
        hbuf = [self.sb([128, T], BF16, f"ffhb_{i}", stack=st2) for i in range(2)]
        w2lbuf = self.sb([128, 8 * 1024], BF16, "ffw2l", stack=st2)
        self._bcast_rows(tg + "qh", sx_sb, inv_sb)
        for g in range(4):
            for k in range(8):
                hc = g * 8 + k
                b = hc % 2
                dma("qgp", hbuf[b][:],
                    bass.AP(self.h_scr, hc * 128 * T, [[T, 128], [1, T]]),
                    r=(f"ffhs{hc}",), w=(f"ffhb{b}",), label=f"{tg}_hld{hc}")
                dma("qsync", w2lbuf[:, k * 1024:(k + 1) * 1024],
                    bass.AP(self.w2qT, hc * 128 * 1024, [[1024, 128], [1, 1024]]),
                    w=(f"ffw2l{k}",), label=f"{tg}_w2l{hc}")
                op("vector", lambda e, b=b: e.tensor_mul(scrA[:], hbuf[b][:], sx_sb[:]),
                   r=(f"ffhb{b}", "ff_sxsb"), w=("ffA",), label=f"{tg}_hv{hc}")
                op("scalar", lambda e, b=b: e.activation(scrB[:], hbuf[b][:], AF.Sign),
                   r=(f"ffhb{b}",), w=("ffB",), label=f"{tg}_hs{hc}")
                op("vector", lambda e: e.scalar_tensor_tensor(
                       iqt[:], scrB[:], self.halfkk[:], scrA[:], ALU.mult, ALU.add),
                   r=("ffA", "ffB", "halfkk"), w=("ffI",), label=f"{tg}_hi{hc}")
                op("vector", lambda e, k=k: e.tensor_mul(
                       self.u[:, k * T:(k + 1) * T], iqt[:], inv_sb[:]),
                   r=("ffI", "ff_invsb"), w=self.u_tags(k), label=f"{tg}_hq{hc}")
            for ts in range(4):
                for oc in range(8):
                    def mm2(e, g=g, ts=ts, oc=oc):
                        ins = None
                        for k in range(8):
                            ins = e.matmul(
                                self.PS[:, oc * 512:(oc + 1) * 512],
                                w2lbuf[:, k * 1024 + oc * 128: k * 1024 + (oc + 1) * 128],
                                self.u[:, k * T + ts * 512: k * T + (ts + 1) * 512],
                                start=(k == 0), stop=(k == 7))
                        return ins
                    op("tensor", mm2,
                       r=self.u_tags()
                         + tuple(f"ffw2l{k}" for k in range(8)),
                       w=(f"ps{oc}",), label=f"{tg}_mm2_{g}_{ts}_{oc}")
                for oc in range(8):
                    xs = self.x[:, oc * T + ts * 512: oc * T + (ts + 1) * 512]
                    if g == 0:
                        pass
                    op("vector", lambda e, oc=oc, xs=xs: e.scalar_tensor_tensor(
                           xs, self.PS[:, oc * 512:(oc + 1) * 512], 0.5, xs,
                           ALU.mult, ALU.add),
                       r=(f"ps{oc}",) + self.x_tags(oc, range(ts * 4, ts * 4 + 4)),
                       w=self.x_tags(oc, range(ts * 4, ts * 4 + 4)),
                       label=f"{tg}_res_{g}_{ts}_{oc}")
        st2.close()
        stC.close()
        self.sched.barrier()

    def _rope_apply(self, tag, pre_lo, pre_hi, out32, trig, tmp, st, out_tag):
        op, dma = self.op, self.dma
        cos = trig[:, 0:T]
        sin = trig[:, T:2 * T]
        h1 = self.sb([16, T], BF16, f"{tag}_h1", stack=st)
        h2 = self.sb([16, T], BF16, f"{tag}_h2", stack=st)
        lo_tags = ("ps0", "ps1", "ps2", "ps3")
        hi_tags = ("ps4", "ps5", "ps6", "ps7")
        op("vector", lambda e: e.tensor_mul(tmp[0], pre_lo, cos),
           r=lo_tags + ("at_trig",), w=("r0_0",), label=f"{tag}_a")
        op("vector", lambda e: e.tensor_mul(tmp[1], pre_hi, sin),
           r=hi_tags + ("at_trig",), w=("r0_1",), label=f"{tag}_b")
        op("vector", lambda e: e.tensor_sub(h1[:], tmp[0], tmp[1]),
           r=("r0_0", "r0_1"), w=(f"{tag}_h1",), label=f"{tag}_c")
        op("vector", lambda e: e.tensor_mul(tmp[0], pre_hi, cos),
           r=hi_tags + ("at_trig",), w=("r0_0",), label=f"{tag}_d")
        op("vector", lambda e: e.tensor_mul(tmp[1], pre_lo, sin),
           r=lo_tags + ("at_trig",), w=("r0_1",), label=f"{tag}_e")
        op("vector", lambda e: e.tensor_add(h2[:], tmp[0], tmp[1]),
           r=("r0_0", "r0_1"), w=(f"{tag}_h2",), label=f"{tag}_f")
        dma("qgp", out32[0:16, :], h1[:], r=(f"{tag}_h1",), w=(out_tag,),
            label=f"{tag}_g")
        dma("qgp", out32[16:32, :], h2[:], r=(f"{tag}_h2",), w=(out_tag,),
            label=f"{tag}_h")

    def _build_VT(self, tg, VT, ckv, wvu_b, NH):
        op = self.op
        for tt in range(16):
            bk = (tt % 2) * 2
            for hi in range(2):
                op("tensor", lambda e, tt=tt, hi=hi, bk=bk: e.matmul(
                       self.PS[:, (bk + hi) * 512:(bk + hi + 1) * 512],
                       ckv[:, tt * 128:(tt + 1) * 128],
                       wvu_b[:, hi * 512:(hi + 1) * 512],
                       start=True, stop=True),
                   r=("at_ckv", "at_wvu"), w=(f"ps{bk + hi}",),
                   label=f"{tg}_vt{tt}_{hi}")
            def vev(e, tt=tt, bk=bk):
                ins = None
                for h8 in range(16):
                    src_c = bk * 512 + h8 * 64
                    ins = e.tensor_copy(
                        VT[:, tt * (16 * 65) + h8 * 65: tt * (16 * 65) + h8 * 65 + 64],
                        self.PS[:, src_c:src_c + 64])
                return ins
            op("vector", vev, r=(f"ps{bk}", f"ps{bk + 1}"), w=("at_VT",),
               label=f"{tg}_vte{tt}")
        op("vector", lambda e: e.memset(
               bass.AP(VT, 64, [[16 * 65 * 16, 128], [65, 16 * 16], [1, 1]]), 1.0),
           r=("at_VT",), w=("at_VT",), label=f"{tg}_vt1")


    def l0_attn(self):
        op, dma = self.op, self.dma
        tg = "att"
        NH, HD, DR = 16, 64, 32
        SC = 1.0 / math.sqrt(HD + DR)
        self.constf(SC)
        self.constf(-1.0)
        self.constf(math.pi / 2.0)
        st = ExitStack()
        attn_f = self.sb([128, 8 * T], BF16, "at_out", stack=st)
        ckv = self.sb([128, T], BF16, "at_ckv", stack=st)
        Q97 = self.sb([97, T], BF16, "at_Q97", stack=st)
        K97 = self.sb([97, T], BF16, "at_K97", stack=st)
        PTb = self.sb([128, 1024], BF16, "at_PT", stack=st)
        PT = [PTb[:, 0:512], PTb[:, 512:1024]]
        masks = self.sb([128, 4 * 512], BF16, "at_mask", stack=st)
        tmp16 = [self.rows[0:16, 0:T], self.rows[0:16, T:2 * T]]
        it16 = self.rows[0:16, T:2 * T].bitcast(I32)
        wbuf = [self.sb([128, 512], F32, "at_wf", stack=st)] * 2
        wku_b = self.sb([128, 1024], BF16, "at_wku", stack=st)
        wvu_b = self.sb([128, 1024], BF16, "at_wvu", stack=st)
        stW = ExitStack()
        trig = self.sb([16, 2 * T], BF16, "at_trig", stack=stW)
        wkv_b = self.sb([128, 8 * 128], BF16, "at_wkv", stack=stW)
        wkr_b = self.sb([128, 32], BF16, "at_wkr", stack=stW)
        wqr_b = self.sb([128, 8 * 32], BF16, "at_wqr", stack=stW)

        def ldcast(hname, ncols, dst, dtag, ap_dims=None, bi=0):
            dims = ap_dims or [[ncols, 128], [1, ncols]]
            for c0 in range(0, ncols, 512):
                cw_ = min(512, ncols - c0)
                if ap_dims is None:
                    src = bass.AP(self.hbm[hname], c0, [[ncols, 128], [1, cw_]])
                else:
                    assert ncols <= 512
                    src = bass.AP(self.hbm[hname], 0, ap_dims)
                dma("qgp", wbuf[bi][:, 0:cw_], src,
                    w=("at_wf0",), label=f"{tg}_{dtag}ld{c0}")
                op("vector", lambda e, c0=c0, cw_=cw_: e.tensor_copy(
                       dst[:, c0:c0 + cw_], wbuf[bi][:, 0:cw_]),
                   r=("at_wf0",), w=(dtag,), label=f"{tg}_{dtag}c{c0}")
        for g4 in range(2):
            dma("qgp", wbuf[0][:],
                bass.AP(self.hbm["L0_attn_W_kv_down"], g4 * 4 * 128 * 128,
                        [[128, 128], [128 * 128, 4], [1, 128]]),
                w=("at_wf0",), label=f"{tg}_kvld{g4}")
            op("vector", lambda e, g4=g4: e.tensor_copy(
                   wkv_b[:, g4 * 512:(g4 + 1) * 512], wbuf[0][:]),
               r=("at_wf0",), w=("at_wkv",), label=f"{tg}_kvc{g4}")
        ldcast("L0_attn_W_k_up", 1024, wku_b, "at_wku", bi=1)
        ldcast("L0_attn_W_v_up", 1024, wvu_b, "at_wvu", bi=0)
        ldcast("L0_attn_W_k_rope", 32, wkr_b, "at_wkr", bi=1)
        ldcast("L0_attn_W_q_rope", 256, wqr_b, "at_wqr",
               ap_dims=[[32, 128], [4096, 8], [1, 32]], bi=0)

        stM = ExitStack()
        itc = self.sb([128, 512], F32, "at_itc", stack=stM)
        itr = self.sb([128, 512], F32, "at_itr", stack=stM)
        op("gpsimd", lambda e: e.iota(itc[:], [[1, 512]], channel_multiplier=0,
                                      allow_small_or_imprecise_dtypes=True),
           w=("at_itc",), label=f"{tg}_ic")
        op("gpsimd", lambda e: e.iota(itr[:], [[0, 512]], channel_multiplier=1,
                                      allow_small_or_imprecise_dtypes=True),
           w=("at_itr",), label=f"{tg}_ir")
        for di in range(4):
            op("vector", lambda e, di=di: e.scalar_tensor_tensor(
                   masks[:, di * 512:(di + 1) * 512], itr[:], float(di * 128),
                   itc[:], ALU.add, ALU.is_le),
               r=("at_itr", "at_itc"), w=("at_mask",), label=f"{tg}_mk{di}")
        stM.close()
        self.sched.barrier()

        op("gpsimd", lambda e: e.iota(self.row(0, 0, 0, 16), [[1, 16]],
                                      channel_multiplier=0,
                                      allow_small_or_imprecise_dtypes=True),
           w=("r0_0",), label=f"{tg}_fi")
        op("vector", lambda e: e.tensor_scalar_mul(
               self.row(0, 0, 0, 16), self.row(0, 0, 0, 16),
               -math.log(10000.0) / 16.0),
           r=("r0_0",), w=("r0_0",), label=f"{tg}_fs")
        op("scalar", lambda e: e.activation(self.row(0, 0, 0, 16),
                                            self.row(0, 0, 0, 16), AF.Exp),
           r=("r0_0",), w=("r0_0",), label=f"{tg}_fe")
        op("gpsimd", lambda e: e.iota(self.row(0, 1), [[1, T]], channel_multiplier=0,
                                      allow_small_or_imprecise_dtypes=True),
           w=("r0_1",), label=f"{tg}_ti")
        for j in range(4):
            op("tensor", lambda e, j=j: e.matmul(
                   self.PS[0:16, j * 512:(j + 1) * 512],
                   self.row(0, 0, 0, 16), self.row(0, 1, j * 512, (j + 1) * 512),
                   start=True, stop=True),
               r=("r0_0", "r0_1"), w=(f"ps{j}",), label=f"{tg}_th{j}")
        TP = 2.0 * math.pi
        op("vector", lambda e: e.tensor_scalar(
               it16, self.PS[0:16, 0:2048], 1.0 / TP, self.halfkk[0:16, 0:1],
               ALU.mult, ALU.add),
           r=("ps0", "ps1", "ps2", "ps3", "halfkk"), w=("r0_1",), label=f"{tg}_rk")
        op("vector", lambda e: e.tensor_copy(tmp16[0], it16),
           r=("r0_1",), w=("r0_0",), label=f"{tg}_kf")
        op("vector", lambda e: e.scalar_tensor_tensor(
               tmp16[1], tmp16[0], -TP, self.PS[0:16, 0:2048], ALU.mult, ALU.add),
           r=("r0_0", "ps0", "ps1", "ps2", "ps3"), w=("r0_1",), label=f"{tg}_thr")
        op("scalar", lambda e: e.activation(trig[:, T:2 * T], tmp16[1], AF.Sin),
           r=("r0_1",), w=("at_trig",), label=f"{tg}_sin")
        op("vector", lambda e: e.scalar_tensor_tensor(
               tmp16[0], tmp16[1], -1.0, tmp16[1], ALU.mult, ALU.max),
           r=("r0_1",), w=("r0_0",), label=f"{tg}_abs")
        op("scalar", lambda e: e.activation(trig[:, 0:T], tmp16[0], AF.Sin,
                                            scale=-1.0, bias=math.pi / 2.0),
           r=("r0_0",), w=("at_trig",), label=f"{tg}_cos")

        for ts in range(4):
            def mmc(e, ts=ts):
                ins = None
                for dc in range(8):
                    ins = e.matmul(
                        self.PS[:, 2048 + ts * 512: 2048 + (ts + 1) * 512],
                        wkv_b[:, dc * 128:(dc + 1) * 128],
                        self.u[:, dc * T + ts * 512: dc * T + (ts + 1) * 512],
                        start=(dc == 0), stop=(dc == 7))
                return ins
            op("tensor", mmc, r=("at_wkv",) + self.u_tags(),
               w=(f"ps{4 + ts}",), label=f"{tg}_ckv{ts}")
        op("vector", lambda e: e.tensor_copy(ckv[:], self.PS[:, 2048:4096]),
           r=("ps4", "ps5", "ps6", "ps7"), w=("at_ckv",), label=f"{tg}_ckve")

        for hi in range(2):
            for ts in range(4):
                op("tensor", lambda e, hi=hi, ts=ts: e.matmul(
                       self.PS[0:16, hi * 2048 + ts * 512: hi * 2048 + (ts + 1) * 512],
                       wkr_b[:, hi * 16:(hi + 1) * 16],
                       ckv[:, ts * 512:(ts + 1) * 512],
                       start=True, stop=True),
                   r=("at_wkr", "at_ckv"), w=(f"ps{hi * 4 + ts}",),
                   label=f"{tg}_krp{hi}_{ts}")
        self._rope_apply(tg + "kr", self.PS[0:16, 0:2048], self.PS[0:16, 2048:4096],
                         K97[64:96, :], trig, tmp16, stW, "at_K97")
        for hi in range(2):
            for ts in range(4):
                def mmqr(e, hi=hi, ts=ts):
                    ins = None
                    for dc in range(8):
                        ins = e.matmul(
                            self.PS[0:16, hi * 2048 + ts * 512: hi * 2048 + (ts + 1) * 512],
                            wqr_b[:, dc * 32 + hi * 16: dc * 32 + hi * 16 + 16],
                            self.u[:, dc * T + ts * 512: dc * T + (ts + 1) * 512],
                            start=(dc == 0), stop=(dc == 7))
                    return ins
                op("tensor", mmqr, r=("at_wqr",) + self.u_tags(),
                   w=(f"ps{hi * 4 + ts}",), label=f"{tg}_qrp{hi}_{ts}")
        self._rope_apply(tg + "qr", self.PS[0:16, 0:2048], self.PS[0:16, 2048:4096],
                         Q97[64:96, :], trig, tmp16, stW, "at_Q97")

        stW.close()
        self.sched.barrier()
        VT = self.sb([128, 16 * (NH * 65)], BF16, "at_VT", stack=st)
        wq_b = [self.sb([128, 8 * 64], BF16, "at_wq", stack=st)] * 2
        wo_b = [self.sb([128, 8 * 128], BF16, "at_wo", stack=st)] * 2
        rmax = self.sb([128, 16], F32, "at_rmax", stack=st)
        rm128 = self.rows[0:16, 1024:1152]
        osb = self.rows[0:64, 0:256].bitcast(BF16)
        linv = self.rows[0:1, 512:1024]
        self._build_VT(tg, VT, ckv, wvu_b, NH)
        op("vector", lambda e: e.memset(K97[96:97, :], 1.0),
           r=("at_K97",), w=("at_K97",), label=f"{tg}_k97o")

        for h in range(NH):
            hb = h % 2
            dma("qgp", wbuf[hb][:],
                bass.AP(self.hbm["L0_attn_W_q"], h * 64,
                        [[1024, 128], [128 * 1024, 8], [1, 64]]),
                w=("at_wf0",), label=f"{tg}_wq{h}")
            op("vector", lambda e, hb=hb: e.tensor_copy(wq_b[hb][:], wbuf[hb][:, 0:512]),
               r=("at_wf0",), w=("at_wq0",), label=f"{tg}_wqc{h}")
            for ts in range(4):
                def mmq(e, ts=ts, hb=hb):
                    ins = None
                    for dc in range(8):
                        ins = e.matmul(
                            self.PS[0:64, ts * 512:(ts + 1) * 512],
                            wq_b[hb][:, dc * 64:(dc + 1) * 64],
                            self.u[:, dc * T + ts * 512: dc * T + (ts + 1) * 512],
                            start=(dc == 0), stop=(dc == 7))
                    return ins
                op("tensor", mmq, r=("at_wq0",) + self.u_tags(),
                   w=(f"ps{ts}",), label=f"{tg}_q{h}_{ts}")
                op("tensor", lambda e, h=h, ts=ts: e.matmul(
                       self.PS[0:64, 2048 + ts * 512: 2048 + (ts + 1) * 512],
                       wku_b[:, h * 64:(h + 1) * 64],
                       ckv[:, ts * 512:(ts + 1) * 512],
                       start=True, stop=True),
                   r=("at_wku", "at_ckv"), w=(f"ps{4 + ts}",), label=f"{tg}_k{h}_{ts}")
            op("vector", lambda e: e.tensor_copy(Q97[0:64, :], self.PS[0:64, 0:2048]),
               r=("ps0", "ps1", "ps2", "ps3"), w=("at_Q97",), label=f"{tg}_qe{h}")
            op("vector", lambda e: e.tensor_copy(K97[0:64, :], self.PS[0:64, 2048:4096]),
               r=("ps4", "ps5", "ps6", "ps7"), w=("at_K97",), label=f"{tg}_ke{h}")
            for qi in range(16):
                nk = 128 * (qi + 1)
                nsl = (nk + 511) // 512
                def mmp1(e, qi=qi, nk=nk, nsl=nsl):
                    ins = None
                    for sl in range(nsl):
                        n0 = sl * 512
                        n1 = min(nk, n0 + 512)
                        ins = e.matmul(
                            self.PS[:, sl * 512: sl * 512 + (n1 - n0)],
                            Q97[0:96, qi * 128:(qi + 1) * 128],
                            K97[0:96, n0:n1],
                            start=True, stop=True)
                    return ins
                op("tensor", mmp1, r=("at_Q97", "at_K97"),
                   w=tuple(f"ps{b2}" for b2 in range(nsl)),
                   label=f"{tg}_p1_{h}_{qi}")
                op("vector", lambda e, qi=qi, nk=nk, nsl=nsl: e.reduce_max(
                       rmax[:, qi:qi + 1], self.PS[:, 0:nk], axis=AX.X,
                       negate=True),
                   r=tuple(f"ps{b2}" for b2 in range(nsl)),
                   w=("at_rmax",), label=f"{tg}_rx_{h}_{qi}")
            op("tensor", lambda e: e.transpose(self.PS[0:16, 3584:3712], rmax[:],
                                               self.ident_f[:]),
               r=("at_rmax", "ident_f"), w=("ps7",), label=f"{tg}_rxT{h}")
            op("vector", lambda e: e.tensor_copy(rm128[:], self.PS[0:16, 3584:3712]),
               r=("ps7",), w=("r0_0",), label=f"{tg}_rxE{h}")
            dma("qgp", Q97[96:97, :], rm128[:],
                r=("r0_0",), w=("at_Q97",), label=f"{tg}_rxG{h}")
            for qc in range(4):
                nkt = (qc + 1) * 4
                obank = 6
                for kt0 in range(0, nkt, 2):
                    for kt in (kt0, kt0 + 1):
                        sbk = 4 + kt % 2
                        op("tensor", lambda e, kt=kt, qc=qc, sbk=sbk: e.matmul(
                               self.PS[:, sbk * 512:(sbk + 1) * 512],
                               K97[:, kt * 128:(kt + 1) * 128],
                               Q97[:, qc * 512:(qc + 1) * 512],
                               start=True, stop=True),
                           r=("at_Q97", "at_K97"), w=(f"ps{sbk}",),
                           label=f"{tg}_p2_{h}_{qc}_{kt}")
                    op("scalar", lambda e: e.activation(
                           PTb[:], self.PS[:, 2048:3072], AF.Exp, scale=SC),
                       r=("ps4", "ps5"), w=("at_PT",),
                       label=f"{tg}_ex_{h}_{qc}_{kt0}")
                    for kt in (kt0, kt0 + 1):
                        di = kt - qc * 4
                        if di >= 0:
                            op("vector", lambda e, pb=kt % 2, di=di: e.tensor_mul(
                                   PT[pb], PT[pb],
                                   masks[:, di * 512:(di + 1) * 512]),
                               r=("at_PT", "at_mask"), w=("at_PT",),
                               label=f"{tg}_msk_{h}_{qc}_{kt}")
                    for kt in (kt0, kt0 + 1):
                        op("tensor",
                           lambda e, kt=kt, pb=kt % 2, h=h, nkt=nkt, obank=obank:
                           e.matmul(
                               self.PS[0:65, obank * 512:(obank + 1) * 512],
                               VT[:, kt * (16 * 65) + h * 65: kt * (16 * 65) + (h + 1) * 65],
                               PT[pb],
                               start=(kt == 0), stop=(kt == nkt - 1)),
                           r=("at_PT", "at_VT"), w=(f"ps{obank}",),
                           label=f"{tg}_av_{h}_{qc}_{kt}")
                op("scalar", lambda e, obank=obank: e.activation(
                       linv[:], self.PS[64:65, obank * 512:(obank + 1) * 512],
                       AF.Ln),
                   r=(f"ps{obank}",), w=("r0_0",), label=f"{tg}_li_{h}_{qc}")
                op("scalar", lambda e: e.activation(
                       linv[:], linv[:], AF.Exp, scale=-1.0),
                   r=("r0_0",), w=("r0_0",), label=f"{tg}_li2_{h}_{qc}")
                op("tensor", lambda e: e.matmul(
                       self.PS[0:64, 3584:4096], self.ones_row_f[0:1, 0:64],
                       linv[:], start=True, stop=True),
                   r=("r0_0", "ones_row_f"), w=("ps7",), label=f"{tg}_lb_{h}_{qc}")
                op("scalar", lambda e, obank=obank: e.copy(
                       osb[:], self.PS[0:64, obank * 512:(obank + 1) * 512]),
                   r=(f"ps{obank}",), w=("r32_0",), label=f"{tg}_oe_{h}_{qc}")
                dc0 = h // 2
                r0 = 64 * (h % 2)
                op("vector", lambda e, dc0=dc0, r0=r0, qc=qc: e.tensor_mul(
                       attn_f[r0:r0 + 64, dc0 * T + qc * 512: dc0 * T + (qc + 1) * 512],
                       osb[:], self.PS[0:64, 3584:4096]),
                   r=("r32_0", "ps7"), w=(f"at_o{dc0}",),
                   label=f"{tg}_on_{h}_{qc}")

        for oc in range(8):
            hb = oc % 2
            for g4 in range(2):
                dma("qgp", wbuf[hb][:],
                    bass.AP(self.hbm["L0_attn_W_o"], oc * 128 + g4 * 4 * 128 * 1024,
                            [[1024, 128], [128 * 1024, 4], [1, 128]]),
                    w=("at_wf0",), label=f"{tg}_wo{oc}_{g4}")
                op("vector", lambda e, hb=hb, g4=g4: e.tensor_copy(
                       wo_b[hb][:, g4 * 512:(g4 + 1) * 512], wbuf[hb][:]),
                   r=("at_wf0",), w=("at_wo0",), label=f"{tg}_woc{oc}_{g4}")
            for ts in range(4):
                bk = (oc % 2) * 4 + ts
                def mmo(e, ts=ts, hb=hb, bk=bk):
                    ins = None
                    for dc in range(8):
                        ins = e.matmul(
                            self.PS[:, bk * 512:(bk + 1) * 512],
                            wo_b[hb][:, dc * 128:(dc + 1) * 128],
                            attn_f[:, dc * T + ts * 512: dc * T + (ts + 1) * 512],
                            start=(dc == 0), stop=(dc == 7))
                    return ins
                op("tensor", mmo,
                   r=("at_wo0",) + tuple(f"at_o{d}" for d in range(8)),
                   w=(f"ps{bk}",), label=f"{tg}_mo_{oc}_{ts}")
                op("vector", lambda e, oc=oc, ts=ts, bk=bk: e.tensor_add(
                       self.x[:, oc * T + ts * 512: oc * T + (ts + 1) * 512],
                       self.x[:, oc * T + ts * 512: oc * T + (ts + 1) * 512],
                       self.PS[:, bk * 512:(bk + 1) * 512]),
                   r=(f"ps{bk}",) + self.x_tags(oc, range(ts * 4, ts * 4 + 4)),
                   w=self.x_tags(oc, range(ts * 4, ts * 4 + 4)),
                   label=f"{tg}_res_{oc}_{ts}")
        st.close()
        self.sched.barrier()

    def prep_l1_ffn_weights(self):
        op, dma = self.op, self.dma
        tg = "p1f"
        st = ExitStack()
        wf = [self.sb([128, 2048], F32, f"p1f_wf{i}", stack=st) for i in range(2)]
        wb = [self.sb([128, 2048], BF16, f"p1f_wb{i}", stack=st) for i in range(2)]
        stg = [self.sb([128, 2048], BF16, f"p1f_stg{i}", stack=st) for i in range(2)]
        wc = self.hbm["L1_ffn_w_cproj"]
        for oc in range(8):
            for slab in range(8):
                b = slab % 2
                dma("qgp", wf[b][:],
                    bass.AP(wc, oc * 128 * 16384 + slab * 2048,
                            [[16384, 128], [1, 2048]]),
                    w=(f"p1f_wf{b}",), label=f"{tg}_ld{oc}_{slab}")
                op("vector", lambda e, b=b: e.tensor_copy(wb[b][:], wf[b][:]),
                   r=(f"p1f_wf{b}",), w=(f"p1f_wb{b}",), label=f"{tg}_c{oc}_{slab}")
                for g8 in range(2):
                    bank = 2 + g8
                    psb = self.PS[:, bank * 512:(bank + 1) * 512].bitcast(BF16)
                    for k in range(8):
                        blk = g8 * 8 + k
                        op("tensor", lambda e, b=b, blk=blk, psb=psb, k=k: e.transpose(
                               psb[:, k * 128:(k + 1) * 128],
                               wb[b][:, blk * 128:(blk + 1) * 128], self.ident_b[:]),
                           r=(f"p1f_wb{b}", "ident_b"), w=(f"ps{bank}",),
                           label=f"{tg}_T{oc}_{slab}_{blk}")
                    op("scalar", lambda e, psb=psb, b=b, g8=g8: e.copy(
                           stg[b][:, g8 * 1024:(g8 + 1) * 1024], psb[:]),
                       r=(f"ps{bank}",), w=(f"p1f_stg{b}",),
                       label=f"{tg}_E{oc}_{slab}_{g8}")
                dma("qsync", bass.AP(self.wcT,
                                     slab * 16 * 128 * 1024 + oc * 128,
                                     [[1024, 128], [128 * 1024, 16], [1, 128]]),
                    stg[b][:],
                    r=(f"p1f_stg{b}",), w=("wcT_out",), label=f"{tg}_st{oc}_{slab}")
        st.close()
        self.sched.barrier()
        st2 = ExitStack()
        Cst = self.sb([128, 32 * 256], BF16, "l1_Cstb", stack=st2)
        mk = self.sb([128, 16], F32, "p1f_mk", stack=st2)
        upf = self.sb([128, 16], F32, "p1f_up", stack=st2)
        mkT = self.sb([16, 128], BF16, "p1f_mkT", stack=st2)
        upT = self.sb([16, 128], BF16, "p1f_upT", stack=st2)
        bm1 = self.sb([32, 128], F32, "p1f_bm1", stack=st2)
        bm2 = self.sb([32, 128], F32, "p1f_bm2", stack=st2)
        bdm = self.sb([32, 128], BF16, "p1f_bdm", stack=st2)
        op("gpsimd", lambda e: e.iota(bm1[:], [[0, 128]], channel_multiplier=1,
                                      allow_small_or_imprecise_dtypes=True),
           w=("p1f_bm1",), label=f"{tg}_bi1")
        op("gpsimd", lambda e: e.iota(bm2[:], [[1, 128]], channel_multiplier=0,
                                      allow_small_or_imprecise_dtypes=True),
           w=("p1f_bm2",), label=f"{tg}_bi2")
        op("vector", lambda e: e.tensor_single_scalar(bm1[:], bm1[:], 16.0, ALU.is_ge),
           r=("p1f_bm1",), w=("p1f_bm1",), label=f"{tg}_bg1")
        op("vector", lambda e: e.tensor_single_scalar(bm2[:], bm2[:], 64.0, ALU.is_ge),
           r=("p1f_bm2",), w=("p1f_bm2",), label=f"{tg}_bg2")
        op("vector", lambda e: e.tensor_tensor(bdm[:], bm1[:], bm2[:], ALU.is_equal),
           r=("p1f_bm1", "p1f_bm2"), w=("p1f_bdm",), label=f"{tg}_bdm")
        op("vector", lambda e, Cst=Cst: e.memset(Cst[:], 0.0),
           w=("l1_Cst",), label=f"{tg}_cst0")
        mkh = self.hbm["L1_ffn_rail_maker"]
        uph = self.hbm["L1_ffn_rail_up_proj"]
        for g in range(32):
            h, c = g // 8, g % 8
            dma("qgp", mk[:], bass.AP(mkh, (h * 64 + 8 * c) * 256,
                                      [[16, 128], [1, 16]]),
                w=("p1f_mk",), label=f"{tg}_mk{g}")
            op("tensor", lambda e: e.transpose(self.PS[0:16, 0:128],
                                               mk[:], self.ident_f[:]),
               r=("p1f_mk", "ident_f"), w=("ps0",), label=f"{tg}_mT{g}")
            op("vector", lambda e: e.tensor_copy(mkT[:], self.PS[0:16, 0:128]),
               r=("ps0",), w=("p1f_mkT",), label=f"{tg}_mE{g}")
            for m in range(4):
                dma("qgp", upf[:], bass.AP(uph, (h * 64 + 8 * c + 2 * m) * 1024,
                                           [[16, 128], [1, 16]]),
                    w=("p1f_up",), label=f"{tg}_up{g}_{m}")
                op("tensor", lambda e: e.transpose(self.PS[0:16, 512:640],
                                                   upf[:], self.ident_f[:]),
                   r=("p1f_up", "ident_f"), w=("ps1",), label=f"{tg}_uT{g}_{m}")
                op("vector", lambda e: e.tensor_copy(upT[:], self.PS[0:16, 512:640]),
                   r=("ps1",), w=("p1f_upT",), label=f"{tg}_uE{g}_{m}")
                op("tensor", lambda e, m=m: e.matmul(
                       self.PS[0:32, 1024:1152], mkT[:, m * 32:(m + 1) * 32],
                       upT[:], start=True, stop=True),
                   r=("p1f_mkT", "p1f_upT"), w=("ps2",), label=f"{tg}_cm{g}_{m}")
                if m < 3:
                    dst = Cst[32 * m:32 * (m + 1), g * 256:g * 256 + 128]
                else:
                    dst = Cst[96:128, g * 256 + 128:g * 256 + 256]
                op("vector", lambda e, dst=dst: e.tensor_mul(
                       dst, self.PS[0:32, 1024:1152], bdm[:]),
                   r=("ps2", "p1f_bdm"), w=("l1_Cst",), label=f"{tg}_cw{g}_{m}")
        dma("qsync", self.CsT_hbm.ap(), Cst[:], r=("l1_Cst",), w=("CsT_out",),
            label=f"{tg}_cst_save")
        st2.close()
        self.sched.barrier()

    def l1_ffn(self, call):
        op, dma = self.op, self.dma
        tg = f"l1f{call}"
        st = ExitStack()
        Cst = self.sb([128, 32 * 256], BF16, "l1_Cst", stack=st)
        dma("qgp", Cst[:], self.CsT_hbm.ap(), r=("CsT_out",), w=("l1_Cst",),
            label=f"{tg}_cst_ld")
        hact = self.sb([128, 32 * 512], BF16, "l1_hact", stack=st)
        spill = self.sb([128, 8 * 512], F32, "l1_spill", stack=st)
        wl = [self.sb([128, 2048], BF16, f"l1_wl{i}", stack=st) for i in range(2)]
        for tc in range(4):
            for h in range(4):
                for c in range(8):
                    g = h * 8 + c
                    gl = c * 4
                    bsel = (c % 2) * 4
                    def mmr(e, g=g, c=c, tc=tc, bsel=bsel):
                        ins = None
                        for m in range(4):
                            if m < 3:
                                lh = Cst[32 * m:32 * (m + 1),
                                         g * 256:g * 256 + 128]
                                rh = self.u[32 * m:32 * (m + 1),
                                            c * T + tc * 512:c * T + (tc + 1) * 512]
                            else:
                                lh = Cst[64:128, g * 256 + 128:g * 256 + 256]
                                rh = self.u[64:128,
                                            c * T + tc * 512:c * T + (tc + 1) * 512]
                            ins = e.matmul(
                                self.PS[:, (bsel + m) * 512:(bsel + m + 1) * 512],
                                lh, rh, start=True, stop=True)
                        return ins
                    op("tensor", mmr, r=("l1_Cst",) + self.u_tags(c),
                       w=tuple(f"ps{bsel + m}" for m in range(4)),
                       label=f"{tg}_r{tc}_{g}")
                    op("scalar", lambda e, gl=gl, bsel=bsel: e.activation(
                           hact[:, gl * 512:(gl + 4) * 512],
                           self.PS[:, bsel * 512:(bsel + 4) * 512], AF.Gelu),
                       r=tuple(f"ps{bsel + m}" for m in range(4)),
                       w=("l1_hact",), label=f"{tg}_g{tc}_{g}")
                for i32 in range(32):
                    hc = h * 32 + i32
                    b = (i32 // 2) % 2
                    if i32 % 2 == 0:
                        dma("qgp", wl[b][:],
                            bass.AP(self.wcT, hc * 128 * 1024,
                                    [[1024, 128], [131072, 2], [1, 1024]]),
                            w=(f"l1_wl{b}",), label=f"{tg}_w{tc}_{hc}")
                    half = (i32 % 2) * 1024
                    def mmw(e, i32=i32, b=b, half=half):
                        ins = None
                        for oc in range(8):
                            ins = e.matmul(
                                self.PS[:, oc * 512:(oc + 1) * 512],
                                wl[b][:, half + oc * 128: half + (oc + 1) * 128],
                                hact[:, i32 * 512:(i32 + 1) * 512],
                                start=(i32 == 0), stop=(i32 == 31))
                        return ins
                    op("tensor", mmw, r=(f"l1_wl{b}", "l1_hact"),
                       w=tuple(f"ps{oc}" for oc in range(8)),
                       label=f"{tg}_m{tc}_{hc}")
                for oc in range(8):
                    if h == 0:
                        op("vector", lambda e, oc=oc: e.tensor_copy(
                               spill[:, oc * 512:(oc + 1) * 512],
                               self.PS[:, oc * 512:(oc + 1) * 512]),
                           r=(f"ps{oc}",), w=("l1_spill",), label=f"{tg}_s{tc}_{h}_{oc}")
                    else:
                        op("vector", lambda e, oc=oc: e.tensor_add(
                               spill[:, oc * 512:(oc + 1) * 512],
                               spill[:, oc * 512:(oc + 1) * 512],
                               self.PS[:, oc * 512:(oc + 1) * 512]),
                           r=(f"ps{oc}", "l1_spill"), w=("l1_spill",),
                           label=f"{tg}_s{tc}_{h}_{oc}")
            for oc in range(8):
                xs = self.x[:, oc * T + tc * 512: oc * T + (tc + 1) * 512]
                op("vector", lambda e, oc=oc, xs=xs: e.scalar_tensor_tensor(
                       xs, spill[:, oc * 512:(oc + 1) * 512], 0.5, xs,
                       ALU.mult, ALU.add),
                   r=("l1_spill",) + self.x_tags(oc, range(tc * 4, tc * 4 + 4)),
                   w=self.x_tags(oc, range(tc * 4, tc * 4 + 4)),
                   label=f"{tg}_x{tc}_{oc}")
        st.close()
        self.sched.barrier()

    def prep_l1_attn_weights(self):
        op, dma = self.op, self.dma
        tg = "p1a"
        st = ExitStack()
        wf = [self.sb([128, 2048], F32, f"p1a_wf{i}", stack=st) for i in range(2)]
        wb = [self.sb([128, 2048], BF16, f"p1a_wb{i}", stack=st) for i in range(2)]
        stg = [self.sb([128, 2048], BF16, f"p1a_stg{i}", stack=st) for i in range(2)]

        def tile_pass(hname, scratch, n_tiles, nblk, lbl):
            h = self.hbm[hname]
            ncol = nblk * 128
            for it in range(n_tiles):
                b = it % 2
                dma("qgp", wf[b][:, 0:ncol],
                    bass.AP(h, it * 128 * ncol, [[ncol, 128], [1, ncol]]),
                    w=(f"p1a_wf{b}",), label=f"{lbl}_ld{it}")
                op("vector", lambda e, b=b, ncol=ncol: e.tensor_copy(
                       wb[b][:, 0:ncol], wf[b][:, 0:ncol]),
                   r=(f"p1a_wf{b}",), w=(f"p1a_wb{b}",), label=f"{lbl}_c{it}")
                for g in range(nblk // 8):
                    bank = 2 + g % 2
                    psb = self.PS[:, bank * 512:(bank + 1) * 512].bitcast(BF16)
                    for k in range(8):
                        blk = g * 8 + k
                        op("tensor", lambda e, b=b, blk=blk, psb=psb, k=k:
                           e.transpose(psb[:, k * 128:(k + 1) * 128],
                                       wb[b][:, blk * 128:(blk + 1) * 128],
                                       self.ident_b[:]),
                           r=(f"p1a_wb{b}", "ident_b"), w=(f"ps{bank}",),
                           label=f"{lbl}_T{it}_{blk}")
                    op("scalar", lambda e, psb=psb, b=b, g=g: e.copy(
                           stg[b][:, g * 1024:(g + 1) * 1024], psb[:]),
                       r=(f"ps{bank}",), w=(f"p1a_stg{b}",),
                       label=f"{lbl}_E{it}_{g}")
                dma("qsync",
                    bass.AP(scratch, it * 128 * ncol, [[ncol, 128], [1, ncol]]),
                    stg[b][:, 0:ncol],
                    r=(f"p1a_stg{b}",), w=(f"{lbl}_out",), label=f"{lbl}_st{it}")

        tile_pass("L1_attn_W_in", self.winT, 32, 8, "pwi")
        tile_pass("L1_attn_W_x", self.wxT, 16, 16, "pwx")
        tile_pass("L1_attn_W_out", self.woT, 8, 16, "pwo")
        wxh = self.hbm["L1_attn_W_x"]
        dma("qgp", wf[0][0:32, 0:2048],
            bass.AP(wxh, 2048 * 2048, [[2048, 32], [1, 2048]]),
            w=("p1a_wf0",), label=f"{tg}_bcld")
        op("vector", lambda e: e.tensor_copy(wb[0][0:32, 0:2048], wf[0][0:32, 0:2048]),
           r=("p1a_wf0",), w=("p1a_wb0",), label=f"{tg}_bcc")
        psb = self.PS[:, 1024:1536].bitcast(BF16)
        for blk in range(16):
            op("tensor", lambda e, blk=blk, psb=psb: e.transpose(
                   psb[:, blk * 32:(blk + 1) * 32],
                   wb[0][0:32, blk * 128:(blk + 1) * 128],
                   self.ident_b[0:32, 0:32]),
               r=("p1a_wb0", "ident_b"), w=("ps2",), label=f"{tg}_bcT{blk}")
        op("scalar", lambda e, psb=psb: e.copy(stg[0][:, 0:512], psb[:, 0:512]),
           r=("ps2",), w=("p1a_stg0",), label=f"{tg}_bcE")
        dma("qsync",
            bass.AP(self.wxT, 16 * 128 * 2048, [[2048, 128], [1, 512]]),
            stg[0][:, 0:512],
            r=("p1a_stg0",), w=("pwx_out",), label=f"{tg}_bcst")
        st.close()
        self.sched.barrier()

    def l1_attn(self):
        op, dma = self.op, self.dma
        tg = "mam"
        L, W, TC = 64, 16 * 65, 512
        one = self.constf(1.0)
        st = ExitStack()
        sel = self.sb([48, 2048], BF16, "mam_sel", stack=st)
        stS = ExitStack()
        ncol = self.sb([48, 2048], F32, "mam_ncol", stack=stS)
        qv = self.sb([48, 4], F32, "mam_qv", stack=stS)
        op("gpsimd", lambda e: e.iota(ncol[:], [[1, 16], [0, 128]],
                                      channel_multiplier=0,
                                      allow_small_or_imprecise_dtypes=True),
           w=("mam_ncol",), label=f"{tg}_in")
        op("gpsimd", lambda e: e.iota(qv[:, 0:1], [[0, 1]], channel_multiplier=1,
                                      allow_small_or_imprecise_dtypes=True),
           w=("mam_qv",), label=f"{tg}_iq")
        op("vector", lambda e: e.tensor_single_scalar(
               qv[:, 1:2], qv[:, 0:1], 16.0, ALU.is_ge),
           r=("mam_qv",), w=("mam_qv",), label=f"{tg}_q1")
        op("vector", lambda e: e.tensor_single_scalar(
               qv[:, 2:3], qv[:, 0:1], 32.0, ALU.is_ge),
           r=("mam_qv",), w=("mam_qv",), label=f"{tg}_q2")
        op("vector", lambda e: e.scalar_tensor_tensor(
               qv[:, 3:4], qv[:, 1:2], -16.0, qv[:, 0:1], ALU.mult, ALU.add),
           r=("mam_qv",), w=("mam_qv",), label=f"{tg}_q3")
        op("vector", lambda e: e.scalar_tensor_tensor(
               qv[:, 3:4], qv[:, 2:3], -16.0, qv[:, 3:4], ALU.mult, ALU.add),
           r=("mam_qv",), w=("mam_qv",), label=f"{tg}_q4")
        op("vector", lambda e: e.tensor_tensor(
               sel[:], ncol[:], bass.AP(qv, 3, [[4, 48], [0, 2048]]),
               ALU.is_equal),
           r=("mam_ncol", "mam_qv"), w=("mam_sel",), label=f"{tg}_sel")
        stS.close()
        self.sched.barrier()
        xy = self.sb([128, 8240], BF16, "mam_xy", stack=st)
        hconv = self.sb([128, 8192], BF16, "mam_hconv", stack=st)
        dl = self.sb([128, 8192], BF16, "mam_dl", stack=st)
        BbS = [self.sb([128, 1024], BF16, f"mam_Bb{i}", stack=st) for i in range(2)]
        CbS = [self.sb([128, 1024], BF16, f"mam_Cb{i}", stack=st) for i in range(2)]
        ab = [self.sb([128, W], BF16, f"mam_ab{i}", stack=st) for i in range(2)]
        bu = [self.sb([128, W], BF16, f"mam_bu{i}", stack=st) for i in range(2)]
        hs = [self.sb([128, W], BF16, f"mam_hs{i}", stack=st) for i in range(2)]
        du = [self.sb([128, 1024], BF16, "mam_du", stack=st)] * 2
        G = [self.sb([128, 1024], BF16, "mam_G", stack=st)] * 2
        BCsb = self.sb([48, 512], BF16, "mam_BC", stack=st)
        A_sb = self.sb([128, 256], BF16, "mam_A", stack=st)
        Dcol = self.sb([128, 16], F32, "mam_D", stack=st)
        cw2 = self.sb([128, 64], F32, "mam_cw", stack=st)
        cb2 = self.sb([128, 16], F32, "mam_cb", stack=st)
        carr = self.sb([128, 256], BF16, "mam_carr", stack=st)
        halocp = self.sb([128, 48], BF16, "mam_halo", stack=st)
        ztmp = [self.sb([128, 512], BF16, "mam_zt", stack=st)] * 2
        rtmp = [self.sb([128, 64], F32, f"mam_rt{i}", stack=st) for i in range(2)]
        s2f = [self.sb([128, 512], F32, f"mam_s2f{i}", stack=st) for i in range(2)]
        wbig = self.sb([128, 4096], BF16, "mam_wbig", stack=st)

        def wq(lo, hi):
            return tuple(f"mam_wb{q}" for q in range(lo // 1024, (hi + 1023) // 1024))
        hcs = tuple(f"mam_hc{ec}" for ec in range(16))
        dls = tuple(f"mam_dl{ec}" for ec in range(16))

        dma("qscal", s2f[0][:, 0:256],
            bass.AP(self.hbm["L1_attn_A_log"], 0, [[16, 128], [2048, 16], [1, 16]]),
            w=("mam_s2f0",), label=f"{tg}_ldA", slow_ok=True)
        op("scalar", lambda e: e.activation(s2f[1][:, 0:256], s2f[0][:, 0:256],
                                            AF.Exp),
           r=("mam_s2f0",), w=("mam_s2f1",), label=f"{tg}_expA")
        op("vector", lambda e: e.tensor_scalar_mul(A_sb[:], s2f[1][:, 0:256], -1.0),
           r=("mam_s2f1",), w=("mam_A",), label=f"{tg}_negA")
        dma("qscal", Dcol[:],
            bass.AP(self.hbm["L1_attn_D"], 0, [[1, 128], [128, 16]]),
            w=("mam_D",), label=f"{tg}_ldD", slow_ok=True)
        dma("qscal", cw2[:],
            bass.AP(self.hbm["L1_attn_conv_w"], 0, [[4, 128], [512, 16], [1, 4]]),
            w=("mam_cw",), label=f"{tg}_ldcw", slow_ok=True)
        dma("qscal", cb2[:],
            bass.AP(self.hbm["L1_attn_conv_b"], 0, [[1, 128], [128, 16]]),
            w=("mam_cb",), label=f"{tg}_ldcb", slow_ok=True)
        op("gpsimd", lambda e: e.memset(carr[:], 0.0), w=("mam_carr",),
           label=f"{tg}_carr0")
        op("gpsimd", lambda e: e.memset(halocp[:], 0.0), w=("mam_halo",),
           label=f"{tg}_halo0")
        for i in range(2):
            op("gpsimd", lambda e, i=i: e.memset(
                   bass.AP(ab[i], 0, [[W, 128], [65, 16], [1, 1]]), 0.0),
               w=(f"mam_ab{i}",), label=f"{tg}_abz{i}")

        for tc in range(4):
            t0 = tc * TC
            for jc in range(16):
                b = jc % 2
                dma("qgp", wbig[:, b * 1024:(b + 1) * 1024],
                    bass.AP(self.winT, jc * 128 * 1024, [[1024, 128], [1, 1024]]),
                    w=wq(b * 1024, (b + 1) * 1024), label=f"{tg}_wi{tc}_{jc}")
                bank = jc % 6
                def mmxz(e, jc=jc, b=b, bank=bank, t0=t0):
                    ins = None
                    for dc in range(8):
                        ins = e.matmul(
                            self.PS[:, bank * 512:(bank + 1) * 512],
                            wbig[:, b * 1024 + dc * 128: b * 1024 + (dc + 1) * 128],
                            self.u[:, dc * T + t0: dc * T + t0 + TC],
                            start=(dc == 0), stop=(dc == 7))
                    return ins
                op("tensor", mmxz,
                   r=wq(b * 1024, (b + 1) * 1024) + self.u_tags(),
                   w=(f"ps{bank}",), label=f"{tg}_xz{tc}_{jc}")
                op("scalar", lambda e, jc=jc, bank=bank: e.copy(
                       bass.AP(xy, jc * 515 + 3, [[8240, 128], [1, 512]]),
                       self.PS[:, bank * 512:(bank + 1) * 512]),
                   r=(f"ps{bank}",), w=("mam_xy",), label=f"{tg}_xe{tc}_{jc}")
            op("vector", lambda e: e.tensor_copy(
                   bass.AP(xy, 0, [[8240, 128], [515, 16], [1, 3]]),
                   bass.AP(halocp, 0, [[48, 128], [3, 16], [1, 3]])),
               r=("mam_halo",), w=("mam_xy",), label=f"{tg}_hr{tc}")
            for ec in range(16):
                base = ec * 515
                cb_ = ec % 2
                cv = s2f[cb_]
                ctag = f"mam_s2f{cb_}"
                op("vector", lambda e, base=base, cv=cv, ec=ec: e.tensor_scalar_mul(
                       cv[:], bass.AP(xy, base, [[8240, 128], [1, 512]]),
                       cw2[:, ec * 4:ec * 4 + 1]),
                   r=("mam_xy", "mam_cw"), w=(ctag,), label=f"{tg}_cv{tc}_{ec}_0")
                for k in range(1, 4):
                    op("vector", lambda e, base=base, cv=cv, ec=ec, k=k:
                       e.scalar_tensor_tensor(
                           cv[:], bass.AP(xy, base + k, [[8240, 128], [1, 512]]),
                           cw2[:, ec * 4 + k:ec * 4 + k + 1], cv[:],
                           ALU.mult, ALU.add),
                       r=("mam_xy", "mam_cw", ctag), w=(ctag,),
                       label=f"{tg}_cv{tc}_{ec}_{k}")
                op("scalar", lambda e, cv=cv, ec=ec: e.activation(
                       hconv[:, ec * 512:(ec + 1) * 512], cv[:], AF.Silu,
                       bias=cb2[:, ec:ec + 1]),
                   r=(ctag, "mam_cb"), w=(f"mam_hc{ec}",),
                   label=f"{tg}_si{tc}_{ec}")
            op("vector", lambda e: e.tensor_copy(
                   bass.AP(halocp, 0, [[48, 128], [3, 16], [1, 3]]),
                   bass.AP(xy, 512, [[8240, 128], [515, 16], [1, 3]])),
               r=("mam_xy",), w=("mam_halo",), label=f"{tg}_hw{tc}")
            dma("qsync", wbig[:, 2048:2560],
                bass.AP(self.wxT, 16 * 128 * 2048, [[2048, 128], [1, 512]]),
                w=wq(2048, 2560), label=f"{tg}_wbc{tc}")
            def mmB(e, col0=0, pcol=0):
                ins = None
                for ec in range(16):
                    ins = e.matmul(
                        self.PS[0:16, pcol:pcol + 512],
                        wbig[:, 2048 + ec * 32 + col0: 2048 + ec * 32 + col0 + 16],
                        hconv[:, ec * 512:(ec + 1) * 512],
                        start=(ec == 0), stop=(ec == 15))
                return ins
            op("tensor", lambda e: mmB(e, 0, 0), r=wq(2048, 2560) + hcs,
               w=("ps0",), label=f"{tg}_mB{tc}")
            op("tensor", lambda e: mmB(e, 16, 512), r=wq(2048, 2560) + hcs,
               w=("ps1",), label=f"{tg}_mC{tc}")
            op("vector", lambda e: e.tensor_copy(BCsb[0:16, :], self.PS[0:16, 0:512]),
               r=("ps0",), w=("mam_BC",), label=f"{tg}_eB{tc}")
            op("vector", lambda e: e.tensor_copy(BCsb[32:48, :], self.PS[0:16, 512:1024]),
               r=("ps1",), w=("mam_BC",), label=f"{tg}_eC{tc}")
            for fc in range(16):
                b = fc % 2
                dma("qsync", wbig[:, b * 2048:(b + 1) * 2048],
                    bass.AP(self.wxT, fc * 128 * 2048, [[2048, 128], [1, 2048]]),
                    w=wq(b * 2048, (b + 1) * 2048), label=f"{tg}_wx{tc}_{fc}")
                bank = 2 + fc % 4
                def mmd(e, b=b, bank=bank):
                    ins = None
                    for ec in range(16):
                        ins = e.matmul(
                            self.PS[:, bank * 512:(bank + 1) * 512],
                            wbig[:, b * 2048 + ec * 128: b * 2048 + (ec + 1) * 128],
                            hconv[:, ec * 512:(ec + 1) * 512],
                            start=(ec == 0), stop=(ec == 15))
                    return ins
                op("tensor", mmd, r=wq(b * 2048, (b + 1) * 2048) + hcs,
                   w=(f"ps{bank}",), label=f"{tg}_dbc{tc}_{fc}")
                s2 = s2f[b]
                stag = f"mam_s2f{b}"
                pb = self.PS[:, bank * 512:(bank + 1) * 512]
                op("vector", lambda e, s2=s2, pb=pb: e.tensor_scalar_mul(
                       s2[:], pb, -1.0),
                   r=(f"ps{bank}",), w=(stag,), label=f"{tg}_sp0{tc}_{fc}")
                op("vector", lambda e, s2=s2, pb=pb: e.tensor_tensor(
                       s2[:], s2[:], pb, ALU.min),
                   r=(f"ps{bank}", stag), w=(stag,), label=f"{tg}_sp1{tc}_{fc}")
                op("scalar", lambda e, s2=s2: e.activation(s2[:], s2[:], AF.Exp),
                   r=(stag,), w=(stag,), label=f"{tg}_sp2{tc}_{fc}")
                op("scalar", lambda e, s2=s2: e.activation(s2[:], s2[:], AF.Ln,
                                                           bias=one),
                   r=(stag, "constbuf"), w=(stag,), label=f"{tg}_sp3{tc}_{fc}")
                op("vector", lambda e, s2=s2, pb=pb, fc=fc: e.scalar_tensor_tensor(
                       dl[:, fc * 512:(fc + 1) * 512], pb, 0.0, s2[:],
                       ALU.max, ALU.add),
                   r=(f"ps{bank}", stag), w=(f"mam_dl{fc}",),
                   label=f"{tg}_sp4{tc}_{fc}")
            for s in range(8):
                sb2 = s % 2
                op("vector", lambda e, s=s, sb2=sb2: e.tensor_tensor(
                       du[sb2][:],
                       bass.AP(dl, s * L, [[8192, 128], [512, 16], [1, L]]),
                       bass.AP(hconv, s * L, [[8192, 128], [512, 16], [1, L]]),
                       ALU.mult),
                   r=dls + hcs, w=("mam_du",), label=f"{tg}_du{tc}_{s}")
                def mmbc(e, s=s, prow=0, pcol=1024):
                    ins = None
                    for n in range(16):
                        ins = e.matmul(
                            self.PS[:, pcol + n * 64: pcol + (n + 1) * 64],
                            sel[prow:prow + 16, n * 128:(n + 1) * 128],
                            BCsb[prow:prow + 16, s * L:(s + 1) * L],
                            start=True, stop=True)
                    return ins
                op("tensor", lambda e, s=s: mmbc(e, s, 0, 1024),
                   r=("mam_sel", "mam_BC"), w=("ps2", "ps3"),
                   label=f"{tg}_bb{tc}_{s}")
                op("scalar", lambda e, sb2=sb2: e.copy(BbS[sb2][:],
                                                       self.PS[:, 1024:2048]),
                   r=("ps2", "ps3"), w=(f"mam_Bb{sb2}",), label=f"{tg}_be{tc}_{s}")
                op("tensor", lambda e, s=s: mmbc(e, s, 32, 2048),
                   r=("mam_sel", "mam_BC"), w=("ps4", "ps5"),
                   label=f"{tg}_cb{tc}_{s}")
                op("scalar", lambda e, sb2=sb2: e.copy(CbS[sb2][:],
                                                       self.PS[:, 2048:3072]),
                   r=("ps4", "ps5"), w=(f"mam_Cb{sb2}",), label=f"{tg}_ce{tc}_{s}")
                for ec in range(16):
                    ui = (s * 16 + ec) % 2
                    abt, but, hst, Gt = ab[ui], bu[ui], hs[ui], G[ui]
                    ta, tb, th, tG = (f"mam_ab{ui}", f"mam_bu{ui}",
                                      f"mam_hs{ui}", "mam_G")
                    op("gpsimd", lambda e, abt=abt, ec=ec, s=s: e.tensor_tensor(
                           bass.AP(abt, 1, [[W, 128], [65, 16], [1, L]]),
                           bass.AP(dl, ec * 512 + s * L,
                                   [[8192, 128], [0, 16], [1, L]]),
                           bass.AP(A_sb, ec * 16, [[256, 128], [1, 16], [0, L]]),
                           ALU.mult),
                       r=(f"mam_dl{ec}", "mam_A"), w=(ta,),
                       label=f"{tg}_ap{tc}_{s}_{ec}")
                    op("scalar", lambda e, abt=abt: e.activation(
                           bass.AP(abt, 1, [[W, 128], [65, 16], [1, L]]),
                           bass.AP(abt, 1, [[W, 128], [65, 16], [1, L]]),
                           AF.Exp),
                       r=(ta,), w=(ta,), label=f"{tg}_ex{tc}_{s}_{ec}")
                    op("gpsimd", lambda e, but=but, ec=ec, sb2=sb2: e.tensor_tensor(
                           bass.AP(but, 1, [[W, 128], [65, 16], [1, L]]),
                           bass.AP(du[sb2], ec * L, [[1024, 128], [0, 16], [1, L]]),
                           bass.AP(BbS[sb2], 0, [[1024, 128], [64, 16], [1, L]]),
                           ALU.mult),
                       r=("mam_du", f"mam_Bb{sb2}"), w=(tb,),
                       label=f"{tg}_bu{tc}_{s}_{ec}")
                    op("gpsimd", lambda e, but=but, ec=ec: e.tensor_copy(
                           bass.AP(but, 0, [[W, 128], [65, 16], [1, 1]]),
                           bass.AP(carr, ec * 16, [[256, 128], [1, 16], [1, 1]])),
                       r=("mam_carr",), w=(tb,), label=f"{tg}_ci{tc}_{s}_{ec}")
                    op("vector", lambda e, hst=hst, abt=abt, but=but:
                       e.tensor_tensor_scan(hst[:], abt[:], but[:], 0.0,
                                            ALU.mult, ALU.add),
                       r=(ta, tb), w=(th,), label=f"{tg}_sc{tc}_{s}_{ec}")
                    op("gpsimd", lambda e, hst=hst, ec=ec: e.tensor_copy(
                           bass.AP(carr, ec * 16, [[256, 128], [1, 16], [1, 1]]),
                           bass.AP(hst, 64, [[W, 128], [65, 16], [1, 1]])),
                       r=(th,), w=("mam_carr",), label=f"{tg}_co{tc}_{s}_{ec}")
                    op("vector", lambda e, Gt=Gt, hst=hst, sb2=sb2: e.tensor_tensor(
                           Gt[:],
                           bass.AP(hst, 1, [[W, 128], [65, 16], [1, L]]),
                           bass.AP(CbS[sb2], 0, [[1024, 128], [64, 16], [1, L]]),
                           ALU.mult),
                       r=(th, f"mam_Cb{sb2}"), w=(tG,),
                       label=f"{tg}_G{tc}_{s}_{ec}")
                    rb = (s * 16 + ec) % 2
                    op("vector", lambda e, Gt=Gt, rb=rb: e.tensor_reduce(
                           rtmp[rb][:],
                           bass.AP(Gt, 0, [[1024, 128], [1, L], [L, 16]]),
                           AX.X, ALU.add),
                       r=(tG,), w=(f"mam_rt{rb}",), label=f"{tg}_f1{tc}_{s}_{ec}")
                    op("scalar", lambda e, rb=rb, ec=ec, s=s: e.copy(
                           bass.AP(xy, ec * 512 + s * L, [[8240, 128], [1, L]]),
                           rtmp[rb][:]),
                       r=(f"mam_rt{rb}",), w=("mam_xy",),
                       label=f"{tg}_f2{tc}_{s}_{ec}")
            for ec in range(16):
                op("vector", lambda e, ec=ec: e.scalar_tensor_tensor(
                       bass.AP(xy, ec * 512, [[8240, 128], [1, 512]]),
                       hconv[:, ec * 512:(ec + 1) * 512],
                       Dcol[:, ec:ec + 1],
                       bass.AP(xy, ec * 512, [[8240, 128], [1, 512]]),
                       ALU.mult, ALU.add),
                   r=(f"mam_hc{ec}", "mam_D", "mam_xy"), w=("mam_xy",),
                   label=f"{tg}_dk{tc}_{ec}")
            for jc2 in range(16):
                b = jc2 % 2
                dma("qgp", wbig[:, b * 1024:(b + 1) * 1024],
                    bass.AP(self.winT, (16 + jc2) * 128 * 1024,
                            [[1024, 128], [1, 1024]]),
                    w=wq(b * 1024, (b + 1) * 1024), label=f"{tg}_wz{tc}_{jc2}")
                bank = jc2 % 4
                def mmz(e, jc2=jc2, b=b, bank=bank, t0=t0):
                    ins = None
                    for dc in range(8):
                        ins = e.matmul(
                            self.PS[:, bank * 512:(bank + 1) * 512],
                            wbig[:, b * 1024 + dc * 128: b * 1024 + (dc + 1) * 128],
                            self.u[:, dc * T + t0: dc * T + t0 + TC],
                            start=(dc == 0), stop=(dc == 7))
                    return ins
                op("tensor", mmz,
                   r=wq(b * 1024, (b + 1) * 1024) + self.u_tags(),
                   w=(f"ps{bank}",), label=f"{tg}_z{tc}_{jc2}")
                op("scalar", lambda e, b=b, bank=bank: e.activation(
                       ztmp[b][:], self.PS[:, bank * 512:(bank + 1) * 512],
                       AF.Silu),
                   r=(f"ps{bank}",), w=("mam_zt",), label=f"{tg}_ze{tc}_{jc2}")
                op("vector", lambda e, b=b, jc2=jc2: e.tensor_mul(
                       bass.AP(xy, jc2 * 512, [[8240, 128], [1, 512]]),
                       bass.AP(xy, jc2 * 512, [[8240, 128], [1, 512]]),
                       ztmp[b][:]),
                   r=("mam_zt", "mam_xy"), w=("mam_xy",),
                   label=f"{tg}_gt{tc}_{jc2}")
            for oc in range(8):
                b = oc % 2
                dma("qscal", wbig[:, b * 2048:(b + 1) * 2048],
                    bass.AP(self.woT, oc * 128 * 2048, [[2048, 128], [1, 2048]]),
                    w=wq(b * 2048, (b + 1) * 2048), label=f"{tg}_wo{tc}_{oc}")
                def mmo(e, b=b, oc=oc):
                    ins = None
                    for ec in range(16):
                        ins = e.matmul(
                            self.PS[:, oc * 512:(oc + 1) * 512],
                            wbig[:, b * 2048 + ec * 128: b * 2048 + (ec + 1) * 128],
                            bass.AP(xy, ec * 512, [[8240, 128], [1, 512]]),
                            start=(ec == 0), stop=(ec == 15))
                    return ins
                op("tensor", mmo,
                   r=wq(b * 2048, (b + 1) * 2048) + ("mam_xy",),
                   w=(f"ps{oc}",), label=f"{tg}_wm{tc}_{oc}")
                xs_t = self.x_tags(oc, range(tc * 4, tc * 4 + 4))
                op("vector", lambda e, oc=oc, t0=t0: e.tensor_add(
                       self.x[:, oc * T + t0: oc * T + t0 + TC],
                       self.x[:, oc * T + t0: oc * T + t0 + TC],
                       self.PS[:, oc * 512:(oc + 1) * 512]),
                   r=(f"ps{oc}",) + xs_t, w=xs_t, label=f"{tg}_xr{tc}_{oc}")
        st.close()
        self.sched.barrier()

    def store_out(self):
        stio = ExitStack()
        stage = [self.sb([128, D], F32, f"ostage{i}", stack=stio) for i in range(2)]
        for tt in range(16):
            s = stage[tt % 2]
            stag = f"iostage{tt % 2}"
            for dc in range(8):
                pb = self.pool[dc % 4]
                ptag = f"ps{dc % 4}"
                self.op("tensor",
                        lambda e, dc=dc, tt=tt, pb=pb: e.transpose(
                            pb[:, 0:128],
                            self.x[:, dc * T + tt * 128: dc * T + (tt + 1) * 128],
                            self.ident_f[:]),
                        r=(f"x:{dc}:{tt}", "ident_f"), w=(ptag,),
                        label=f"oT{tt}_{dc}")
                self.op("vector",
                        lambda e, dc=dc, pb=pb, s=s: e.tensor_copy(
                            s[:, dc * 128:(dc + 1) * 128], pb[:, 0:128]),
                        r=(ptag,), w=(stag,), label=f"oEv{tt}_{dc}")
            self.dma("qsync",
                     bass.AP(self.out_hbm, tt * 128 * D, [[D, 128], [1, D]]),
                     s[:],
                     r=(stag,), w=(f"osent{tt}",), label=f"ostore{tt}")
        stio.close()



INPUT_META = [
    ("L0_attn_x", (T, D), F32),
    ("L0_attn_W_kv_down", (1024, 128), F32),
    ("L0_attn_W_k_up", (128, 1024), F32),
    ("L0_attn_W_v_up", (128, 1024), F32),
    ("L0_attn_W_q", (1024, 1024), F32),
    ("L0_attn_W_q_rope", (1024, 32), F32),
    ("L0_attn_W_k_rope", (128, 32), F32),
    ("L0_attn_W_o", (1024, 1024), F32),
    ("L0_ffn_w1", (4096, 1024), F32),
    ("L0_ffn_conv_w", (4096, 1, 3), F32),
    ("L0_ffn_conv_b", (4096,), F32),
    ("L0_ffn_w2", (1024, 4096), F32),
    ("L1_attn_W_in", (4096, 1024), F32),
    ("L1_attn_conv_w", (2048, 1, 4), F32),
    ("L1_attn_conv_b", (2048,), F32),
    ("L1_attn_W_x", (2080, 2048), F32),
    ("L1_attn_A_log", (2048, 16), F32),
    ("L1_attn_D", (2048,), F32),
    ("L1_attn_W_out", (1024, 2048), F32),
    ("L1_ffn_rail_maker", (4, 64, 16, 16), F32),
    ("L1_ffn_rail_up_proj", (4, 64, 64, 16), F32),
    ("L1_ffn_w_cproj", (1024, 16384), F32),
]


def build_nc(n_sub=6):
    nc = bass.Bass("TRN2", target_bir_lowering=False)
    es = ExitStack()
    with es:
        b = Builder(nc, es, n_sub=n_sub)
        b.build(INPUT_META)
    return nc


def kernel(**inputs):
    n_sub = int(inputs.pop("_n_sub", 6))
    nc = build_nc(n_sub=n_sub)
    x = np.ascontiguousarray(inputs["L0_attn_x"])
    in_maps = []
    for c in range(NCORE):
        m = {k: np.ascontiguousarray(v) for k, v in inputs.items()}
        m["L0_attn_x"] = np.ascontiguousarray(x[c])
        in_maps.append(m)
    res = run_bass_kernel_spmd(nc, in_maps, core_ids=list(range(NCORE)))
    out = np.stack([r["out"] for r in res.results], axis=0)
    return out
